# version 18
# baseline (speedup 1.0000x reference)
import contextlib
import numpy as np
import ml_dtypes
import concourse.bass as bass
import concourse.mybir as mybir
from concourse.bass_utils import run_bass_kernel_spmd

F32 = mybir.dt.float32
BF16 = mybir.dt.bfloat16
AF = mybir.ActivationFunctionType
ALU = mybir.AluOpType
AX = mybir.AxisListType
BF = ml_dtypes.bfloat16

D = 2048
NH = 16
NG = 4
GS = 4
DH = 128
HID = 1024
NE = 8
SINK = 4
SCALE = DH ** -0.5
EPS = 1e-5
NEG = -1e30
ENGINES = ("sync", "scalar", "vector", "tensor", "gpsimd")


class Sched:
    def __init__(self):
        self.items = []
        self.counts = {}
        self.pending = {e: [] for e in ENGINES}
        self.phase = "setup"

    def _bump(self, sem, amt):
        k = id(sem)
        self.counts[k] = self.counts.get(k, 0) + amt
        return self.counts[k]

    def val(self, sem):
        return self.counts.get(id(sem), 0)

    def op(self, eng, fn, waits=(), inc=None, amt=1):
        tok = None
        if inc is not None:
            tok = self._bump(inc, amt)
        w = self.pending[eng] + [x for x in waits if x is not None and x[0] is not None]
        self.pending[eng] = []
        self.items.append((eng, fn, w, inc, amt, self.phase))
        return tok

    def dma(self, eng, fn, waits=(), inc=None):
        return self.op(eng, fn, waits, inc, 16)

    def barrier(self, waits):
        ws = [x for x in waits if x is not None and x[0] is not None]
        for e in ENGINES:
            self.pending[e] = self.pending[e] + ws

    def emit(self, block, nc):
        for engname in ENGINES:
            def make(engname=engname):
                def emitter(eng):
                    seen = {}
                    for (e, fn, waits, incsem, amt, phase) in self.items:
                        if e != engname:
                            continue
                        with nc.named_scope(phase):
                            for (sm, v) in waits:
                                k = id(sm)
                                if seen.get(k, -1) < v:
                                    eng.wait_ge(sm, v)
                                    seen[k] = v
                            inst = fn(eng)
                            if incsem is not None:
                                inst.then_inc(incsem, amt)
                return emitter
            getattr(block, engname)(make())


def host_consts(T):
    half = DH // 2
    inv = 1.0 / (10000.0 ** (np.arange(half, dtype=np.float64) * 2.0 / DH))
    t = np.arange(T, dtype=np.float64)
    ang = np.outer(inv, t)
    cos = np.cos(ang).astype(np.float32)
    sin = np.sin(ang).astype(np.float32)
    cos2 = np.concatenate([cos, cos], 0)
    sin2n = np.concatenate([-sin, sin], 0)

    il = np.arange(128)[None, :]
    jl = np.arange(128)[:, None]

    def mk(keep):
        keep = np.broadcast_to(keep, (128, 128))
        m = np.where(keep, 0.0, NEG).astype(np.float32)
        return np.tile(m, (1, 4)).astype(BF)
    return {
        "cos2": cos2, "sin2n": sin2n,
        "m_diag": mk(il >= jl),
        "m_win": mk(jl >= il),
        "m_wsink": mk((jl >= il) | (jl < SINK)),
        "m_sink": mk(jl < SINK),
        "idn_bf": np.eye(128, dtype=BF),
        "ones_col_bf": np.ones((1, 128), dtype=BF),
        "ones128_bf": np.ones((128, 1), dtype=BF),
        "ones_row_f": np.ones((1, 128), dtype=np.float32),
        "one1": np.ones((1, 1), dtype=np.float32),
        "eps_col": np.full((128, 1), EPS, dtype=np.float32),
    }


def blocks_for(qt):
    if qt <= 4:
        out = []
        for kt in range(qt + 1):
            if kt == qt:
                out.append((kt, "m_diag"))
            elif kt == 0 and qt == 4:
                out.append((kt, "m_wsink"))
            else:
                out.append((kt, None))
        return out
    return [(qt - 4, "m_win"), (qt - 3, None), (qt - 2, None), (qt - 1, None),
            (qt, "m_diag"), (0, "m_sink")]


class Builder:
    def __init__(self, T):
        assert T % 512 == 0
        self.T = T
        self.NT = T // 128
        self.NM = T // 512
        nc = bass.Bass("TRN2", target_bir_lowering=False, debug=False)
        self.nc = nc
        self.s = Sched()
        self.ctx = contextlib.ExitStack()

        dt = nc.dram_tensor
        self.x = dt("x", [T, D], F32, kind="ExternalInput").ap()
        self.wq = dt("wq", [D, D], F32, kind="ExternalInput").ap()
        self.bq = dt("bq", [1, D], F32, kind="ExternalInput").ap()
        self.wk = dt("wk", [D, 512], F32, kind="ExternalInput").ap()
        self.bk = dt("bk", [1, 512], F32, kind="ExternalInput").ap()
        self.wv = dt("wv", [D, 512], F32, kind="ExternalInput").ap()
        self.bv = dt("bv", [1, 512], F32, kind="ExternalInput").ap()
        self.wo = dt("wo", [D, D], F32, kind="ExternalInput").ap()
        self.bo = dt("bo", [1, D], F32, kind="ExternalInput").ap()
        self.gw = dt("gw", [NE, D], F32, kind="ExternalInput").ap()
        self.w1t = dt("w1t", [NE + 1, D, HID], F32, kind="ExternalInput").ap()
        self.w2t = dt("w2t", [NE + 1, HID, D], F32, kind="ExternalInput").ap()
        self.w3t = dt("w3t", [NE + 1, D, HID], F32, kind="ExternalInput").ap()
        self.c_cos2 = dt("cos2", [128, T], F32, kind="ExternalInput").ap()
        self.c_sin2n = dt("sin2n", [128, T], F32, kind="ExternalInput").ap()
        for nm in ("m_diag", "m_win", "m_wsink", "m_sink"):
            setattr(self, "c_" + nm, dt(nm, [128, 512], BF16, kind="ExternalInput").ap())
        self.c_idn_bf = dt("idn_bf", [128, 128], BF16, kind="ExternalInput").ap()
        self.c_ones_col_bf = dt("ones_col_bf", [1, 128], BF16, kind="ExternalInput").ap()
        self.c_ones128_bf = dt("ones128_bf", [128, 1], BF16, kind="ExternalInput").ap()
        self.c_ones_row_f = dt("ones_row_f", [1, 128], F32, kind="ExternalInput").ap()
        self.c_one1 = dt("one1", [1, 1], F32, kind="ExternalInput").ap()
        self.c_eps_col = dt("eps_col", [128, 1], F32, kind="ExternalInput").ap()
        self.xn_scr = dt("xn_scr", [T, D], BF16, kind="Internal").ap()
        self.q_scr = dt("q_scr", [128, NH, T], BF16, kind="Internal").ap()
        self.sw_scr = dt("sw_scr", [2, 128, D], F32, kind="Internal").ap()
        self.x1_scr = dt("x1_scr", [T, D], F32, kind="Internal").ap()
        self.xn2_scr = dt("xn2_scr", [T, D], BF16, kind="Internal").ap()
        self.g_scr = dt("g_scr", [NE, T], BF16, kind="Internal").ap()
        self.w1_bfs = dt("w1_bfs", [(NE + 1) * 128, 16 * HID], BF16, kind="Internal").ap()
        self.w3_bfs = dt("w3_bfs", [(NE + 1) * 128, 16 * HID], BF16, kind="Internal").ap()
        self.w2_bfs = dt("w2_bfs", [(NE + 1) * 128, 16 * HID], BF16, kind="Internal").ap()
        self.out = dt("out", [T, D], F32, kind="ExternalOutput").ap()

        self.sems = {}
        for nm in ["c_pe", "c_se", "c_ve", "c_po",
                   "d_w0", "d_w1", "d_w2", "d_w3",
                   "d_x0", "d_x1",
                   "d_xnw0", "d_xnw1",
                   "d_xnt0", "d_xnt1",
                   "d_sw0", "d_sw1",
                   "d_qw",
                   "d_qt0", "d_qt1",
                   "d_cos", "d_cos1", "d_cst", "d_gw",
                   "d_x1w", "d_x1r", "d_xn2w",
                   "d_outS", "d_outG", "d_outG1", "d_gsw", "d_gsr0", "d_gsr1", "d_wc",
                   "d_wb0", "d_wb1", "d_wb2"]:
            self.sems[nm] = self.ctx.enter_context(self.nc.semaphore(nm))

    def sb(self, name, shape, dtype, stack=None):
        return (stack or self.ctx).enter_context(self.nc.sbuf_tensor(name, shape, dtype))

    def ps(self, name, shape, dtype, stack=None):
        return (stack or self.ctx).enter_context(self.nc.psum_tensor(name, shape, dtype))

    def load_wmat(self, src_ap, dst, sem, waits, nchunk=16):
        s = self.s
        for part in range(4):
            s.dma("gpsimd", (lambda eng, part=part, src_ap=src_ap, dst=dst, nchunk=nchunk:
                  eng.dma_start(
                      out=dst[:].rearrange("p (c n) -> p c n", c=nchunk)
                          [:, part * (nchunk // 4):(part + 1) * (nchunk // 4), :],
                      in_=src_ap.rearrange("(c p) n -> p c n", p=128)
                          [:, part * (nchunk // 4):(part + 1) * (nchunk // 4), :])),
                  waits=(waits if part == 0 else []), inc=self.sems[sem])
        return s.val(self.sems[sem])

    def ln_tile(self, x_ap, out_bf, stats, mv, first_waits):
        s, sems = self.s, self.sems
        for k in range(4):
            s.op("vector", (lambda eng, k=k, x_ap=x_ap, stats=stats: eng.bn_stats(
                stats[:, k * 6:(k + 1) * 6], x_ap[:, k * 512:(k + 1) * 512])),
                waits=(first_waits if k == 0 else []))
        s.op("vector", lambda eng: eng.drain())
        s.op("vector", (lambda eng, stats=stats, mv=mv: eng.bn_aggr(mv[:, 0:2], stats[:, 0:24])))
        s.op("vector", lambda eng: eng.drain())
        tok = s.op("vector", lambda eng: eng.nop(), inc=sems["c_ve"])
        tok_sq = s.op("scalar", (lambda eng, mv=mv: eng.activation(
            out=mv[:, 2:3], in_=mv[:, 1:2], func=AF.Sqrt,
            bias=self.cst["eps_col"][:], scale=1.0)),
            waits=[(sems["c_ve"], tok)], inc=sems["c_se"])
        s.op("vector", (lambda eng, mv=mv: eng.reciprocal(mv[:, 3:4], mv[:, 2:3])),
             waits=[(sems["c_se"], tok_sq)])
        s.op("vector", lambda eng: eng.drain())
        return s.op("vector", (lambda eng, x_ap=x_ap, out_bf=out_bf, mv=mv: eng.tensor_scalar(
            out=out_bf[:], in0=x_ap[:], scalar1=mv[:, 0:1], scalar2=mv[:, 3:4],
            op0=ALU.subtract, op1=ALU.mult)), inc=sems["c_ve"])

    def rope(self, m, gi, src_f32, qsw, t1b, t2b, out_bf, cos4, sin4, cos_tok,
             tok_src, slot_par, prev_t2, out_guard=None, cos_sem="d_cos",
             prev_add=None):
        s, sems = self.s, self.sems
        slot = 0
        swsem = sems["d_sw0"]
        tok_sw_w = s.dma("sync", (lambda eng, slot=slot, src_f32=src_f32: eng.dma_start(
            out=self.sw_scr[slot], in_=src_f32[:])),
            waits=[tok_src, prev_t2, prev_add], inc=swsem)
        s.dma("sync", (lambda eng, slot=slot, qsw=qsw: eng.dma_start(
            out=qsw[0:64, :], in_=self.sw_scr[slot, 64:128, :])),
            waits=[(swsem, tok_sw_w)], inc=swsem)
        tok_sw_r = s.dma("sync", (lambda eng, slot=slot, qsw=qsw: eng.dma_start(
            out=qsw[64:128, :], in_=self.sw_scr[slot, 0:64, :])), inc=swsem)
        s.op("vector", lambda eng: eng.drain())
        tok_t1 = s.op("vector", (lambda eng, src_f32=src_f32, t1b=t1b, cos4=cos4: eng.tensor_tensor(
            out=t1b[:], in0=src_f32[:], in1=cos4[:], op=ALU.mult)),
            waits=[tok_src, (sems[cos_sem], cos_tok)], inc=sems["c_ve"])
        tok_t2 = s.op("gpsimd", (lambda eng, qsw=qsw, t2b=t2b, sin4=sin4: eng.tensor_tensor(
            out=t2b[:], in0=qsw[:], in1=sin4[:], op=ALU.mult)),
            waits=[(swsem, tok_sw_r), (sems[cos_sem], cos_tok), prev_add], inc=sems["c_po"])
        s.op("vector", lambda eng: eng.drain())
        tok_out = s.op("vector", (lambda eng, t1b=t1b, t2b=t2b, out_bf=out_bf: eng.tensor_tensor(
            out=out_bf[:], in0=t1b[:], in1=t2b[:], op=ALU.add)),
            waits=[(sems["c_po"], tok_t2), (sems["c_ve"], tok_t1), out_guard],
            inc=sems["c_ve"])
        return tok_out, tok_t2

    def load_trig(self, m, cos4, sin4, gate, sem="d_cos"):
        s, sems = self.s, self.sems
        for r in range(4):
            s.dma("scalar", (lambda eng, m=m, r=r, cos4=cos4: eng.dma_start(
                out=cos4[:, r * 512:(r + 1) * 512],
                in_=self.c_cos2[:, m * 512:(m + 1) * 512])),
                waits=(gate if r == 0 else []), inc=sems[sem])
            s.dma("scalar", (lambda eng, m=m, r=r, sin4=sin4: eng.dma_start(
                out=sin4[:, r * 512:(r + 1) * 512],
                in_=self.c_sin2n[:, m * 512:(m + 1) * 512])), inc=sems[sem])
        return s.val(sems[sem])

    def build(self):
        self._setup()
        self._pass_q()
        self._pass_kv()
        self._pass_attn()
        self._pass_moe()
        block = self.ctx.enter_context(self.nc.Block())
        self.s.emit(block, self.nc)
        self.ctx.close()
        return self.nc

    def _setup(self):
        s, sems = self.s, self.sems
        cst = {}
        for nm, shape, dty, src in [
            ("idn_bf", [128, 128], BF16, self.c_idn_bf),
            ("ones_col_bf", [1, 128], BF16, self.c_ones_col_bf),
            ("ones128_bf", [128, 1], BF16, self.c_ones128_bf),
            ("ones_row_f", [1, 128], F32, self.c_ones_row_f),
            ("one1", [1, 1], F32, self.c_one1),
            ("eps_col", [128, 1], F32, self.c_eps_col),
        ]:
            t = self.sb("cst_" + nm, shape, dty)
            cst[nm] = t
            s.dma("sync", (lambda eng, t=t, src=src: eng.dma_start(out=t[:], in_=src)),
                  inc=sems["d_cst"])
        self.cst = cst
        self.bqs_col = self.sb("bqs_col", [128, NH], F32)
        self.bk_col = self.sb("bk_col", [128, NG], F32)
        self.gatewT = self.sb("gatewT", [128, NH * NE], BF16)

        with contextlib.ExitStack() as c:
            for nm, shape, dty, src in [
                ("bq_row", [1, D], F32, self.bq),
                ("bk_row", [1, 512], F32, self.bk),
            ]:
                t = self.sb("cst_" + nm, shape, dty, c)
                cst[nm] = t
                s.dma("sync", (lambda eng, t=t, src=src: eng.dma_start(out=t[:], in_=src)),
                      inc=sems["d_cst"])
            gw_bf = self.sb("gw_bf", [NE, D], BF16, c)
            s.dma("gpsimd", lambda eng: eng.dma_start(out=gw_bf[:], in_=self.gw),
                  inc=sems["d_gw"])
            psA = self.ps("psA", [128, 512], F32, c)
            psB = self.ps("psB", [128, 1024], BF16, c)
            cw = (sems["d_cst"], s.val(sems["d_cst"]))
            gwd = (sems["d_gw"], s.val(sems["d_gw"]))
            for h in range(NH):
                tok_pe = s.op("tensor", (lambda eng, h=h, psA=psA: eng.matmul(
                    psA[:, h:h + 1], self.cst["bq_row"][0:1, h * 128:(h + 1) * 128],
                    self.cst["one1"][:], start=True, stop=True)),
                    waits=([cw] if h == 0 else []), inc=sems["c_pe"])
            for g in range(NG):
                tok_pe = s.op("tensor", (lambda eng, g=g, psA=psA: eng.matmul(
                    psA[:, 16 + g:17 + g], self.cst["bk_row"][0:1, g * 128:(g + 1) * 128],
                    self.cst["one1"][:], start=True, stop=True)), inc=sems["c_pe"])
            tok_v = s.op("vector", (lambda eng, psA=psA: eng.tensor_scalar(
                out=self.bqs_col[:], in0=psA[:, 0:16], scalar1=SCALE, scalar2=None,
                op0=ALU.mult)),
                waits=[(sems["c_pe"], tok_pe)], inc=sems["c_ve"])
            tok_v = s.op("vector", (lambda eng, psA=psA: eng.tensor_copy(
                self.bk_col[:], psA[:, 16:20])), inc=sems["c_ve"])


            for fc in range(NH):
                tok_pe = s.op("tensor", (lambda eng, fc=fc, psB=psB, gw_bf=gw_bf: eng.transpose(
                    psB[:, fc * 8:(fc + 1) * 8], gw_bf[:, fc * 128:(fc + 1) * 128],
                    self.cst["idn_bf"][0:8, 0:8])),
                    waits=([gwd, cw] if fc == 0 else []), inc=sems["c_pe"])
            tok_v3 = s.op("vector", (lambda eng, psB=psB: eng.tensor_copy(
                self.gatewT[:], psB[:, 0:NH * NE])),
                waits=[(sems["c_pe"], tok_pe)], inc=sems["c_ve"])
            self.setup_done = [(sems["c_ve"], s.val(sems["c_ve"])),
                               (sems["c_pe"], s.val(sems["c_pe"]))]

    def _pass_q(self):
        s, sems, cst = self.s, self.sems, self.cst
        s.phase = "passq"
        NM, NT = self.NM, self.NT
        s.barrier(self.setup_done)
        with contextlib.ExitStack() as c:
            wq_bf = self.sb("wq_bf", [128, NH * D], BF16, c)
            qx_sb = self.sb("qx_sb", [128, D], F32, c)
            qx_sb2 = self.sb("qx_sb2", [128, D], F32, c)
            x_sb = [qx_sb, qx_sb2]
            xn_bf = [self.sb(f"xn_bf{i}", [128, D], BF16, c) for i in range(2)]
            xnt = [self.sb(f"xnt{i}", [128, NH * 512], BF16, c) for i in range(2)]
            qtf = [self.sb(f"qtf{i}", [128, D], F32, c) for i in range(2)]
            qsw = self.sb("qsw", [128, D], F32, c)
            t1b = self.sb("t1b", [128, D], F32, c)
            t2b = self.sb("t2b", [128, D], F32, c)
            qbf1 = self.sb("qbf", [128, D], BF16, c)
            qbf = [qbf1, qbf1]
            cos4 = [self.sb(f"cos4{i}", [128, D], F32, c) for i in range(2)]
            sin4 = [self.sb(f"sin4{i}", [128, D], F32, c) for i in range(2)]
            stats = self.sb("stats", [128, 32], F32, c)
            mv = self.sb("mv", [128, 4], F32, c)
            psq = [self.ps(f"psq{i}", [128, 512], F32, c) for i in range(2)]

            wq_tok = self.load_wmat(self.wq, wq_bf, "d_w0", [])

            tok_norm = {}
            tok_xnw = {}
            tok_xnt_use = {}
            tok_xnt = {}
            tok_qev = {}
            tok_rope_v = {}
            tok_rope_p = {}
            tok_qbf_st = {}
            prev_t2 = {}
            sp = 0

            cos_tok = {}

            def stage_a(m):
                mi = m % 2
                gate = [(sems["c_ve"], tok_rope_v[m - 2]),
                        (sems["c_po"], tok_rope_p[m - 2])] if m >= 2 else []
                cos_tok[m] = self.load_trig(m, cos4[mi], sin4[mi], gate,
                                            sem=("d_cos" if mi == 0 else "d_cos1"))
                xl_toks = {}

                def emit_xload(tt):
                    qt = m * 4 + tt
                    ti = qt % 2
                    w = [(sems["c_ve"], tok_norm[qt - 2])] if qt - 2 in tok_norm else []
                    s.dma("sync", (lambda eng, qt=qt, ti=ti: eng.dma_start(
                        out=x_sb[ti][:], in_=self.x[qt * 128:(qt + 1) * 128, :])),
                        waits=w, inc=sems[f"d_x{ti}"])
                    xl_toks[tt] = s.val(sems[f"d_x{ti}"])

                emit_xload(0)
                emit_xload(1)
                for tt in range(4):
                    if tt >= 2:
                        emit_xload(tt)
                    qt = m * 4 + tt
                    ti = qt % 2
                    wv = [(sems[f"d_x{ti}"], xl_toks[tt])]
                    if qt - 2 in tok_xnw:
                        wv.append((sems[f"d_xnw{ti}"], tok_xnw[qt - 2]))
                    tok_norm[qt] = self.ln_tile(x_sb[ti], xn_bf[ti], stats, mv, wv)
                    w = [(sems["c_ve"], tok_norm[qt])]
                    if m - 2 in tok_xnt and tt == 0:
                        w.append((sems[f"d_xnt{(m - 2) % 2}"], tok_xnt[m - 2]))
                    tok_xnw[qt] = s.dma("sync", (lambda eng, qt=qt, ti=ti: eng.dma_start(
                        out=self.xn_scr[qt * 128:(qt + 1) * 128, :], in_=xn_bf[ti][:])),
                        waits=w, inc=sems[f"d_xnw{ti}"])

                w = [(sems["d_xnw0"], s.val(sems["d_xnw0"])),
                     (sems["d_xnw1"], s.val(sems["d_xnw1"]))]
                if m - 2 in tok_xnt_use:
                    w.append((sems["c_pe"], tok_xnt_use[m - 2]))
                for fc in range(NH):
                    s.dma("scalar", (lambda eng, m=m, mi=mi, fc=fc: eng.dma_start(
                        out=xnt[mi][:, fc * 512:(fc + 1) * 512],
                        in_=self.xn_scr[m * 512:(m + 1) * 512, fc * 128:(fc + 1) * 128],
                        transpose=True)),
                        waits=(w if fc == 0 else []), inc=sems[f"d_xnt{mi}"])
                tok_xnt[m] = s.val(sems[f"d_xnt{mi}"])

            def stage_b(m):
                mi = m % 2
                tok_pe = None
                for g in range(NG):
                    gi = g % 2
                    for u in range(GS):
                        h = g * GS + u
                        pw = [(sems[f"d_xnt{mi}"], tok_xnt[m]), (sems["d_w0"], wq_tok)]
                        if (h - 2) in tok_qev or m > 0:
                            hh = h - 2 if h - 2 >= 0 else NG * GS + (h - 2)
                            if hh in tok_qev:
                                pw.append((sems["c_se"], tok_qev[hh]))
                        if u == 0:
                            key = (m * NG + g) - 2
                            if key >= 0:
                                pw.append((sems["c_ve"], self._q_rope_tok[key]))
                        for fc in range(NH):
                            tok_pe = s.op("tensor", (lambda eng, h=h, fc=fc, mi=mi: eng.matmul(
                                psq[h % 2][:],
                                wq_bf[:, fc * D + h * 128:fc * D + (h + 1) * 128],
                                xnt[mi][:, fc * 512:(fc + 1) * 512],
                                start=(fc == 0), stop=(fc == NH - 1))),
                                waits=(pw if fc == 0 else []),
                                inc=(sems["c_pe"] if fc == NH - 1 else None))
                        tok_qev[h] = s.op("scalar", (lambda eng, h=h, u=u, gi=gi: eng.activation(
                            out=qtf[gi][:, u * 512:(u + 1) * 512], in_=psq[h % 2][:],
                            func=AF.Identity, bias=self.bqs_col[:, h:h + 1], scale=SCALE)),
                            waits=[(sems["c_pe"], tok_pe)], inc=sems["c_se"])
                    tok_rope, tok_t2 = self.rope(
                        m, gi, qtf[gi], qsw, t1b, t2b, qbf[gi],
                        cos4[mi], sin4[mi], cos_tok[m],
                        (sems["c_se"], tok_qev[g * GS + GS - 1]),
                        self._sp, prev_t2.get(0),
                        out_guard=(sems["d_qw"], s.val(sems["d_qw"])),
                        cos_sem=("d_cos" if mi == 0 else "d_cos1"),
                        prev_add=prev_t2.get(1))
                    prev_t2[0] = (sems["c_po"], tok_t2)
                    prev_t2[1] = (sems["c_ve"], tok_rope)
                    self._q_rope_tok[m * NG + g] = tok_rope
                    self._sp += 1
                    tok_rope_v[m] = tok_rope
                    tok_rope_p[m] = tok_t2
                    w = [(sems["c_ve"], tok_rope)]
                    tok_qbf_st[(m, g)] = s.dma("scalar", (lambda eng, m=m, g=g, gi=gi: eng.dma_start(
                        out=self.q_scr[:, g * GS:(g + 1) * GS, m * 512:(m + 1) * 512],
                        in_=qbf[gi][:].rearrange("p (u n) -> p u n", u=GS))),
                        waits=w, inc=sems["d_qw"])
                tok_xnt_use[m] = tok_pe

            self._q_rope_tok = {}
            self._sp = 0
            for m in range(NM):
                if m >= 1:
                    stage_b(m - 1)
                stage_a(m)
            stage_b(NM - 1)

            self.q_all_done = [(sems["d_qw"], s.val(sems["d_qw"]))]
            self.pq_end = [(sems[k], s.val(sems[k])) for k in
                           ("c_pe", "c_se", "c_ve", "c_po", "d_sw0", "d_sw1",
                            "d_xnw0", "d_xnw1")]

    def _pass_kv(self):
        s, sems, cst = self.s, self.sems, self.cst
        s.phase = "passkv"
        T, NM = self.T, self.NM
        s.barrier(self.pq_end + self.q_all_done)
        self.kv_scope = contextlib.ExitStack()
        self.ktc = self.sb("ktc", [128, NG * T], BF16, self.kv_scope)
        self.vc = self.sb("vc", [128, self.NT * 512], BF16, self.kv_scope)
        self.kabs = self.sb("kabs", [128, NG * NM], F32, self.kv_scope)
        with contextlib.ExitStack() as c:
            wk_bf = self.sb("wk_bf", [128, NH * 512], BF16, c)
            wv_bf = self.sb("wv_bf", [128, NH * 512], BF16, c)
            xnt = [self.sb(f"kxnt{i}", [128, NH * 512], BF16, c) for i in range(2)]
            ktf = [self.sb(f"ktf{i}", [128, D], F32, c) for i in range(2)]
            ksw = self.sb("ksw", [128, D], F32, c)
            t1b = self.sb("kt1b", [128, D], F32, c)
            t2b = self.sb("kt2b", [128, D], F32, c)
            kbf = [self.sb(f"kbf{i}", [128, D], BF16, c) for i in range(2)]
            cos4 = self.sb("kcos4", [128, D], F32, c)
            sin4 = self.sb("ksin4", [128, D], F32, c)
            psk = [self.ps(f"psk{i}", [128, 512], F32, c) for i in range(2)]
            psv = [self.ps(f"psv{i}", [128, 512], F32, c) for i in range(2)]

            bv_rep = self.sb("bv_rep", [128, 512], F32, c)
            self.bv_rep = bv_rep
            bvrow = self.sb("bvrow", [1, 512], F32, c)
            wk_tok = self.load_wmat(self.wk, wk_bf, "d_w1", [], nchunk=NH)
            wv_tok = self.load_wmat(self.wv, wv_bf, "d_w1", [], nchunk=NH)
            tok_bl = s.dma("sync", lambda eng: eng.dma_start(
                out=bvrow[:], in_=self.bv), inc=sems["d_cst"])
            tok_bp = s.op("tensor", lambda eng: eng.matmul(
                psk[0][:], cst["ones_row_f"][:], bvrow[:], start=True, stop=True),
                waits=[(sems["d_cst"], tok_bl)], inc=sems["c_pe"])
            tok_bv = s.op("vector", lambda eng: eng.tensor_copy(bv_rep[:], psk[0][:]),
                          waits=[(sems["c_pe"], tok_bp)], inc=sems["c_ve"])
            self._bv_done = (sems["c_ve"], tok_bv)

            tok_xnt_use = {}
            tok_xnt = {}
            tok_kev = {}
            tok_vev = {}
            tok_rope_v = {}
            tok_rope_p = {}
            tok_kc = {}
            prev_t2 = {}
            sp = 0
            for m in range(NM):
                mi = m % 2
                gate = [(sems["c_ve"], tok_rope_v[m - 1]),
                        (sems["c_po"], tok_rope_p[m - 1])] if m >= 1 else []
                cos_tok = self.load_trig(m, cos4, sin4, gate)
                w = [(sems["c_pe"], tok_xnt_use[m - 2])] if m - 2 in tok_xnt_use else []
                for fc in range(NH):
                    s.dma("scalar", (lambda eng, m=m, mi=mi, fc=fc: eng.dma_start(
                        out=xnt[mi][:, fc * 512:(fc + 1) * 512],
                        in_=self.xn_scr[m * 512:(m + 1) * 512, fc * 128:(fc + 1) * 128],
                        transpose=True)),
                        waits=(w if fc == 0 else []), inc=sems[f"d_xnt{mi}"])
                tok_xnt[m] = s.val(sems[f"d_xnt{mi}"])

                for g in range(NG):
                    w0 = [(sems[f"d_xnt{mi}"], tok_xnt[m]), (sems["d_w1"], wv_tok),
                          self._bv_done]
                    if g - 2 in tok_kev or m > 0:
                        key = g - 2 if g - 2 >= 0 else NG + g - 2
                        if key in tok_kev:
                            w0.append((sems["c_se"], tok_kev[key]))
                    if g == 0 and m >= 1:
                        w0.append((sems["c_ve"], tok_rope_v[m - 1]))
                    for fc in range(NH):
                        tok_pe = s.op("tensor", (lambda eng, g=g, fc=fc, mi=mi: eng.matmul(
                            psk[g % 2][:],
                            wk_bf[:, fc * 512 + g * 128:fc * 512 + (g + 1) * 128],
                            xnt[mi][:, fc * 512:(fc + 1) * 512],
                            start=(fc == 0), stop=(fc == NH - 1))),
                            waits=(w0 if fc == 0 else []),
                            inc=(sems["c_pe"] if fc == NH - 1 else None))
                    tok_kev[g] = s.op("scalar", (lambda eng, g=g, mi=mi: eng.activation(
                        out=ktf[mi][:, g * 512:(g + 1) * 512], in_=psk[g % 2][:],
                        func=AF.Identity, bias=self.bk_col[:, g:g + 1], scale=1.0)),
                        waits=[(sems["c_pe"], tok_pe)], inc=sems["c_se"])
                for tt in range(4):
                    qt = m * 4 + tt
                    w0 = []
                    if tt - 2 in tok_vev or m > 0:
                        key = tt - 2 if tt - 2 >= 0 else 4 + tt - 2
                        if key in tok_vev:
                            w0.append((sems["c_ve"], tok_vev[key]))
                    for fc in range(NH):
                        tok_pe = s.op("tensor", (lambda eng, tt=tt, fc=fc, mi=mi: eng.matmul(
                            psv[tt % 2][:],
                            xnt[mi][:, fc * 512 + tt * 128:fc * 512 + (tt + 1) * 128],
                            wv_bf[:, fc * 512:(fc + 1) * 512],
                            start=(fc == 0), stop=(fc == NH - 1))),
                            waits=(w0 if fc == 0 else []),
                            inc=(sems["c_pe"] if fc == NH - 1 else None))
                    tok_vev[tt] = s.op("vector", (lambda eng, qt=qt, tt=tt: eng.tensor_tensor(
                        out=self.vc[:, qt * 512:(qt + 1) * 512], in0=psv[tt % 2][:],
                        in1=self.bv_rep[:], op=ALU.add)),
                        waits=[(sems["c_pe"], tok_pe)], inc=sems["c_ve"])
                tok_xnt_use[m] = tok_pe

                tok_rope, tok_t2 = self.rope(
                    m, mi, ktf[mi], ksw, t1b, t2b, kbf[mi],
                    cos4, sin4, cos_tok,
                    (sems["c_se"], tok_kev[NG - 1]),
                    sp, prev_t2.get(0), prev_add=prev_t2.get(1))
                prev_t2[0] = (sems["c_po"], tok_t2)
                prev_t2[1] = (sems["c_ve"], tok_rope)
                sp += 1
                tok_rope_v[m] = tok_rope
                tok_rope_p[m] = tok_t2
                s.op("vector", lambda eng: eng.drain())
                tok_kc[m] = s.op("vector", (lambda eng, m=m, mi=mi: eng.tensor_copy(
                    self.ktc[:].rearrange("p (g t) -> p g t", g=NG)[:, :, m * 512:(m + 1) * 512],
                    kbf[mi][:].rearrange("p (g n) -> p g n", g=NG))),
                    waits=[(sems["c_ve"], tok_rope)], inc=sems["c_ve"])
                for g in range(NG):
                    s.op("vector", (lambda eng, m=m, g=g, mi=mi: eng.reduce_max(
                        out=self.kabs[:, m * NG + g:m * NG + g + 1],
                        in_=kbf[mi][:, g * 512:(g + 1) * 512], axis=AX.X,
                        apply_absolute_value=True)))
                s.op("vector", lambda eng: eng.drain())
                tok_rope_v[m] = s.op("vector", lambda eng: eng.nop(), inc=sems["c_ve"])

            self.kv_end = [(sems[k], s.val(sems[k])) for k in
                           ("c_pe", "c_se", "c_ve", "c_po", "d_sw0", "d_sw1")]

    def _precast_mat(self, k):
        s, sems = self.s, self.sems
        e, which = divmod(k, 3)
        src_t = [self.w1t, self.w3t, self.w2t][which]
        dst_t = [self.w1_bfs, self.w3_bfs, self.w2_bfs][which]
        nch = 8 if which == 2 else 16
        for part in range(2):
            s.dma("gpsimd", (lambda eng, e=e, src_t=src_t, dst_t=dst_t,
                             nch=nch, part=part: eng.dma_start(
                out=dst_t[e * 128:(e + 1) * 128, :]
                    .rearrange("p (c n) -> p c n", c=nch)
                    [:, part * (nch // 2):(part + 1) * (nch // 2), :],
                in_=src_t[e].rearrange("(c p) n -> p c n", p=128)
                    [:, part * (nch // 2):(part + 1) * (nch // 2), :])),
                inc=sems["d_wc"])

    def _pass_attn(self):
        s, sems, cst = self.s, self.sems, self.cst
        s.phase = "attn"
        T, NT = self.T, self.NT
        s.barrier(self.kv_end)
        with contextlib.ExitStack() as c:
            for nm, msrc in [("m_diag", self.c_m_diag), ("m_win", self.c_m_win),
                             ("m_wsink", self.c_m_wsink), ("m_sink", self.c_m_sink)]:
                t = self.sb("cst_" + nm, [128, 512], BF16, c)
                cst[nm] = t
                s.dma("sync", (lambda eng, t=t, msrc=msrc: eng.dma_start(
                    out=t[:], in_=msrc)), inc=sems["d_cst"])
            self._mask_tok = (sems["d_cst"], s.val(sems["d_cst"]))
            bo_rep = self.sb("bo_rep", [128, D], F32, c)
            self.bo_rep = bo_rep
            wo_bf = self.sb("wo_bf", [128, NH * D], BF16, c)
            negk = self.sb("negk", [1, NG * 128], BF16, c)
            kinf_row = self.sb("kinf_row", [1, NG * self.NM], F32, c)
            kinf_sc = self.sb("kinf_sc", [1, NG], F32, c)
            qt_sb = [self.sb(f"qt_sb{i}", [128, D], BF16, c) for i in range(2)]
            ax_sb = self.sb("ax_sb", [128, D], F32, c)
            x_sb = [ax_sb, ax_sb]

            l1row = self.sb("l1row", [1, D], BF16, c)
            pt_sb = [self.sb(f"pt_sb{i}", [128, 6 * 512], BF16, c) for i in range(2)]
            ot_sb = self.sb("ot_sb", [128, D], BF16, c)
            rs_sb = self.sb("rs_sb", [1, 512], F32, c)
            rinv = self.sb("rinv", [128, 512], F32, c)
            tout = [self.sb(f"tout{i}", [128, 512], F32, c) for i in range(2)]
            x1_sb = self.sb("x1_sb", [128, D], F32, c)
            xn2_bf = self.sb("xn2_bf", [128, D], BF16, c)
            absq = self.sb("absq", [128, D], BF16, c)
            stats = self.sb("astats", [128, 32], F32, c)
            mv = self.sb("amv", [128, 4], F32, c)
            psS = [self.ps(f"psS{i}", [128, 512], F32, c) for i in range(3)]
            psO = [self.ps(f"psO{i}", [128, 512], F32, c) for i in range(2)]
            psR = self.ps("psR", [128, 512], F32, c)
            psU = [self.ps(f"psU{i}", [128, 512], F32, c) for i in range(2)]

            wo_tok = self.load_wmat(self.wo, wo_bf, "d_w2", [])

            tok_bo = None
            for nq in range(4):
                tok_l = s.dma("sync", (lambda eng, nq=nq: eng.dma_start(
                    out=rs_sb[:], in_=self.bo[0:1, nq * 512:(nq + 1) * 512])),
                    waits=[tok_bo], inc=sems["d_cst"])
                tok_pe = s.op("tensor", (lambda eng, nq=nq: eng.matmul(
                    psU[0][:], cst["ones_row_f"][:], rs_sb[:],
                    start=True, stop=True)),
                    waits=[(sems["d_cst"], tok_l)], inc=sems["c_pe"])
                tok_v = s.op("vector", (lambda eng, nq=nq: eng.tensor_copy(
                    bo_rep[:, nq * 512:(nq + 1) * 512], psU[0][:])),
                    waits=[(sems["c_pe"], tok_pe)], inc=sems["c_ve"])
                tok_bo = (sems["c_ve"], tok_v)
            self._bo_done = tok_bo
            tok_u_free_init = tok_bo

            tok_kg = s.op("gpsimd", lambda eng: eng.tensor_reduce(
                out=kinf_row[:], in_=self.kabs[:], axis=AX.C, op=ALU.max),
                inc=sems["c_po"])
            for g in range(NG):
                s.op("vector", (lambda eng, g=g: eng.reduce_max(
                    out=kinf_sc[0:1, g:g + 1],
                    in_=kinf_row[0:1, :].rearrange("o (m g) -> o g m", g=NG)[0:1, g, :],
                    axis=AX.X)),
                    waits=([(sems["c_po"], tok_kg)] if g == 0 else []))
            s.op("vector", lambda eng: eng.drain())
            for g in range(NG):
                s.op("vector", (lambda eng, g=g: eng.tensor_scalar(
                    out=negk[0:1, g * 128:(g + 1) * 128], in0=cst["ones_row_f"][:],
                    scalar1=kinf_sc[0:1, g:g + 1], scalar2=-1.0,
                    op0=ALU.mult, op1=ALU.mult)))
            s.op("vector", lambda eng: eng.drain())
            tok_negk = s.op("vector", lambda eng: eng.nop(), inc=sems["c_ve"])

            bi = 0
            gg = 0
            tok_exp = {}
            tok_avdone = {}
            tok_otev = {}
            tok_rs_chain = None
            tok_u_free = {0: tok_u_free_init, 1: None}
            tok_l1 = {}
            tok_scores_end = {}
            tok_l1mm_end = {}
            tok_x_use = {}
            tok_x1w = {}
            tok_xn2w = {}
            tok_norm_prev = None

            NMAT = 3 * (NE + 1)
            for qt in range(NT):
                if qt < NMAT:
                    self._precast_mat(qt)
                ti = qt % 2
                blocks = blocks_for(qt)
                nb = len(blocks)
                w = [(sems["c_pe"], tok_scores_end[qt - 2])] if qt - 2 in tok_scores_end else []
                if qt == 0:
                    w += self.q_all_done
                for part in range(4):
                    s.dma("sync", (lambda eng, qt=qt, ti=ti, part=part: eng.dma_start(
                        out=qt_sb[ti][:].rearrange("p (h n) -> p h n", h=NH)
                            [:, part * 4:(part + 1) * 4, :],
                        in_=self.q_scr[:, part * 4:(part + 1) * 4, qt * 128:(qt + 1) * 128])),
                        waits=(w if part == 0 else []), inc=sems[f"d_qt{ti}"])
                qt_tok = s.val(sems[f"d_qt{ti}"])
                w = [(sems["c_ve"], tok_x_use[qt - 1])] if qt - 1 in tok_x_use else []
                s.dma("sync", (lambda eng, qt=qt, ti=ti: eng.dma_start(
                    out=x_sb[ti][:], in_=self.x[qt * 128:(qt + 1) * 128, :])),
                    waits=w, inc=sems[f"d_x{ti}"])
                xl_tok = s.val(sems[f"d_x{ti}"])
                w = [(sems[f"d_qt{ti}"], qt_tok)]
                if qt - 1 in tok_l1mm_end:
                    w.append((sems["c_pe"], tok_l1mm_end[qt - 1]))
                tok_absq = s.op("vector", (lambda eng, ti=ti: eng.scalar_tensor_tensor(
                    out=absq[:], in0=qt_sb[ti][:], scalar=-1.0, in1=qt_sb[ti][:],
                    op0=ALU.mult, op1=ALU.max)),
                    waits=w, inc=sems["c_ve"])
                for g in range(NG):
                    ub = g % 2
                    w = [(sems["c_ve"], tok_absq), tok_u_free[ub]]
                    tok_pe = s.op("tensor", (lambda eng, g=g, ub=ub: eng.matmul(
                        psU[ub][0:1, :], cst["ones128_bf"][:],
                        absq[:, g * 512:(g + 1) * 512], start=True, stop=True)),
                        waits=w, inc=sems["c_pe"])
                    ev_w = [(sems["c_pe"], tok_pe)]
                    if g == 0 and qt - 1 in tok_scores_end:
                        ev_w.append((sems["c_pe"], tok_scores_end[qt - 1]))
                    tok_l1[g] = s.op("scalar", (lambda eng, g=g, ub=ub: eng.activation(
                        out=l1row[0:1, g * 512:(g + 1) * 512], in_=psU[ub][0:1, :],
                        func=AF.Copy)),
                        waits=ev_w, inc=sems["c_se"])
                    tok_u_free[ub] = (sems["c_se"], tok_l1[g])
                tok_l1mm_end[qt] = tok_pe

                for g in range(NG):
                    pi = gg % 2
                    my_bis = []
                    for j, (kt, mask) in enumerate(blocks):
                        bank = bi % 3
                        w = [(sems[f"d_qt{ti}"], qt_tok), (sems["c_se"], tok_l1[g]),
                             (sems["c_ve"], tok_negk), self._mask_tok]
                        if (bi - 3) in tok_exp:
                            w.append((sems["c_se"], tok_exp[bi - 3]))
                        s.op("tensor", (lambda eng, g=g, kt=kt, bank=bank, ti=ti: eng.matmul(
                            psS[bank][:],
                            self.ktc[:, g * T + kt * 128:g * T + (kt + 1) * 128],
                            qt_sb[ti][:, g * 512:(g + 1) * 512],
                            start=True, stop=False)), waits=w)
                        if mask is not None:
                            s.op("tensor", (lambda eng, mask=mask, bank=bank: eng.matmul(
                                psS[bank][:], cst["idn_bf"][:], cst[mask][:],
                                start=False, stop=False)))
                        tok_pe = s.op("tensor", (lambda eng, g=g, bank=bank: eng.matmul(
                            psS[bank][:], negk[0:1, g * 128:(g + 1) * 128],
                            l1row[0:1, g * 512:(g + 1) * 512],
                            start=False, stop=True)), inc=sems["c_pe"])
                        w = [(sems["c_pe"], tok_pe)]
                        if gg - 2 in tok_avdone:
                            w.append((sems["c_pe"], tok_avdone[gg - 2]))
                        tok_exp[bi] = s.op("scalar", (lambda eng, j=j, bank=bank, pi=pi: eng.activation(
                            out=pt_sb[pi][:, j * 512:(j + 1) * 512], in_=psS[bank][:],
                            func=AF.Exp)), waits=w, inc=sems["c_se"])
                        my_bis.append(bi)
                        bi += 1
                    tok_scores_end[qt] = tok_pe
                    w_rs0 = [(sems["c_ve"], tok_rs_chain)] if tok_rs_chain else []
                    for j, (kt, mask) in enumerate(blocks):
                        w = [(sems["c_se"], tok_exp[my_bis[j]])]
                        if j == 0 and gg - 2 in tok_otev:
                            w.append((sems["c_ve"], tok_otev[gg - 2]))
                        tok_av = s.op("tensor", (lambda eng, g=g, kt=kt, j=j, pi=pi, gg=gg, nb=nb: eng.matmul(
                            psO[gg % 2][:],
                            self.vc[:, kt * 512 + g * 128:kt * 512 + (g + 1) * 128],
                            pt_sb[pi][:, j * 512:(j + 1) * 512],
                            start=(j == 0), stop=(j == nb - 1))),
                            waits=w, inc=(sems["c_pe"] if j == nb - 1 else None))
                    for j, (kt, mask) in enumerate(blocks):
                        tok_rs = s.op("tensor", (lambda eng, j=j, pi=pi, nb=nb: eng.matmul(
                            psR[0:1, :], cst["ones128_bf"][:],
                            pt_sb[pi][:, j * 512:(j + 1) * 512],
                            start=(j == 0), stop=(j == nb - 1))),
                            waits=(w_rs0 if j == 0 else []),
                            inc=(sems["c_pe"] if j == nb - 1 else None))
                    tok_avdone[gg] = tok_rs
                    tok_rsev = s.op("scalar", lambda eng: eng.activation(
                        out=rs_sb[:], in_=psR[0:1, :], func=AF.Copy),
                        waits=[(sems["c_pe"], tok_rs)], inc=sems["c_se"])
                    tok_repl = s.op("tensor", lambda eng: eng.matmul(
                        psR[:], cst["ones_row_f"][:], rs_sb[:], start=True, stop=True),
                        waits=[(sems["c_se"], tok_rsev)], inc=sems["c_pe"])
                    s.op("vector", lambda eng: eng.drain())
                    tok_rinv = s.op("vector", lambda eng: eng.reciprocal(rinv[:], psR[:]),
                                    waits=[(sems["c_pe"], tok_repl)], inc=sems["c_ve"])
                    tok_rs_chain = tok_rinv
                    s.op("vector", lambda eng: eng.drain())
                    tok_otev[gg] = s.op("vector", (lambda eng, g=g, gg=gg: eng.tensor_tensor(
                        out=ot_sb[:, g * 512:(g + 1) * 512], in0=psO[gg % 2][:],
                        in1=rinv[:], op=ALU.mult)),
                        waits=[(sems["c_pe"], tok_avdone[gg]), (sems["c_ve"], tok_rinv)],
                        inc=sems["c_ve"])
                    gg += 1

                w_ot = [(sems["c_ve"], tok_otev[gg - 1 - k]) for k in range(NG)]
                tok_x1_done = None
                tok_pool_q = getattr(self, "_tok_pool_q", {})
                for nq in range(4):
                    ub = nq % 2
                    w = list(w_ot) + [(sems["d_w2"], wo_tok), tok_u_free[ub]]
                    for h in range(NH):
                        tok_pe = s.op("tensor", (lambda eng, h=h, nq=nq, ub=ub: eng.matmul(
                            psU[ub][:], ot_sb[:, h * 128:(h + 1) * 128],
                            wo_bf[:, h * D + nq * 512:h * D + (nq + 1) * 512],
                            start=(h == 0), stop=(h == NH - 1))),
                            waits=(w if h == 0 else []),
                            inc=(sems["c_pe"] if h == NH - 1 else None))
                    tok_t = s.op("vector", (lambda eng, nq=nq, ub=ub, ti=ti: eng.tensor_tensor(
                        out=tout[ub][:], in0=psU[ub][:],
                        in1=x_sb[ti][:, nq * 512:(nq + 1) * 512], op=ALU.add)),
                        waits=[(sems["c_pe"], tok_pe), (sems[f"d_x{ti}"], xl_tok),
                               tok_pool_q.get(ub)],
                        inc=sems["c_ve"])
                    tok_u_free[ub] = (sems["c_ve"], tok_t)
                    w = [(sems["c_ve"], tok_t)]
                    if nq == 0 and qt == 0:
                        w.append(self._bo_done)
                    if nq == 0 and qt - 1 in tok_x1w:
                        w.append((sems["d_x1w"], tok_x1w[qt - 1]))
                        w.append((sems["d_xn2w"], tok_xn2w[qt - 1]))
                        w.append((sems["c_ve"], tok_norm_prev))
                    tok_x1_done = s.op("gpsimd", (lambda eng, nq=nq, ub=ub: eng.tensor_tensor(
                        out=x1_sb[:, nq * 512:(nq + 1) * 512], in0=tout[ub][:],
                        in1=self.bo_rep[:, nq * 512:(nq + 1) * 512], op=ALU.add)),
                        waits=w, inc=sems["c_po"])
                    tok_pool_q[ub] = (sems["c_po"], tok_x1_done)
                self._tok_pool_q = tok_pool_q
                tok_x_use[qt] = tok_t
                tok_x1w[qt] = s.dma("scalar", (lambda eng, qt=qt: eng.dma_start(
                    out=self.x1_scr[qt * 128:(qt + 1) * 128, :], in_=x1_sb[:])),
                    waits=[(sems["c_po"], tok_x1_done)], inc=sems["d_x1w"])
                tok_norm_prev = self.ln_tile(
                    x1_sb, xn2_bf, stats, mv,
                    [(sems["c_po"], tok_x1_done),
                     (sems["c_pe"], tok_l1mm_end[qt])] +
                    ([(sems["d_xn2w"], tok_xn2w[qt - 1])] if qt - 1 in tok_xn2w else []))
                tok_xn2w[qt] = s.dma("scalar", (lambda eng, qt=qt: eng.dma_start(
                    out=self.xn2_scr[qt * 128:(qt + 1) * 128, :], in_=xn2_bf[:])),
                    waits=[(sems["c_ve"], tok_norm_prev)], inc=sems["d_xn2w"])

            for k in range(min(NT, 3 * (NE + 1)), 3 * (NE + 1)):
                self._precast_mat(k)
            self._wc_done = (sems["d_wc"], s.val(sems["d_wc"]))
            self.attn_end = [(sems[k], s.val(sems[k])) for k in
                             ("c_pe", "c_se", "c_ve", "c_po", "d_x1w", "d_xn2w",
                              "d_qt0", "d_qt1", "d_x0", "d_x1")]

    def _pass_moe(self):
        s, sems, cst = self.s, self.sems, self.cst
        s.phase = "moe"
        T, NM = self.T, self.NM
        s.barrier(self.attn_end)
        self.kv_scope.close()
        with contextlib.ExitStack() as c:
            wbuf = [self.sb(f"wm{i}", [128, 16 * HID], BF16, c) for i in range(3)]
            xnt = self.sb("mxnt", [128, NH * 512], BF16, c)
            xg = [self.sb(f"xg{i}", [128, NH * 512], BF16, c) for i in range(2)]
            hT = self.sb("hT", [128, 8 * 512], BF16, c)
            ssb = self.sb("ssb", [128, 512], F32, c)
            grep = [self.sb(f"grep{i}", [128, 512], BF16, c) for i in range(2)]
            gTs = [self.sb(f"gTs{i}", [8, 128], BF16, c) for i in range(2)]
            g_row = [self.sb(f"g_row{i}", [1, 512], BF16, c) for i in range(2)]
            lgt = self.sb("lgt", [128, 16], F32, c)
            gtmp = self.sb("gtmp", [128, 8], F32, c)
            gbf = self.sb("gbf", [128, 8], BF16, c)
            osb = [self.sb(f"osb{i}", [128, D], F32, c) for i in range(2)]
            x1t = xg[0][:].bitcast(F32)
            psZ = [self.ps(f"psZ{i}", [128, 512], F32, c) for i in range(4)]
            psW = [self.ps(f"psW{i}", [128, 512], F32, c) for i in range(2)]
            psG = self.ps("psG", [128, 512], F32, c)
            psT = self.ps("psT", [8, 1024], BF16, c)

            bsrcs = [self.w1_bfs, self.w3_bfs, self.w2_bfs]
            buf_free = {}
            tok_w = {}

            def load_bf(e, which, waits):
                toks = None
                for part in range(2):
                    toks = s.dma("sync", (lambda eng, e=e, which=which, part=part:
                        eng.dma_start(
                            out=wbuf[which][:, part * 8192:(part + 1) * 8192],
                            in_=bsrcs[which][e * 128:(e + 1) * 128,
                                             part * 8192:(part + 1) * 8192])),
                        waits=(waits if part == 0 else []), inc=sems[f"d_wb{which}"])
                return toks

            for which in range(3):
                tok_w[(NE, which)] = load_bf(NE, which, [self._wc_done])

            tok_gtrans_ev = None
            tok_xg_done = None
            tok_osb_free = [None, None]
            tok_hT_done = None
            tok_grep_done = None
            tok_grep_prev = None
            tok_grep_free = {}
            self._g_ct = 0
            tok_x1_use = {}
            x1_tok = {}
            tok_lgcopy = None
            gsw_toks = {}
            tok_grow_use = {}
            prev_psW = {0: None, 1: None}
            silu_toks = {}
            h_toks = {}
            tok_xg_m = {}
            tok_xg_read = {}

            for ei, e in enumerate([NE] + list(range(NE))):
                shared = (e == NE)
                b1, b3, b2 = 0, 1, 2
                w1_tok, w3_tok, w2_tok = tok_w[(e, 0)], tok_w[(e, 1)], tok_w[(e, 2)]
                for m in range(NM + 1):
                    if m < NM and not shared:
                        par = (ei * NM + m) % 2
                        wld = [(sems["d_gsw"], s.val(sems["d_gsw"]))]
                        if par in tok_grow_use:
                            wld.append(tok_grow_use[par])
                        s.dma("scalar", (lambda eng, e=e, m=m, par=par: eng.dma_start(
                            out=g_row[par][:], in_=self.g_scr[e:e + 1, m * 512:(m + 1) * 512])),
                            waits=wld, inc=sems[f"d_gsr{par}"])
                        grow_tok = s.val(sems[f"d_gsr{par}"])
                    if m < NM:
                        w = [tok_xg_done]
                        for fc in range(NH):
                            s.dma("scalar", (lambda eng, m=m, fc=fc: eng.dma_start(
                                out=xnt[:, fc * 512:(fc + 1) * 512],
                                in_=self.xn2_scr[m * 512:(m + 1) * 512, fc * 128:(fc + 1) * 128],
                                transpose=True)),
                                waits=(w if fc == 0 else []), inc=sems["d_xnt0"])
                        xnt_tok = s.val(sems["d_xnt0"])
                    if shared and m == NM:
                        break

                    if shared:
                        for tt in range(4):
                            w = [(sems["d_xnt0"], xnt_tok)]
                            if tok_lgcopy is not None:
                                w.append((sems["c_ve"], tok_lgcopy))
                            for fc in range(NH):
                                tok_pe = s.op("tensor", (lambda eng, tt=tt, fc=fc: eng.matmul(
                                    psG[:, 0:8],
                                    xnt[:, fc * 512 + tt * 128:fc * 512 + (tt + 1) * 128],
                                    self.gatewT[:, fc * 8:(fc + 1) * 8],
                                    start=(fc == 0), stop=(fc == NH - 1))),
                                    waits=(w if fc == 0 else []),
                                    inc=(sems["c_pe"] if fc == NH - 1 else None))
                            tok_lg = s.op("vector", lambda eng: eng.tensor_copy(
                                lgt[:, 0:8], psG[:, 0:8]),
                                waits=[(sems["c_pe"], tok_pe)], inc=sems["c_ve"])
                            tok_lgcopy = tok_lg
                            s.op("vector", lambda eng: eng.drain())
                            s.op("vector", lambda eng: eng.reduce_max(
                                lgt[:, 8:9], lgt[:, 0:8], axis=AX.X))
                            s.op("vector", lambda eng: eng.drain())
                            s.op("vector", lambda eng: eng.tensor_scalar(
                                out=gtmp[:], in0=lgt[:, 0:8], scalar1=lgt[:, 8:9],
                                scalar2=None, op0=ALU.is_ge))
                            s.op("vector", lambda eng: eng.drain())
                            s.op("vector", lambda eng: eng.scalar_tensor_tensor(
                                out=gtmp[:], in0=gtmp[:], scalar=-1e30, in1=lgt[:, 0:8],
                                op0=ALU.mult, op1=ALU.add))
                            s.op("vector", lambda eng: eng.drain())
                            s.op("vector", lambda eng: eng.reduce_max(
                                lgt[:, 9:10], gtmp[:], axis=AX.X))
                            s.op("vector", lambda eng: eng.drain())
                            tok_keep = s.op("vector", lambda eng: eng.tensor_scalar(
                                out=gtmp[:], in0=lgt[:, 0:8], scalar1=lgt[:, 9:10],
                                scalar2=None, op0=ALU.is_ge), inc=sems["c_ve"])
                            s.op("vector", lambda eng: eng.drain())
                            tok_sig = s.op("scalar", lambda eng: eng.activation(
                                out=lgt[:, 0:8], in_=lgt[:, 0:8], func=AF.Sigmoid),
                                waits=[(sems["c_ve"], tok_keep)], inc=sems["c_se"])
                            tok_g = s.op("vector", lambda eng: eng.tensor_tensor(
                                out=gbf[:], in0=gtmp[:], in1=lgt[:, 0:8], op=ALU.mult),
                                waits=[(sems["c_se"], tok_sig)], inc=sems["c_ve"])
                            w = [(sems["c_ve"], tok_g)]
                            if tok_gtrans_ev is not None:
                                w.append(tok_gtrans_ev)
                            tok_tp = s.op("tensor", (lambda eng, tt=tt: eng.transpose(
                                psT[:, tt * 128:(tt + 1) * 128], gbf[:], cst["idn_bf"][:])),
                                waits=w, inc=sems["c_pe"])
                            qt = m * 4 + tt
                            w = [(sems["c_pe"], tok_tp),
                                 (sems["d_gsw"], s.val(sems["d_gsw"]))]
                            tok_gtev = s.op("vector", (lambda eng, tt=tt: eng.tensor_copy(
                                gTs[tt % 2][:],
                                psT[0:8, tt * 128:(tt + 1) * 128])),
                                waits=w, inc=sems["c_ve"])
                            tok_gtrans_ev = (sems["c_ve"], tok_gtev)
                            gsw_toks[qt] = s.dma("sync", (lambda eng, qt=qt, tt=tt: eng.dma_start(
                                out=self.g_scr[:, qt * 128:(qt + 1) * 128],
                                in_=gTs[tt % 2][:])),
                                waits=[(sems["c_ve"], tok_gtev)], inc=sems["d_gsw"])
                        zsrc = xnt
                        tok_zsrc = (sems["d_xnt0"], xnt_tok)
                    else:
                        if m < NM:
                            mi = m % 2
                            gp = self._g_ct % 2
                            self._g_ct += 1
                            w = [tok_grep_prev, (sems[f"d_gsr{par}"], grow_tok)]
                            tok_pe = s.op("tensor", (lambda eng, par=par: eng.matmul(
                                psG[:], cst["ones_col_bf"][:],
                                g_row[par][:], start=True, stop=True)),
                                waits=w, inc=sems["c_pe"])
                            tok_grow_use[par] = (sems["c_pe"], tok_pe)
                            tok_grep = s.op("scalar", (lambda eng, gp=gp: eng.activation(
                                out=grep[gp][:], in_=psG[:], func=AF.Copy)),
                                waits=[(sems["c_pe"], tok_pe),
                                       tok_grep_free.get(gp)], inc=sems["c_se"])
                            tok_grep_prev = (sems["c_se"], tok_grep)
                            for fc in range(NH):
                                xgw = []
                                if fc == 0:
                                    xgw = [(sems["c_se"], tok_grep),
                                           (sems["d_xnt0"], xnt_tok),
                                           tok_xg_read.get(mi)]
                                    if ei == 1 and m == 0 and (NM * 4 - 1) in tok_x1_use:
                                        xgw.append(tok_x1_use[NM * 4 - 1])
                                tok_xg = s.op("gpsimd", (lambda eng, fc=fc, mi=mi, gp=gp: eng.tensor_tensor(
                                    out=xg[mi][:, fc * 512:(fc + 1) * 512],
                                    in0=xnt[:, fc * 512:(fc + 1) * 512], in1=grep[gp][:],
                                    op=ALU.mult)),
                                    waits=xgw,
                                    inc=(sems["c_po"] if fc == NH - 1 else None))
                            tok_xg_m[m] = (sems["c_po"], tok_xg)
                            tok_xg_done = (sems["c_po"], tok_xg)
                            tok_grep_free[gp] = (sems["c_po"], tok_xg)
                        if m == 0:
                            continue
                        m = m - 1
                        zsrc = xg[m % 2]
                        tok_zsrc = tok_xg_m[m]

                    last_z = None
                    for hc in range(8):
                        zb1 = hc % 2
                        zb3 = 2 + hc % 2
                        w1w = [tok_zsrc, (sems[f"d_wb{b1}"], w1_tok)]
                        w1w.append(silu_toks.get(hc - 2 if hc >= 2 else 6 + hc))
                        for fc in range(NH):
                            tok_z1 = s.op("tensor", (lambda eng, hc=hc, fc=fc, zb1=zb1, b1=b1, zsrc=zsrc: eng.matmul(
                                psZ[zb1][:],
                                wbuf[b1][:, fc * HID + hc * 128:fc * HID + (hc + 1) * 128],
                                zsrc[:, fc * 512:(fc + 1) * 512],
                                start=(fc == 0), stop=(fc == NH - 1))),
                                waits=(w1w if fc == 0 else []),
                                inc=(sems["c_pe"] if fc == NH - 1 else None))
                        buf_free[b1] = (sems["c_pe"], tok_z1)
                        w3w = [(sems[f"d_wb{b3}"], w3_tok)]
                        w3w.append(h_toks.get(hc - 2 if hc >= 2 else 6 + hc))
                        for fc in range(NH):
                            tok_z3 = s.op("tensor", (lambda eng, hc=hc, fc=fc, zb3=zb3, b3=b3, zsrc=zsrc: eng.matmul(
                                psZ[zb3][:],
                                wbuf[b3][:, fc * HID + hc * 128:fc * HID + (hc + 1) * 128],
                                zsrc[:, fc * 512:(fc + 1) * 512],
                                start=(fc == 0), stop=(fc == NH - 1))),
                                waits=(w3w if fc == 0 else []),
                                inc=(sems["c_pe"] if fc == NH - 1 else None))
                        last_z = tok_z3
                        buf_free[b3] = (sems["c_pe"], tok_z3)
                        tok_silu = s.op("scalar", (lambda eng, zb1=zb1: eng.activation(
                            out=ssb[:], in_=psZ[zb1][:], func=AF.Silu)),
                            waits=[(sems["c_pe"], tok_z1),
                                   (h_toks[hc - 1] if hc >= 1 else None)],
                            inc=sems["c_se"])
                        hw = [(sems["c_se"], tok_silu), (sems["c_pe"], tok_z3)]
                        if hc == 0:
                            hw.append(tok_hT_done)
                        tok_h = s.op("vector", (lambda eng, hc=hc, zb3=zb3: eng.tensor_tensor(
                            out=hT[:, hc * 512:(hc + 1) * 512], in0=ssb[:],
                            in1=psZ[zb3][:], op=ALU.mult)),
                            waits=hw, inc=sems["c_ve"])
                        silu_toks[hc] = (sems["c_se"], tok_silu)
                        h_toks[hc] = (sems["c_ve"], tok_h)
                    tok_hT = tok_h
                    if shared:
                        tok_xg_done = (sems["c_pe"], last_z)
                    else:
                        tok_xg_read[m % 2] = (sems["c_pe"], last_z)

                    for tt in range(4):
                        qt = m * 4 + tt
                        if shared:
                            w = [tok_x1_use[qt - 1]] if qt - 1 in tok_x1_use else []
                            s.dma("sync", (lambda eng, qt=qt: eng.dma_start(
                                out=x1t[:, 0:D], in_=self.x1_scr[qt * 128:(qt + 1) * 128, :])),
                                waits=w, inc=sems["d_x1r"])
                            x1_tok[qt] = s.val(sems["d_x1r"])
                        oi = tt % 2
                        tok_ev_last = None
                        ev_toks = {}
                        for nq in range(4):
                            wb = nq % 2
                            w = [(sems["c_ve"], tok_hT), (sems[f"d_wb{b2}"], w2_tok),
                                 tok_osb_free[oi] if nq == 0 else None,
                                 prev_psW[wb]]
                            for hc in range(8):
                                tok_pe = s.op("tensor", (lambda eng, tt=tt, nq=nq, hc=hc, wb=wb, b2=b2: eng.matmul(
                                    psW[wb][:],
                                    hT[:, hc * 512 + tt * 128:hc * 512 + (tt + 1) * 128],
                                    wbuf[b2][:, hc * D + nq * 512:hc * D + (nq + 1) * 512],
                                    start=(hc == 0), stop=(hc == 7))),
                                    waits=(w if hc == 0 else []),
                                    inc=(sems["c_pe"] if hc == 7 else None))
                            if shared:
                                tok_ev = s.op("vector", (lambda eng, nq=nq, oi=oi, wb=wb: eng.tensor_tensor(
                                    out=osb[oi][:, nq * 512:(nq + 1) * 512], in0=psW[wb][:],
                                    in1=x1t[:, nq * 512:(nq + 1) * 512], op=ALU.add)),
                                    waits=[(sems["c_pe"], tok_pe),
                                           (sems["d_x1r"], x1_tok[qt])],
                                    inc=sems["c_ve"])
                                sem_ev = sems["c_ve"]
                            else:
                                if nq % 2 == 0:
                                    tok_ev = s.op("vector", (lambda eng, nq=nq, oi=oi, wb=wb: eng.tensor_copy(
                                        osb[oi][:, nq * 512:(nq + 1) * 512], psW[wb][:])),
                                        waits=[(sems["c_pe"], tok_pe)], inc=sems["c_ve"])
                                    sem_ev = sems["c_ve"]
                                else:
                                    tok_ev = s.op("scalar", (lambda eng, nq=nq, oi=oi, wb=wb: eng.activation(
                                        out=osb[oi][:, nq * 512:(nq + 1) * 512],
                                        in_=psW[wb][:], func=AF.Copy)),
                                        waits=[(sems["c_pe"], tok_pe)], inc=sems["c_se"])
                                    sem_ev = sems["c_se"]
                            ev_toks[nq] = (sem_ev, tok_ev)
                            prev_psW[wb] = (sem_ev, tok_ev)
                            tok_ev_last = (sem_ev, tok_ev)
                            buf_free[b2] = (sems["c_pe"], tok_pe)
                        if shared:
                            tok_x1_use[qt] = tok_ev_last
                            tok_store = s.dma("sync", (lambda eng, qt=qt, oi=oi: eng.dma_start(
                                out=self.out[qt * 128:(qt + 1) * 128, :], in_=osb[oi][:])),
                                waits=[ev_toks[2], ev_toks[3],
                                       (sems["d_outS"], s.val(sems["d_outS"]))],
                                inc=sems["d_outS"])
                            tok_osb_free[oi] = (sems["d_outS"], s.val(sems["d_outS"]))
                        else:
                            gsem = "d_outG" if oi == 0 else "d_outG1"
                            w = [ev_toks[0], ev_toks[1], ev_toks[2], ev_toks[3],
                                 (sems["d_outS"], s.val(sems["d_outS"])),
                                 (sems[gsem], s.val(sems[gsem]))]
                            tok_store = s.dma("gpsimd", (lambda eng, qt=qt, oi=oi: eng.dma_start(
                                out=self.out[qt * 128:(qt + 1) * 128, :], in_=osb[oi][:],
                                accum_op=ALU.add)),
                                waits=w, inc=sems[gsem])
                            tok_osb_free[oi] = (sems[gsem], s.val(sems[gsem]))
                    tok_hT_done = (sems["c_pe"], tok_pe)

                if ei < NE:
                    ne = 0 if shared else e + 1
                    for nw in range(3):
                        tok_w[(ne, nw)] = load_bf(ne, nw, [buf_free.get(nw)])

            s.op("sync", lambda eng: eng.nop(),
                 waits=[(sems["d_outS"], s.val(sems["d_outS"])),
                        (sems["d_outG"], s.val(sems["d_outG"])),
                        (sems["d_outG1"], s.val(sems["d_outG1"]))])


def _build(T):
    return Builder(T).build()


def kernel(L0_attn_x, L0_attn_Wq, L0_attn_bq, L0_attn_Wk, L0_attn_bk,
           L0_attn_Wv, L0_attn_bv, L0_attn_Wo, L0_attn_bo,
           L0_ffn_gate_w, L0_ffn_w1, L0_ffn_w2, L0_ffn_w3,
           L0_ffn_sw1, L0_ffn_sw2, L0_ffn_sw3):
    T = np.asarray(L0_attn_x).shape[1]
    n_cores = 8
    consts = host_consts(T)
    f32 = np.float32

    def c(a):
        return np.ascontiguousarray(np.asarray(a))

    w1t = c(np.concatenate([np.asarray(L0_ffn_w1).transpose(0, 2, 1),
                            np.asarray(L0_ffn_sw1).T[None]], axis=0).astype(f32))
    w3t = c(np.concatenate([np.asarray(L0_ffn_w3).transpose(0, 2, 1),
                            np.asarray(L0_ffn_sw3).T[None]], axis=0).astype(f32))
    w2t = c(np.concatenate([np.asarray(L0_ffn_w2).transpose(0, 2, 1),
                            np.asarray(L0_ffn_sw2).T[None]], axis=0).astype(f32))

    base = {
        "wq": c(L0_attn_Wq), "bq": c(np.asarray(L0_attn_bq)[None, :]),
        "wk": c(L0_attn_Wk), "bk": c(np.asarray(L0_attn_bk)[None, :]),
        "wv": c(L0_attn_Wv), "bv": c(np.asarray(L0_attn_bv)[None, :]),
        "wo": c(L0_attn_Wo), "bo": c(np.asarray(L0_attn_bo)[None, :]),
        "gw": c(L0_ffn_gate_w), "w1t": w1t, "w2t": w2t, "w3t": w3t,
    }
    base.update(consts)

    nc = _build(T)
    in_maps = []
    xfull = np.asarray(L0_attn_x)
    for core in range(n_cores):
        m = dict(base)
        m["x"] = c(xfull[core])
        in_maps.append(m)
    res = run_bass_kernel_spmd(nc, in_maps, core_ids=list(range(n_cores)))
    return np.stack([r["out"] for r in res.results], axis=0)


# version 19
# speedup vs baseline: 1.0042x; 1.0042x over previous
import contextlib
import numpy as np
import ml_dtypes
import concourse.bass as bass
import concourse.mybir as mybir
from concourse.bass_utils import run_bass_kernel_spmd

F32 = mybir.dt.float32
BF16 = mybir.dt.bfloat16
AF = mybir.ActivationFunctionType
ALU = mybir.AluOpType
AX = mybir.AxisListType
BF = ml_dtypes.bfloat16

D = 2048
NH = 16
NG = 4
GS = 4
DH = 128
HID = 1024
NE = 8
SINK = 4
SCALE = DH ** -0.5
EPS = 1e-5
NEG = -1e30
ENGINES = ("sync", "scalar", "vector", "tensor", "gpsimd")


class Sched:
    def __init__(self):
        self.items = []
        self.counts = {}
        self.pending = {e: [] for e in ENGINES}
        self.phase = "setup"

    def _bump(self, sem, amt):
        k = id(sem)
        self.counts[k] = self.counts.get(k, 0) + amt
        return self.counts[k]

    def val(self, sem):
        return self.counts.get(id(sem), 0)

    def op(self, eng, fn, waits=(), inc=None, amt=1):
        tok = None
        if inc is not None:
            tok = self._bump(inc, amt)
        w = self.pending[eng] + [x for x in waits if x is not None and x[0] is not None]
        self.pending[eng] = []
        self.items.append((eng, fn, w, inc, amt, self.phase))
        return tok

    def dma(self, eng, fn, waits=(), inc=None):
        return self.op(eng, fn, waits, inc, 16)

    def barrier(self, waits):
        ws = [x for x in waits if x is not None and x[0] is not None]
        for e in ENGINES:
            self.pending[e] = self.pending[e] + ws

    def emit(self, block, nc):
        for engname in ENGINES:
            def make(engname=engname):
                def emitter(eng):
                    seen = {}
                    for (e, fn, waits, incsem, amt, phase) in self.items:
                        if e != engname:
                            continue
                        with nc.named_scope(phase):
                            for (sm, v) in waits:
                                k = id(sm)
                                if seen.get(k, -1) < v:
                                    eng.wait_ge(sm, v)
                                    seen[k] = v
                            inst = fn(eng)
                            if incsem is not None:
                                inst.then_inc(incsem, amt)
                return emitter
            getattr(block, engname)(make())


def host_consts(T):
    half = DH // 2
    inv = 1.0 / (10000.0 ** (np.arange(half, dtype=np.float64) * 2.0 / DH))
    t = np.arange(T, dtype=np.float64)
    ang = np.outer(inv, t)
    cos = np.cos(ang).astype(np.float32)
    sin = np.sin(ang).astype(np.float32)
    cos2 = np.concatenate([cos, cos], 0)
    sin2n = np.concatenate([-sin, sin], 0)

    il = np.arange(128)[None, :]
    jl = np.arange(128)[:, None]

    def mk(keep):
        keep = np.broadcast_to(keep, (128, 128))
        m = np.where(keep, 0.0, NEG).astype(np.float32)
        return np.tile(m, (1, 4)).astype(BF)
    return {
        "cos2": cos2, "sin2n": sin2n,
        "m_diag": mk(il >= jl),
        "m_win": mk(jl >= il),
        "m_wsink": mk((jl >= il) | (jl < SINK)),
        "m_sink": mk(jl < SINK),
        "idn_bf": np.eye(128, dtype=BF),
        "ones_col_bf": np.ones((1, 128), dtype=BF),
        "ones128_bf": np.ones((128, 1), dtype=BF),
        "ones_row_f": np.ones((1, 128), dtype=np.float32),
        "one1": np.ones((1, 1), dtype=np.float32),
        "eps_col": np.full((128, 1), EPS, dtype=np.float32),
    }


def blocks_for(qt):
    if qt <= 4:
        out = []
        for kt in range(qt + 1):
            if kt == qt:
                out.append((kt, "m_diag"))
            elif kt == 0 and qt == 4:
                out.append((kt, "m_wsink"))
            else:
                out.append((kt, None))
        return out
    return [(qt - 4, "m_win"), (qt - 3, None), (qt - 2, None), (qt - 1, None),
            (qt, "m_diag"), (0, "m_sink")]


class Builder:
    def __init__(self, T):
        assert T % 512 == 0
        self.T = T
        self.NT = T // 128
        self.NM = T // 512
        nc = bass.Bass("TRN2", target_bir_lowering=False, debug=False)
        self.nc = nc
        self.s = Sched()
        self.ctx = contextlib.ExitStack()

        dt = nc.dram_tensor
        self.x = dt("x", [T, D], F32, kind="ExternalInput").ap()
        self.wq = dt("wq", [D, D], F32, kind="ExternalInput").ap()
        self.bq = dt("bq", [1, D], F32, kind="ExternalInput").ap()
        self.wk = dt("wk", [D, 512], F32, kind="ExternalInput").ap()
        self.bk = dt("bk", [1, 512], F32, kind="ExternalInput").ap()
        self.wv = dt("wv", [D, 512], F32, kind="ExternalInput").ap()
        self.bv = dt("bv", [1, 512], F32, kind="ExternalInput").ap()
        self.wo = dt("wo", [D, D], F32, kind="ExternalInput").ap()
        self.bo = dt("bo", [1, D], F32, kind="ExternalInput").ap()
        self.gw = dt("gw", [NE, D], F32, kind="ExternalInput").ap()
        self.w1t = dt("w1t", [NE + 1, D, HID], F32, kind="ExternalInput").ap()
        self.w2t = dt("w2t", [NE + 1, HID, D], F32, kind="ExternalInput").ap()
        self.w3t = dt("w3t", [NE + 1, D, HID], F32, kind="ExternalInput").ap()
        self.c_cos2 = dt("cos2", [128, T], F32, kind="ExternalInput").ap()
        self.c_sin2n = dt("sin2n", [128, T], F32, kind="ExternalInput").ap()
        for nm in ("m_diag", "m_win", "m_wsink", "m_sink"):
            setattr(self, "c_" + nm, dt(nm, [128, 512], BF16, kind="ExternalInput").ap())
        self.c_idn_bf = dt("idn_bf", [128, 128], BF16, kind="ExternalInput").ap()
        self.c_ones_col_bf = dt("ones_col_bf", [1, 128], BF16, kind="ExternalInput").ap()
        self.c_ones128_bf = dt("ones128_bf", [128, 1], BF16, kind="ExternalInput").ap()
        self.c_ones_row_f = dt("ones_row_f", [1, 128], F32, kind="ExternalInput").ap()
        self.c_one1 = dt("one1", [1, 1], F32, kind="ExternalInput").ap()
        self.c_eps_col = dt("eps_col", [128, 1], F32, kind="ExternalInput").ap()
        self.xn_scr = dt("xn_scr", [T, D], BF16, kind="Internal").ap()
        self.q_scr = dt("q_scr", [128, NH, T], BF16, kind="Internal").ap()
        self.sw_scr = dt("sw_scr", [2, 128, D], F32, kind="Internal").ap()
        self.x1_scr = dt("x1_scr", [T, D], F32, kind="Internal").ap()
        self.xn2_scr = dt("xn2_scr", [T, D], BF16, kind="Internal").ap()
        self.g_scr = dt("g_scr", [NE, T], BF16, kind="Internal").ap()
        self.w1_bfs = dt("w1_bfs", [(NE + 1) * 128, 16 * HID], BF16, kind="Internal").ap()
        self.w3_bfs = dt("w3_bfs", [(NE + 1) * 128, 16 * HID], BF16, kind="Internal").ap()
        self.w2_bfs = dt("w2_bfs", [(NE + 1) * 128, 16 * HID], BF16, kind="Internal").ap()
        self.out = dt("out", [T, D], F32, kind="ExternalOutput").ap()

        self.sems = {}
        for nm in ["c_pe", "c_se", "c_ve", "c_po",
                   "d_w0", "d_w1", "d_w2", "d_w3",
                   "d_x0", "d_x1",
                   "d_xnw0", "d_xnw1",
                   "d_xnt0", "d_xnt1",
                   "d_sw0", "d_sw1",
                   "d_qw",
                   "d_qt0", "d_qt1",
                   "d_cos", "d_cos1", "d_cst", "d_gw",
                   "d_x1w", "d_x1r", "d_xn2w",
                   "d_outS", "d_outG", "d_outG1", "d_gsw", "d_gsr0", "d_gsr1", "d_wc",
                   "d_wb0", "d_wb1", "d_wb2"]:
            self.sems[nm] = self.ctx.enter_context(self.nc.semaphore(nm))

    def sb(self, name, shape, dtype, stack=None):
        return (stack or self.ctx).enter_context(self.nc.sbuf_tensor(name, shape, dtype))

    def ps(self, name, shape, dtype, stack=None):
        return (stack or self.ctx).enter_context(self.nc.psum_tensor(name, shape, dtype))

    def load_wmat(self, src_ap, dst, sem, waits, nchunk=16):
        s = self.s
        for part in range(4):
            s.dma("gpsimd", (lambda eng, part=part, src_ap=src_ap, dst=dst, nchunk=nchunk:
                  eng.dma_start(
                      out=dst[:].rearrange("p (c n) -> p c n", c=nchunk)
                          [:, part * (nchunk // 4):(part + 1) * (nchunk // 4), :],
                      in_=src_ap.rearrange("(c p) n -> p c n", p=128)
                          [:, part * (nchunk // 4):(part + 1) * (nchunk // 4), :])),
                  waits=(waits if part == 0 else []), inc=self.sems[sem])
        return s.val(self.sems[sem])

    def ln_tile(self, x_ap, out_bf, stats, mv, first_waits):
        s, sems = self.s, self.sems
        for k in range(4):
            s.op("vector", (lambda eng, k=k, x_ap=x_ap, stats=stats: eng.bn_stats(
                stats[:, k * 6:(k + 1) * 6], x_ap[:, k * 512:(k + 1) * 512])),
                waits=(first_waits if k == 0 else []))
        s.op("vector", lambda eng: eng.drain())
        s.op("vector", (lambda eng, stats=stats, mv=mv: eng.bn_aggr(mv[:, 0:2], stats[:, 0:24])))
        s.op("vector", lambda eng: eng.drain())
        tok = s.op("vector", lambda eng: eng.nop(), inc=sems["c_ve"])
        tok_sq = s.op("scalar", (lambda eng, mv=mv: eng.activation(
            out=mv[:, 2:3], in_=mv[:, 1:2], func=AF.Sqrt,
            bias=self.cst["eps_col"][:], scale=1.0)),
            waits=[(sems["c_ve"], tok)], inc=sems["c_se"])
        s.op("vector", (lambda eng, mv=mv: eng.reciprocal(mv[:, 3:4], mv[:, 2:3])),
             waits=[(sems["c_se"], tok_sq)])
        s.op("vector", lambda eng: eng.drain())
        return s.op("vector", (lambda eng, x_ap=x_ap, out_bf=out_bf, mv=mv: eng.tensor_scalar(
            out=out_bf[:], in0=x_ap[:], scalar1=mv[:, 0:1], scalar2=mv[:, 3:4],
            op0=ALU.subtract, op1=ALU.mult)), inc=sems["c_ve"])

    def rope(self, m, gi, src_f32, qsw, t1b, t2b, out_bf, cos4, sin4, cos_tok,
             tok_src, slot_par, prev_t2, out_guard=None, cos_sem="d_cos",
             prev_add=None):
        s, sems = self.s, self.sems
        slot = 0
        swsem = sems["d_sw0"]
        tok_sw_w = s.dma("sync", (lambda eng, slot=slot, src_f32=src_f32: eng.dma_start(
            out=self.sw_scr[slot], in_=src_f32[:])),
            waits=[tok_src, prev_t2, prev_add], inc=swsem)
        s.dma("sync", (lambda eng, slot=slot, qsw=qsw: eng.dma_start(
            out=qsw[0:64, :], in_=self.sw_scr[slot, 64:128, :])),
            waits=[(swsem, tok_sw_w)], inc=swsem)
        tok_sw_r = s.dma("sync", (lambda eng, slot=slot, qsw=qsw: eng.dma_start(
            out=qsw[64:128, :], in_=self.sw_scr[slot, 0:64, :])), inc=swsem)
        s.op("vector", lambda eng: eng.drain())
        tok_t1 = s.op("vector", (lambda eng, src_f32=src_f32, t1b=t1b, cos4=cos4: eng.tensor_tensor(
            out=t1b[:], in0=src_f32[:], in1=cos4[:], op=ALU.mult)),
            waits=[tok_src, (sems[cos_sem], cos_tok)], inc=sems["c_ve"])
        tok_t2 = s.op("gpsimd", (lambda eng, qsw=qsw, t2b=t2b, sin4=sin4: eng.tensor_tensor(
            out=t2b[:], in0=qsw[:], in1=sin4[:], op=ALU.mult)),
            waits=[(swsem, tok_sw_r), (sems[cos_sem], cos_tok), prev_add], inc=sems["c_po"])
        s.op("vector", lambda eng: eng.drain())
        tok_out = s.op("vector", (lambda eng, t1b=t1b, t2b=t2b, out_bf=out_bf: eng.tensor_tensor(
            out=out_bf[:], in0=t1b[:], in1=t2b[:], op=ALU.add)),
            waits=[(sems["c_po"], tok_t2), (sems["c_ve"], tok_t1), out_guard],
            inc=sems["c_ve"])
        return tok_out, tok_t2

    def load_trig(self, m, cos4, sin4, gate, sem="d_cos"):
        s, sems = self.s, self.sems
        for r in range(4):
            s.dma("scalar", (lambda eng, m=m, r=r, cos4=cos4: eng.dma_start(
                out=cos4[:, r * 512:(r + 1) * 512],
                in_=self.c_cos2[:, m * 512:(m + 1) * 512])),
                waits=(gate if r == 0 else []), inc=sems[sem])
            s.dma("scalar", (lambda eng, m=m, r=r, sin4=sin4: eng.dma_start(
                out=sin4[:, r * 512:(r + 1) * 512],
                in_=self.c_sin2n[:, m * 512:(m + 1) * 512])), inc=sems[sem])
        return s.val(sems[sem])

    def build(self):
        self._setup()
        self._pass_q()
        self._pass_kv()
        self._pass_attn()
        self._pass_moe()
        block = self.ctx.enter_context(self.nc.Block())
        self.s.emit(block, self.nc)
        self.ctx.close()
        return self.nc

    def _setup(self):
        s, sems = self.s, self.sems
        cst = {}
        for nm, shape, dty, src in [
            ("idn_bf", [128, 128], BF16, self.c_idn_bf),
            ("ones_col_bf", [1, 128], BF16, self.c_ones_col_bf),
            ("ones128_bf", [128, 1], BF16, self.c_ones128_bf),
            ("ones_row_f", [1, 128], F32, self.c_ones_row_f),
            ("one1", [1, 1], F32, self.c_one1),
            ("eps_col", [128, 1], F32, self.c_eps_col),
        ]:
            t = self.sb("cst_" + nm, shape, dty)
            cst[nm] = t
            s.dma("sync", (lambda eng, t=t, src=src: eng.dma_start(out=t[:], in_=src)),
                  inc=sems["d_cst"])
        self.cst = cst
        self.bqs_col = self.sb("bqs_col", [128, NH], F32)
        self.bk_col = self.sb("bk_col", [128, NG], F32)
        self.gatewT = self.sb("gatewT", [128, NH * NE], BF16)

        with contextlib.ExitStack() as c:
            for nm, shape, dty, src in [
                ("bq_row", [1, D], F32, self.bq),
                ("bk_row", [1, 512], F32, self.bk),
            ]:
                t = self.sb("cst_" + nm, shape, dty, c)
                cst[nm] = t
                s.dma("sync", (lambda eng, t=t, src=src: eng.dma_start(out=t[:], in_=src)),
                      inc=sems["d_cst"])
            gw_bf = self.sb("gw_bf", [NE, D], BF16, c)
            s.dma("gpsimd", lambda eng: eng.dma_start(out=gw_bf[:], in_=self.gw),
                  inc=sems["d_gw"])
            psA = self.ps("psA", [128, 512], F32, c)
            psB = self.ps("psB", [128, 1024], BF16, c)
            cw = (sems["d_cst"], s.val(sems["d_cst"]))
            gwd = (sems["d_gw"], s.val(sems["d_gw"]))
            for h in range(NH):
                tok_pe = s.op("tensor", (lambda eng, h=h, psA=psA: eng.matmul(
                    psA[:, h:h + 1], self.cst["bq_row"][0:1, h * 128:(h + 1) * 128],
                    self.cst["one1"][:], start=True, stop=True)),
                    waits=([cw] if h == 0 else []), inc=sems["c_pe"])
            for g in range(NG):
                tok_pe = s.op("tensor", (lambda eng, g=g, psA=psA: eng.matmul(
                    psA[:, 16 + g:17 + g], self.cst["bk_row"][0:1, g * 128:(g + 1) * 128],
                    self.cst["one1"][:], start=True, stop=True)), inc=sems["c_pe"])
            tok_v = s.op("vector", (lambda eng, psA=psA: eng.tensor_scalar(
                out=self.bqs_col[:], in0=psA[:, 0:16], scalar1=SCALE, scalar2=None,
                op0=ALU.mult)),
                waits=[(sems["c_pe"], tok_pe)], inc=sems["c_ve"])
            tok_v = s.op("vector", (lambda eng, psA=psA: eng.tensor_copy(
                self.bk_col[:], psA[:, 16:20])), inc=sems["c_ve"])


            for fc in range(NH):
                tok_pe = s.op("tensor", (lambda eng, fc=fc, psB=psB, gw_bf=gw_bf: eng.transpose(
                    psB[:, fc * 8:(fc + 1) * 8], gw_bf[:, fc * 128:(fc + 1) * 128],
                    self.cst["idn_bf"][0:8, 0:8])),
                    waits=([gwd, cw] if fc == 0 else []), inc=sems["c_pe"])
            tok_v3 = s.op("vector", (lambda eng, psB=psB: eng.tensor_copy(
                self.gatewT[:], psB[:, 0:NH * NE])),
                waits=[(sems["c_pe"], tok_pe)], inc=sems["c_ve"])
            self.setup_done = [(sems["c_ve"], s.val(sems["c_ve"])),
                               (sems["c_pe"], s.val(sems["c_pe"]))]

    def _pass_q(self):
        s, sems, cst = self.s, self.sems, self.cst
        s.phase = "passq"
        NM, NT = self.NM, self.NT
        s.barrier(self.setup_done)
        with contextlib.ExitStack() as c:
            wq_bf = self.sb("wq_bf", [128, NH * D], BF16, c)
            qx_sb = self.sb("qx_sb", [128, D], F32, c)
            qx_sb2 = self.sb("qx_sb2", [128, D], F32, c)
            x_sb = [qx_sb, qx_sb2]
            xn_bf = [self.sb(f"xn_bf{i}", [128, D], BF16, c) for i in range(2)]
            xnt = [self.sb(f"xnt{i}", [128, NH * 512], BF16, c) for i in range(2)]
            qtf = [self.sb(f"qtf{i}", [128, D], F32, c) for i in range(2)]
            qsw = self.sb("qsw", [128, D], F32, c)
            t1b = self.sb("t1b", [128, D], F32, c)
            t2b = self.sb("t2b", [128, D], F32, c)
            qbf1 = self.sb("qbf", [128, D], BF16, c)
            qbf = [qbf1, qbf1]
            cos4 = [self.sb(f"cos4{i}", [128, D], F32, c) for i in range(2)]
            sin4 = [self.sb(f"sin4{i}", [128, D], F32, c) for i in range(2)]
            stats = self.sb("stats", [128, 32], F32, c)
            mv = self.sb("mv", [128, 4], F32, c)
            psq = [self.ps(f"psq{i}", [128, 512], F32, c) for i in range(2)]

            wq_tok = self.load_wmat(self.wq, wq_bf, "d_w0", [])

            tok_norm = {}
            tok_xnw = {}
            tok_xnt_use = {}
            tok_xnt = {}
            tok_qev = {}
            tok_rope_v = {}
            tok_rope_p = {}
            tok_qbf_st = {}
            prev_t2 = {}
            sp = 0

            cos_tok = {}

            def stage_a(m):
                mi = m % 2
                gate = [(sems["c_ve"], tok_rope_v[m - 2]),
                        (sems["c_po"], tok_rope_p[m - 2])] if m >= 2 else []
                cos_tok[m] = self.load_trig(m, cos4[mi], sin4[mi], gate,
                                            sem=("d_cos" if mi == 0 else "d_cos1"))
                xl_toks = {}

                def emit_xload(tt):
                    qt = m * 4 + tt
                    ti = qt % 2
                    w = [(sems["c_ve"], tok_norm[qt - 2])] if qt - 2 in tok_norm else []
                    s.dma("sync", (lambda eng, qt=qt, ti=ti: eng.dma_start(
                        out=x_sb[ti][:], in_=self.x[qt * 128:(qt + 1) * 128, :])),
                        waits=w, inc=sems[f"d_x{ti}"])
                    xl_toks[tt] = s.val(sems[f"d_x{ti}"])

                emit_xload(0)
                emit_xload(1)
                for tt in range(4):
                    if tt >= 2:
                        emit_xload(tt)
                    qt = m * 4 + tt
                    ti = qt % 2
                    wv = [(sems[f"d_x{ti}"], xl_toks[tt])]
                    if qt - 2 in tok_xnw:
                        wv.append((sems[f"d_xnw{ti}"], tok_xnw[qt - 2]))
                    tok_norm[qt] = self.ln_tile(x_sb[ti], xn_bf[ti], stats, mv, wv)
                    w = [(sems["c_ve"], tok_norm[qt])]
                    if m - 2 in tok_xnt and tt == 0:
                        w.append((sems[f"d_xnt{(m - 2) % 2}"], tok_xnt[m - 2]))
                    tok_xnw[qt] = s.dma("sync", (lambda eng, qt=qt, ti=ti: eng.dma_start(
                        out=self.xn_scr[qt * 128:(qt + 1) * 128, :], in_=xn_bf[ti][:])),
                        waits=w, inc=sems[f"d_xnw{ti}"])

                w = [(sems["d_xnw0"], s.val(sems["d_xnw0"])),
                     (sems["d_xnw1"], s.val(sems["d_xnw1"]))]
                if m - 2 in tok_xnt_use:
                    w.append((sems["c_pe"], tok_xnt_use[m - 2]))
                for fc in range(NH):
                    s.dma("scalar", (lambda eng, m=m, mi=mi, fc=fc: eng.dma_start(
                        out=xnt[mi][:, fc * 512:(fc + 1) * 512],
                        in_=self.xn_scr[m * 512:(m + 1) * 512, fc * 128:(fc + 1) * 128],
                        transpose=True)),
                        waits=(w if fc == 0 else []), inc=sems[f"d_xnt{mi}"])
                tok_xnt[m] = s.val(sems[f"d_xnt{mi}"])

            def stage_b(m):
                mi = m % 2
                tok_pe = None
                for g in range(NG):
                    gi = g % 2
                    for u in range(GS):
                        h = g * GS + u
                        pw = [(sems[f"d_xnt{mi}"], tok_xnt[m]), (sems["d_w0"], wq_tok)]
                        if (h - 2) in tok_qev or m > 0:
                            hh = h - 2 if h - 2 >= 0 else NG * GS + (h - 2)
                            if hh in tok_qev:
                                pw.append((sems["c_se"], tok_qev[hh]))
                        if u == 0:
                            key = (m * NG + g) - 2
                            if key >= 0:
                                pw.append((sems["c_ve"], self._q_rope_tok[key]))
                        for fc in range(NH):
                            tok_pe = s.op("tensor", (lambda eng, h=h, fc=fc, mi=mi: eng.matmul(
                                psq[h % 2][:],
                                wq_bf[:, fc * D + h * 128:fc * D + (h + 1) * 128],
                                xnt[mi][:, fc * 512:(fc + 1) * 512],
                                start=(fc == 0), stop=(fc == NH - 1))),
                                waits=(pw if fc == 0 else []),
                                inc=(sems["c_pe"] if fc == NH - 1 else None))
                        tok_qev[h] = s.op("scalar", (lambda eng, h=h, u=u, gi=gi: eng.activation(
                            out=qtf[gi][:, u * 512:(u + 1) * 512], in_=psq[h % 2][:],
                            func=AF.Identity, bias=self.bqs_col[:, h:h + 1], scale=SCALE)),
                            waits=[(sems["c_pe"], tok_pe)], inc=sems["c_se"])
                    tok_rope, tok_t2 = self.rope(
                        m, gi, qtf[gi], qsw, t1b, t2b, qbf[gi],
                        cos4[mi], sin4[mi], cos_tok[m],
                        (sems["c_se"], tok_qev[g * GS + GS - 1]),
                        self._sp, prev_t2.get(0),
                        out_guard=(sems["d_qw"], s.val(sems["d_qw"])),
                        cos_sem=("d_cos" if mi == 0 else "d_cos1"),
                        prev_add=prev_t2.get(1))
                    prev_t2[0] = (sems["c_po"], tok_t2)
                    prev_t2[1] = (sems["c_ve"], tok_rope)
                    self._q_rope_tok[m * NG + g] = tok_rope
                    self._sp += 1
                    tok_rope_v[m] = tok_rope
                    tok_rope_p[m] = tok_t2
                    w = [(sems["c_ve"], tok_rope)]
                    tok_qbf_st[(m, g)] = s.dma("scalar", (lambda eng, m=m, g=g, gi=gi: eng.dma_start(
                        out=self.q_scr[:, g * GS:(g + 1) * GS, m * 512:(m + 1) * 512],
                        in_=qbf[gi][:].rearrange("p (u n) -> p u n", u=GS))),
                        waits=w, inc=sems["d_qw"])
                tok_xnt_use[m] = tok_pe

            self._q_rope_tok = {}
            self._sp = 0
            for m in range(NM):
                if m >= 1:
                    stage_b(m - 1)
                stage_a(m)
            stage_b(NM - 1)

            self.q_all_done = [(sems["d_qw"], s.val(sems["d_qw"]))]
            self.pq_end = [(sems[k], s.val(sems[k])) for k in
                           ("c_pe", "c_se", "c_ve", "c_po", "d_sw0", "d_sw1",
                            "d_xnw0", "d_xnw1")]

    def _pass_kv(self):
        s, sems, cst = self.s, self.sems, self.cst
        s.phase = "passkv"
        T, NM = self.T, self.NM
        s.barrier(self.pq_end + self.q_all_done)
        self.kv_scope = contextlib.ExitStack()
        self.ktc = self.sb("ktc", [128, NG * T], BF16, self.kv_scope)
        self.vc = self.sb("vc", [128, self.NT * 512], BF16, self.kv_scope)
        self.kabs = self.sb("kabs", [128, NG * NM], F32, self.kv_scope)
        with contextlib.ExitStack() as c:
            wk_bf = self.sb("wk_bf", [128, NH * 512], BF16, c)
            wv_bf = self.sb("wv_bf", [128, NH * 512], BF16, c)
            xnt = [self.sb(f"kxnt{i}", [128, NH * 512], BF16, c) for i in range(2)]
            ktf = [self.sb(f"ktf{i}", [128, D], F32, c) for i in range(2)]
            ksw = self.sb("ksw", [128, D], F32, c)
            t1b = self.sb("kt1b", [128, D], F32, c)
            t2b = self.sb("kt2b", [128, D], F32, c)
            kbf = [self.sb(f"kbf{i}", [128, D], BF16, c) for i in range(2)]
            cos4 = self.sb("kcos4", [128, D], F32, c)
            sin4 = self.sb("ksin4", [128, D], F32, c)
            psk = [self.ps(f"psk{i}", [128, 512], F32, c) for i in range(2)]
            psv = [self.ps(f"psv{i}", [128, 512], F32, c) for i in range(2)]

            bv_rep = self.sb("bv_rep", [128, 512], F32, c)
            self.bv_rep = bv_rep
            bvrow = self.sb("bvrow", [1, 512], F32, c)
            wk_tok = self.load_wmat(self.wk, wk_bf, "d_w1", [], nchunk=NH)
            wv_tok = self.load_wmat(self.wv, wv_bf, "d_w1", [], nchunk=NH)
            tok_bl = s.dma("sync", lambda eng: eng.dma_start(
                out=bvrow[:], in_=self.bv), inc=sems["d_cst"])
            tok_bp = s.op("tensor", lambda eng: eng.matmul(
                psk[0][:], cst["ones_row_f"][:], bvrow[:], start=True, stop=True),
                waits=[(sems["d_cst"], tok_bl)], inc=sems["c_pe"])
            tok_bv = s.op("vector", lambda eng: eng.tensor_copy(bv_rep[:], psk[0][:]),
                          waits=[(sems["c_pe"], tok_bp)], inc=sems["c_ve"])
            self._bv_done = (sems["c_ve"], tok_bv)

            tok_xnt_use = {}
            tok_xnt = {}
            tok_kev = {}
            tok_vev = {}
            tok_rope_v = {}
            tok_rope_p = {}
            tok_kc = {}
            prev_t2 = {}
            sp = 0
            for m in range(NM):
                mi = m % 2
                gate = [(sems["c_ve"], tok_rope_v[m - 1]),
                        (sems["c_po"], tok_rope_p[m - 1])] if m >= 1 else []
                cos_tok = self.load_trig(m, cos4, sin4, gate)
                w = [(sems["c_pe"], tok_xnt_use[m - 2])] if m - 2 in tok_xnt_use else []
                for fc in range(NH):
                    s.dma("scalar", (lambda eng, m=m, mi=mi, fc=fc: eng.dma_start(
                        out=xnt[mi][:, fc * 512:(fc + 1) * 512],
                        in_=self.xn_scr[m * 512:(m + 1) * 512, fc * 128:(fc + 1) * 128],
                        transpose=True)),
                        waits=(w if fc == 0 else []), inc=sems[f"d_xnt{mi}"])
                tok_xnt[m] = s.val(sems[f"d_xnt{mi}"])

                for g in range(NG):
                    w0 = [(sems[f"d_xnt{mi}"], tok_xnt[m]), (sems["d_w1"], wv_tok),
                          self._bv_done]
                    if g - 2 in tok_kev or m > 0:
                        key = g - 2 if g - 2 >= 0 else NG + g - 2
                        if key in tok_kev:
                            w0.append((sems["c_se"], tok_kev[key]))
                    if g == 0 and m >= 1:
                        w0.append((sems["c_ve"], tok_rope_v[m - 1]))
                    for fc in range(NH):
                        tok_pe = s.op("tensor", (lambda eng, g=g, fc=fc, mi=mi: eng.matmul(
                            psk[g % 2][:],
                            wk_bf[:, fc * 512 + g * 128:fc * 512 + (g + 1) * 128],
                            xnt[mi][:, fc * 512:(fc + 1) * 512],
                            start=(fc == 0), stop=(fc == NH - 1))),
                            waits=(w0 if fc == 0 else []),
                            inc=(sems["c_pe"] if fc == NH - 1 else None))
                    tok_kev[g] = s.op("scalar", (lambda eng, g=g, mi=mi: eng.activation(
                        out=ktf[mi][:, g * 512:(g + 1) * 512], in_=psk[g % 2][:],
                        func=AF.Identity, bias=self.bk_col[:, g:g + 1], scale=1.0)),
                        waits=[(sems["c_pe"], tok_pe)], inc=sems["c_se"])
                for tt in range(4):
                    qt = m * 4 + tt
                    w0 = []
                    if tt - 2 in tok_vev or m > 0:
                        key = tt - 2 if tt - 2 >= 0 else 4 + tt - 2
                        if key in tok_vev:
                            w0.append((sems["c_ve"], tok_vev[key]))
                    for fc in range(NH):
                        tok_pe = s.op("tensor", (lambda eng, tt=tt, fc=fc, mi=mi: eng.matmul(
                            psv[tt % 2][:],
                            xnt[mi][:, fc * 512 + tt * 128:fc * 512 + (tt + 1) * 128],
                            wv_bf[:, fc * 512:(fc + 1) * 512],
                            start=(fc == 0), stop=(fc == NH - 1))),
                            waits=(w0 if fc == 0 else []),
                            inc=(sems["c_pe"] if fc == NH - 1 else None))
                    tok_vev[tt] = s.op("vector", (lambda eng, qt=qt, tt=tt: eng.tensor_tensor(
                        out=self.vc[:, qt * 512:(qt + 1) * 512], in0=psv[tt % 2][:],
                        in1=self.bv_rep[:], op=ALU.add)),
                        waits=[(sems["c_pe"], tok_pe)], inc=sems["c_ve"])
                tok_xnt_use[m] = tok_pe

                tok_rope, tok_t2 = self.rope(
                    m, mi, ktf[mi], ksw, t1b, t2b, kbf[mi],
                    cos4, sin4, cos_tok,
                    (sems["c_se"], tok_kev[NG - 1]),
                    sp, prev_t2.get(0), prev_add=prev_t2.get(1))
                prev_t2[0] = (sems["c_po"], tok_t2)
                prev_t2[1] = (sems["c_ve"], tok_rope)
                sp += 1
                tok_rope_v[m] = tok_rope
                tok_rope_p[m] = tok_t2
                s.op("vector", lambda eng: eng.drain())
                tok_kc[m] = s.op("vector", (lambda eng, m=m, mi=mi: eng.tensor_copy(
                    self.ktc[:].rearrange("p (g t) -> p g t", g=NG)[:, :, m * 512:(m + 1) * 512],
                    kbf[mi][:].rearrange("p (g n) -> p g n", g=NG))),
                    waits=[(sems["c_ve"], tok_rope)], inc=sems["c_ve"])
                for g in range(NG):
                    s.op("vector", (lambda eng, m=m, g=g, mi=mi: eng.reduce_max(
                        out=self.kabs[:, m * NG + g:m * NG + g + 1],
                        in_=kbf[mi][:, g * 512:(g + 1) * 512], axis=AX.X,
                        apply_absolute_value=True)))
                s.op("vector", lambda eng: eng.drain())
                tok_rope_v[m] = s.op("vector", lambda eng: eng.nop(), inc=sems["c_ve"])

            self.kv_end = [(sems[k], s.val(sems[k])) for k in
                           ("c_pe", "c_se", "c_ve", "c_po", "d_sw0", "d_sw1")]

    def _precast_mat(self, k):
        s, sems = self.s, self.sems
        e, which = divmod(k, 3)
        src_t = [self.w1t, self.w3t, self.w2t][which]
        dst_t = [self.w1_bfs, self.w3_bfs, self.w2_bfs][which]
        nch = 8 if which == 2 else 16
        for part in range(2):
            s.dma("gpsimd", (lambda eng, e=e, src_t=src_t, dst_t=dst_t,
                             nch=nch, part=part: eng.dma_start(
                out=dst_t[e * 128:(e + 1) * 128, :]
                    .rearrange("p (c n) -> p c n", c=nch)
                    [:, part * (nch // 2):(part + 1) * (nch // 2), :],
                in_=src_t[e].rearrange("(c p) n -> p c n", p=128)
                    [:, part * (nch // 2):(part + 1) * (nch // 2), :])),
                inc=sems["d_wc"])

    def _pass_attn(self):
        s, sems, cst = self.s, self.sems, self.cst
        s.phase = "attn"
        T, NT = self.T, self.NT
        s.barrier(self.kv_end)
        with contextlib.ExitStack() as c:
            for nm, msrc in [("m_diag", self.c_m_diag), ("m_win", self.c_m_win),
                             ("m_wsink", self.c_m_wsink), ("m_sink", self.c_m_sink)]:
                t = self.sb("cst_" + nm, [128, 512], BF16, c)
                cst[nm] = t
                s.dma("sync", (lambda eng, t=t, msrc=msrc: eng.dma_start(
                    out=t[:], in_=msrc)), inc=sems["d_cst"])
            self._mask_tok = (sems["d_cst"], s.val(sems["d_cst"]))
            bo_rep = self.sb("bo_rep", [128, D], F32, c)
            self.bo_rep = bo_rep
            wo_bf = self.sb("wo_bf", [128, NH * D], BF16, c)
            negk = self.sb("negk", [1, NG * 128], BF16, c)
            kinf_row = self.sb("kinf_row", [1, NG * self.NM], F32, c)
            kinf_sc = self.sb("kinf_sc", [1, NG], F32, c)
            qt_sb = [self.sb(f"qt_sb{i}", [128, D], BF16, c) for i in range(2)]
            ax_sb = self.sb("ax_sb", [128, D], F32, c)
            x_sb = [ax_sb, ax_sb]

            l1row = self.sb("l1row", [1, D], BF16, c)
            pt_sb = [self.sb(f"pt_sb{i}", [128, 6 * 512], BF16, c) for i in range(2)]
            ot_sb = self.sb("ot_sb", [128, D], BF16, c)
            rs_sb = self.sb("rs_sb", [1, 512], F32, c)
            rinv = self.sb("rinv", [128, 512], F32, c)
            tout = [self.sb(f"tout{i}", [128, 512], F32, c) for i in range(2)]
            x1_sb = self.sb("x1_sb", [128, D], F32, c)
            xn2_bf = self.sb("xn2_bf", [128, D], BF16, c)
            absq = self.sb("absq", [128, D], BF16, c)
            stats = self.sb("astats", [128, 32], F32, c)
            mv = self.sb("amv", [128, 4], F32, c)
            psS = [self.ps(f"psS{i}", [128, 512], F32, c) for i in range(3)]
            psO = [self.ps(f"psO{i}", [128, 512], F32, c) for i in range(2)]
            psR = self.ps("psR", [128, 512], F32, c)
            psU = [self.ps(f"psU{i}", [128, 512], F32, c) for i in range(2)]

            wo_tok = self.load_wmat(self.wo, wo_bf, "d_w2", [])

            tok_bo = None
            for nq in range(4):
                tok_l = s.dma("sync", (lambda eng, nq=nq: eng.dma_start(
                    out=rs_sb[:], in_=self.bo[0:1, nq * 512:(nq + 1) * 512])),
                    waits=[tok_bo], inc=sems["d_cst"])
                tok_pe = s.op("tensor", (lambda eng, nq=nq: eng.matmul(
                    psU[0][:], cst["ones_row_f"][:], rs_sb[:],
                    start=True, stop=True)),
                    waits=[(sems["d_cst"], tok_l)], inc=sems["c_pe"])
                tok_v = s.op("vector", (lambda eng, nq=nq: eng.tensor_copy(
                    bo_rep[:, nq * 512:(nq + 1) * 512], psU[0][:])),
                    waits=[(sems["c_pe"], tok_pe)], inc=sems["c_ve"])
                tok_bo = (sems["c_ve"], tok_v)
            self._bo_done = tok_bo
            tok_u_free_init = tok_bo

            tok_kg = s.op("gpsimd", lambda eng: eng.tensor_reduce(
                out=kinf_row[:], in_=self.kabs[:], axis=AX.C, op=ALU.max),
                inc=sems["c_po"])
            for g in range(NG):
                s.op("vector", (lambda eng, g=g: eng.reduce_max(
                    out=kinf_sc[0:1, g:g + 1],
                    in_=kinf_row[0:1, :].rearrange("o (m g) -> o g m", g=NG)[0:1, g, :],
                    axis=AX.X)),
                    waits=([(sems["c_po"], tok_kg)] if g == 0 else []))
            s.op("vector", lambda eng: eng.drain())
            for g in range(NG):
                s.op("vector", (lambda eng, g=g: eng.tensor_scalar(
                    out=negk[0:1, g * 128:(g + 1) * 128], in0=cst["ones_row_f"][:],
                    scalar1=kinf_sc[0:1, g:g + 1], scalar2=-1.0,
                    op0=ALU.mult, op1=ALU.mult)))
            s.op("vector", lambda eng: eng.drain())
            tok_negk = s.op("vector", lambda eng: eng.nop(), inc=sems["c_ve"])

            bi = 0
            gg = 0
            tok_exp = {}
            tok_avdone = {}
            tok_otev = {}
            tok_rs_chain = None
            tok_u_free = {0: tok_u_free_init, 1: None}
            tok_l1 = {}
            tok_scores_end = {}
            tok_l1mm_end = {}
            tok_x_use = {}
            tok_x1w = {}
            tok_xn2w = {}
            tok_norm_prev = None

            NMAT = 3 * (NE + 1)
            for qt in range(NT):
                if qt < NMAT:
                    self._precast_mat(qt)
                ti = qt % 2
                blocks = blocks_for(qt)
                nb = len(blocks)
                w = [(sems["c_pe"], tok_scores_end[qt - 2])] if qt - 2 in tok_scores_end else []
                if qt == 0:
                    w += self.q_all_done
                s.dma("sync", (lambda eng, qt=qt, ti=ti: eng.dma_start(
                    out=qt_sb[ti][:].rearrange("p (h n) -> p h n", h=NH),
                    in_=self.q_scr[:, :, qt * 128:(qt + 1) * 128])),
                    waits=w, inc=sems[f"d_qt{ti}"])
                qt_tok = s.val(sems[f"d_qt{ti}"])
                w = [(sems["c_ve"], tok_x_use[qt - 1])] if qt - 1 in tok_x_use else []
                s.dma("sync", (lambda eng, qt=qt, ti=ti: eng.dma_start(
                    out=x_sb[ti][:], in_=self.x[qt * 128:(qt + 1) * 128, :])),
                    waits=w, inc=sems[f"d_x{ti}"])
                xl_tok = s.val(sems[f"d_x{ti}"])
                w = [(sems[f"d_qt{ti}"], qt_tok)]
                if qt - 1 in tok_l1mm_end:
                    w.append((sems["c_pe"], tok_l1mm_end[qt - 1]))
                tok_absq = s.op("vector", (lambda eng, ti=ti: eng.scalar_tensor_tensor(
                    out=absq[:], in0=qt_sb[ti][:], scalar=-1.0, in1=qt_sb[ti][:],
                    op0=ALU.mult, op1=ALU.max)),
                    waits=w, inc=sems["c_ve"])
                for g in range(NG):
                    ub = g % 2
                    w = [(sems["c_ve"], tok_absq), tok_u_free[ub]]
                    tok_pe = s.op("tensor", (lambda eng, g=g, ub=ub: eng.matmul(
                        psU[ub][0:1, :], cst["ones128_bf"][:],
                        absq[:, g * 512:(g + 1) * 512], start=True, stop=True)),
                        waits=w, inc=sems["c_pe"])
                    ev_w = [(sems["c_pe"], tok_pe)]
                    if g == 0 and qt - 1 in tok_scores_end:
                        ev_w.append((sems["c_pe"], tok_scores_end[qt - 1]))
                    tok_l1[g] = s.op("scalar", (lambda eng, g=g, ub=ub: eng.activation(
                        out=l1row[0:1, g * 512:(g + 1) * 512], in_=psU[ub][0:1, :],
                        func=AF.Copy)),
                        waits=ev_w, inc=sems["c_se"])
                    tok_u_free[ub] = (sems["c_se"], tok_l1[g])
                tok_l1mm_end[qt] = tok_pe

                for g in range(NG):
                    pi = gg % 2
                    my_bis = []
                    for j, (kt, mask) in enumerate(blocks):
                        bank = bi % 3
                        w = [(sems[f"d_qt{ti}"], qt_tok), (sems["c_se"], tok_l1[g]),
                             (sems["c_ve"], tok_negk), self._mask_tok]
                        if (bi - 3) in tok_exp:
                            w.append((sems["c_se"], tok_exp[bi - 3]))
                        s.op("tensor", (lambda eng, g=g, kt=kt, bank=bank, ti=ti: eng.matmul(
                            psS[bank][:],
                            self.ktc[:, g * T + kt * 128:g * T + (kt + 1) * 128],
                            qt_sb[ti][:, g * 512:(g + 1) * 512],
                            start=True, stop=False)), waits=w)
                        if mask is not None:
                            s.op("tensor", (lambda eng, mask=mask, bank=bank: eng.matmul(
                                psS[bank][:], cst["idn_bf"][:], cst[mask][:],
                                start=False, stop=False)))
                        tok_pe = s.op("tensor", (lambda eng, g=g, bank=bank: eng.matmul(
                            psS[bank][:], negk[0:1, g * 128:(g + 1) * 128],
                            l1row[0:1, g * 512:(g + 1) * 512],
                            start=False, stop=True)), inc=sems["c_pe"])
                        w = [(sems["c_pe"], tok_pe)]
                        if gg - 2 in tok_avdone:
                            w.append((sems["c_pe"], tok_avdone[gg - 2]))
                        tok_exp[bi] = s.op("scalar", (lambda eng, j=j, bank=bank, pi=pi: eng.activation(
                            out=pt_sb[pi][:, j * 512:(j + 1) * 512], in_=psS[bank][:],
                            func=AF.Exp)), waits=w, inc=sems["c_se"])
                        my_bis.append(bi)
                        bi += 1
                    tok_scores_end[qt] = tok_pe
                    w_rs0 = [(sems["c_ve"], tok_rs_chain)] if tok_rs_chain else []
                    for j, (kt, mask) in enumerate(blocks):
                        w = [(sems["c_se"], tok_exp[my_bis[j]])]
                        if j == 0 and gg - 2 in tok_otev:
                            w.append((sems["c_ve"], tok_otev[gg - 2]))
                        tok_av = s.op("tensor", (lambda eng, g=g, kt=kt, j=j, pi=pi, gg=gg, nb=nb: eng.matmul(
                            psO[gg % 2][:],
                            self.vc[:, kt * 512 + g * 128:kt * 512 + (g + 1) * 128],
                            pt_sb[pi][:, j * 512:(j + 1) * 512],
                            start=(j == 0), stop=(j == nb - 1))),
                            waits=w, inc=(sems["c_pe"] if j == nb - 1 else None))
                    for j, (kt, mask) in enumerate(blocks):
                        tok_rs = s.op("tensor", (lambda eng, j=j, pi=pi, nb=nb: eng.matmul(
                            psR[0:1, :], cst["ones128_bf"][:],
                            pt_sb[pi][:, j * 512:(j + 1) * 512],
                            start=(j == 0), stop=(j == nb - 1))),
                            waits=(w_rs0 if j == 0 else []),
                            inc=(sems["c_pe"] if j == nb - 1 else None))
                    tok_avdone[gg] = tok_rs
                    tok_rsev = s.op("scalar", lambda eng: eng.activation(
                        out=rs_sb[:], in_=psR[0:1, :], func=AF.Copy),
                        waits=[(sems["c_pe"], tok_rs)], inc=sems["c_se"])
                    tok_repl = s.op("tensor", lambda eng: eng.matmul(
                        psR[:], cst["ones_row_f"][:], rs_sb[:], start=True, stop=True),
                        waits=[(sems["c_se"], tok_rsev)], inc=sems["c_pe"])
                    s.op("vector", lambda eng: eng.drain())
                    tok_rinv = s.op("vector", lambda eng: eng.reciprocal(rinv[:], psR[:]),
                                    waits=[(sems["c_pe"], tok_repl)], inc=sems["c_ve"])
                    tok_rs_chain = tok_rinv
                    s.op("vector", lambda eng: eng.drain())
                    tok_otev[gg] = s.op("vector", (lambda eng, g=g, gg=gg: eng.tensor_tensor(
                        out=ot_sb[:, g * 512:(g + 1) * 512], in0=psO[gg % 2][:],
                        in1=rinv[:], op=ALU.mult)),
                        waits=[(sems["c_pe"], tok_avdone[gg]), (sems["c_ve"], tok_rinv)],
                        inc=sems["c_ve"])
                    gg += 1

                w_ot = [(sems["c_ve"], tok_otev[gg - 1 - k]) for k in range(NG)]
                tok_x1_done = None
                tok_pool_q = getattr(self, "_tok_pool_q", {})
                for nq in range(4):
                    ub = nq % 2
                    w = list(w_ot) + [(sems["d_w2"], wo_tok), tok_u_free[ub]]
                    for h in range(NH):
                        tok_pe = s.op("tensor", (lambda eng, h=h, nq=nq, ub=ub: eng.matmul(
                            psU[ub][:], ot_sb[:, h * 128:(h + 1) * 128],
                            wo_bf[:, h * D + nq * 512:h * D + (nq + 1) * 512],
                            start=(h == 0), stop=(h == NH - 1))),
                            waits=(w if h == 0 else []),
                            inc=(sems["c_pe"] if h == NH - 1 else None))
                    tok_t = s.op("vector", (lambda eng, nq=nq, ub=ub, ti=ti: eng.tensor_tensor(
                        out=tout[ub][:], in0=psU[ub][:],
                        in1=x_sb[ti][:, nq * 512:(nq + 1) * 512], op=ALU.add)),
                        waits=[(sems["c_pe"], tok_pe), (sems[f"d_x{ti}"], xl_tok),
                               tok_pool_q.get(ub)],
                        inc=sems["c_ve"])
                    tok_u_free[ub] = (sems["c_ve"], tok_t)
                    w = [(sems["c_ve"], tok_t)]
                    if nq == 0 and qt == 0:
                        w.append(self._bo_done)
                    if nq == 0 and qt - 1 in tok_x1w:
                        w.append((sems["d_x1w"], tok_x1w[qt - 1]))
                        w.append((sems["d_xn2w"], tok_xn2w[qt - 1]))
                        w.append((sems["c_ve"], tok_norm_prev))
                    tok_x1_done = s.op("gpsimd", (lambda eng, nq=nq, ub=ub: eng.tensor_tensor(
                        out=x1_sb[:, nq * 512:(nq + 1) * 512], in0=tout[ub][:],
                        in1=self.bo_rep[:, nq * 512:(nq + 1) * 512], op=ALU.add)),
                        waits=w, inc=sems["c_po"])
                    tok_pool_q[ub] = (sems["c_po"], tok_x1_done)
                self._tok_pool_q = tok_pool_q
                tok_x_use[qt] = tok_t
                tok_x1w[qt] = s.dma("scalar", (lambda eng, qt=qt: eng.dma_start(
                    out=self.x1_scr[qt * 128:(qt + 1) * 128, :], in_=x1_sb[:])),
                    waits=[(sems["c_po"], tok_x1_done)], inc=sems["d_x1w"])
                tok_norm_prev = self.ln_tile(
                    x1_sb, xn2_bf, stats, mv,
                    [(sems["c_po"], tok_x1_done),
                     (sems["c_pe"], tok_l1mm_end[qt])] +
                    ([(sems["d_xn2w"], tok_xn2w[qt - 1])] if qt - 1 in tok_xn2w else []))
                tok_xn2w[qt] = s.dma("scalar", (lambda eng, qt=qt: eng.dma_start(
                    out=self.xn2_scr[qt * 128:(qt + 1) * 128, :], in_=xn2_bf[:])),
                    waits=[(sems["c_ve"], tok_norm_prev)], inc=sems["d_xn2w"])

            for k in range(min(NT, 3 * (NE + 1)), 3 * (NE + 1)):
                self._precast_mat(k)
            self._wc_done = (sems["d_wc"], s.val(sems["d_wc"]))
            self.attn_end = [(sems[k], s.val(sems[k])) for k in
                             ("c_pe", "c_se", "c_ve", "c_po", "d_x1w", "d_xn2w",
                              "d_qt0", "d_qt1", "d_x0", "d_x1")]

    def _pass_moe(self):
        s, sems, cst = self.s, self.sems, self.cst
        s.phase = "moe"
        T, NM = self.T, self.NM
        s.barrier(self.attn_end)
        self.kv_scope.close()
        with contextlib.ExitStack() as c:
            wbuf = [self.sb(f"wm{i}", [128, 16 * HID], BF16, c) for i in range(3)]
            xnt = self.sb("mxnt", [128, NH * 512], BF16, c)
            xg = [self.sb(f"xg{i}", [128, NH * 512], BF16, c) for i in range(2)]
            hT = self.sb("hT", [128, 8 * 512], BF16, c)
            ssb = self.sb("ssb", [128, 512], F32, c)
            grep = [self.sb(f"grep{i}", [128, 512], BF16, c) for i in range(2)]
            gTs = [self.sb(f"gTs{i}", [8, 128], BF16, c) for i in range(2)]
            g_row = [self.sb(f"g_row{i}", [1, 512], BF16, c) for i in range(2)]
            lgt = self.sb("lgt", [128, 16], F32, c)
            gtmp = self.sb("gtmp", [128, 8], F32, c)
            gbf = self.sb("gbf", [128, 8], BF16, c)
            osb = [self.sb(f"osb{i}", [128, D], F32, c) for i in range(2)]
            x1t = xg[0][:].bitcast(F32)
            psZ = [self.ps(f"psZ{i}", [128, 512], F32, c) for i in range(4)]
            psW = [self.ps(f"psW{i}", [128, 512], F32, c) for i in range(2)]
            psG = self.ps("psG", [128, 512], F32, c)
            psT = self.ps("psT", [8, 1024], BF16, c)

            bsrcs = [self.w1_bfs, self.w3_bfs, self.w2_bfs]
            buf_free = {}
            tok_w = {}

            def load_bf(e, which, waits):
                toks = None
                for part in range(2):
                    toks = s.dma("sync", (lambda eng, e=e, which=which, part=part:
                        eng.dma_start(
                            out=wbuf[which][:, part * 8192:(part + 1) * 8192],
                            in_=bsrcs[which][e * 128:(e + 1) * 128,
                                             part * 8192:(part + 1) * 8192])),
                        waits=(waits if part == 0 else []), inc=sems[f"d_wb{which}"])
                return toks

            for which in range(3):
                tok_w[(NE, which)] = load_bf(NE, which, [self._wc_done])

            tok_gtrans_ev = None
            tok_xg_done = None
            tok_osb_free = [None, None]
            tok_hT_done = None
            tok_grep_done = None
            tok_grep_prev = None
            tok_grep_free = {}
            self._g_ct = 0
            tok_x1_use = {}
            x1_tok = {}
            tok_lgcopy = None
            gsw_toks = {}
            tok_grow_use = {}
            prev_psW = {0: None, 1: None}
            silu_toks = {}
            h_toks = {}
            tok_xg_m = {}
            tok_xg_read = {}

            for ei, e in enumerate([NE] + list(range(NE))):
                shared = (e == NE)
                b1, b3, b2 = 0, 1, 2
                w1_tok, w3_tok, w2_tok = tok_w[(e, 0)], tok_w[(e, 1)], tok_w[(e, 2)]
                for m in range(NM + 1):
                    if m < NM and not shared:
                        par = (ei * NM + m) % 2
                        wld = [(sems["d_gsw"], s.val(sems["d_gsw"]))]
                        if par in tok_grow_use:
                            wld.append(tok_grow_use[par])
                        s.dma("scalar", (lambda eng, e=e, m=m, par=par: eng.dma_start(
                            out=g_row[par][:], in_=self.g_scr[e:e + 1, m * 512:(m + 1) * 512])),
                            waits=wld, inc=sems[f"d_gsr{par}"])
                        grow_tok = s.val(sems[f"d_gsr{par}"])
                    if m < NM:
                        w = [tok_xg_done]
                        for fc in range(NH):
                            s.dma("scalar", (lambda eng, m=m, fc=fc: eng.dma_start(
                                out=xnt[:, fc * 512:(fc + 1) * 512],
                                in_=self.xn2_scr[m * 512:(m + 1) * 512, fc * 128:(fc + 1) * 128],
                                transpose=True)),
                                waits=(w if fc == 0 else []), inc=sems["d_xnt0"])
                        xnt_tok = s.val(sems["d_xnt0"])
                    if shared and m == NM:
                        break

                    if shared:
                        for tt in range(4):
                            w = [(sems["d_xnt0"], xnt_tok)]
                            if tok_lgcopy is not None:
                                w.append((sems["c_ve"], tok_lgcopy))
                            for fc in range(NH):
                                tok_pe = s.op("tensor", (lambda eng, tt=tt, fc=fc: eng.matmul(
                                    psG[:, 0:8],
                                    xnt[:, fc * 512 + tt * 128:fc * 512 + (tt + 1) * 128],
                                    self.gatewT[:, fc * 8:(fc + 1) * 8],
                                    start=(fc == 0), stop=(fc == NH - 1))),
                                    waits=(w if fc == 0 else []),
                                    inc=(sems["c_pe"] if fc == NH - 1 else None))
                            tok_lg = s.op("vector", lambda eng: eng.tensor_copy(
                                lgt[:, 0:8], psG[:, 0:8]),
                                waits=[(sems["c_pe"], tok_pe)], inc=sems["c_ve"])
                            tok_lgcopy = tok_lg
                            s.op("vector", lambda eng: eng.drain())
                            s.op("vector", lambda eng: eng.reduce_max(
                                lgt[:, 8:9], lgt[:, 0:8], axis=AX.X))
                            s.op("vector", lambda eng: eng.drain())
                            s.op("vector", lambda eng: eng.tensor_scalar(
                                out=gtmp[:], in0=lgt[:, 0:8], scalar1=lgt[:, 8:9],
                                scalar2=None, op0=ALU.is_ge))
                            s.op("vector", lambda eng: eng.drain())
                            s.op("vector", lambda eng: eng.scalar_tensor_tensor(
                                out=gtmp[:], in0=gtmp[:], scalar=-1e30, in1=lgt[:, 0:8],
                                op0=ALU.mult, op1=ALU.add))
                            s.op("vector", lambda eng: eng.drain())
                            s.op("vector", lambda eng: eng.reduce_max(
                                lgt[:, 9:10], gtmp[:], axis=AX.X))
                            s.op("vector", lambda eng: eng.drain())
                            tok_keep = s.op("vector", lambda eng: eng.tensor_scalar(
                                out=gtmp[:], in0=lgt[:, 0:8], scalar1=lgt[:, 9:10],
                                scalar2=None, op0=ALU.is_ge), inc=sems["c_ve"])
                            s.op("vector", lambda eng: eng.drain())
                            tok_sig = s.op("scalar", lambda eng: eng.activation(
                                out=lgt[:, 0:8], in_=lgt[:, 0:8], func=AF.Sigmoid),
                                waits=[(sems["c_ve"], tok_keep)], inc=sems["c_se"])
                            tok_g = s.op("vector", lambda eng: eng.tensor_tensor(
                                out=gbf[:], in0=gtmp[:], in1=lgt[:, 0:8], op=ALU.mult),
                                waits=[(sems["c_se"], tok_sig)], inc=sems["c_ve"])
                            w = [(sems["c_ve"], tok_g)]
                            if tok_gtrans_ev is not None:
                                w.append(tok_gtrans_ev)
                            tok_tp = s.op("tensor", (lambda eng, tt=tt: eng.transpose(
                                psT[:, tt * 128:(tt + 1) * 128], gbf[:], cst["idn_bf"][:])),
                                waits=w, inc=sems["c_pe"])
                            qt = m * 4 + tt
                            w = [(sems["c_pe"], tok_tp),
                                 (sems["d_gsw"], s.val(sems["d_gsw"]))]
                            tok_gtev = s.op("vector", (lambda eng, tt=tt: eng.tensor_copy(
                                gTs[tt % 2][:],
                                psT[0:8, tt * 128:(tt + 1) * 128])),
                                waits=w, inc=sems["c_ve"])
                            tok_gtrans_ev = (sems["c_ve"], tok_gtev)
                            gsw_toks[qt] = s.dma("sync", (lambda eng, qt=qt, tt=tt: eng.dma_start(
                                out=self.g_scr[:, qt * 128:(qt + 1) * 128],
                                in_=gTs[tt % 2][:])),
                                waits=[(sems["c_ve"], tok_gtev)], inc=sems["d_gsw"])
                        zsrc = xnt
                        tok_zsrc = (sems["d_xnt0"], xnt_tok)
                    else:
                        if m < NM:
                            mi = m % 2
                            gp = self._g_ct % 2
                            self._g_ct += 1
                            w = [tok_grep_prev, (sems[f"d_gsr{par}"], grow_tok)]
                            tok_pe = s.op("tensor", (lambda eng, par=par: eng.matmul(
                                psG[:], cst["ones_col_bf"][:],
                                g_row[par][:], start=True, stop=True)),
                                waits=w, inc=sems["c_pe"])
                            tok_grow_use[par] = (sems["c_pe"], tok_pe)
                            tok_grep = s.op("scalar", (lambda eng, gp=gp: eng.activation(
                                out=grep[gp][:], in_=psG[:], func=AF.Copy)),
                                waits=[(sems["c_pe"], tok_pe),
                                       tok_grep_free.get(gp)], inc=sems["c_se"])
                            tok_grep_prev = (sems["c_se"], tok_grep)
                            for fc in range(NH):
                                xgw = []
                                if fc == 0:
                                    xgw = [(sems["c_se"], tok_grep),
                                           (sems["d_xnt0"], xnt_tok),
                                           tok_xg_read.get(mi)]
                                    if ei == 1 and m == 0 and (NM * 4 - 1) in tok_x1_use:
                                        xgw.append(tok_x1_use[NM * 4 - 1])
                                tok_xg = s.op("gpsimd", (lambda eng, fc=fc, mi=mi, gp=gp: eng.tensor_tensor(
                                    out=xg[mi][:, fc * 512:(fc + 1) * 512],
                                    in0=xnt[:, fc * 512:(fc + 1) * 512], in1=grep[gp][:],
                                    op=ALU.mult)),
                                    waits=xgw,
                                    inc=(sems["c_po"] if fc == NH - 1 else None))
                            tok_xg_m[m] = (sems["c_po"], tok_xg)
                            tok_xg_done = (sems["c_po"], tok_xg)
                            tok_grep_free[gp] = (sems["c_po"], tok_xg)
                        if m == 0:
                            continue
                        m = m - 1
                        zsrc = xg[m % 2]
                        tok_zsrc = tok_xg_m[m]

                    last_z = None
                    for hc in range(8):
                        zb1 = hc % 2
                        zb3 = 2 + hc % 2
                        w1w = [tok_zsrc, (sems[f"d_wb{b1}"], w1_tok)]
                        w1w.append(silu_toks.get(hc - 2 if hc >= 2 else 6 + hc))
                        for fc in range(NH):
                            tok_z1 = s.op("tensor", (lambda eng, hc=hc, fc=fc, zb1=zb1, b1=b1, zsrc=zsrc: eng.matmul(
                                psZ[zb1][:],
                                wbuf[b1][:, fc * HID + hc * 128:fc * HID + (hc + 1) * 128],
                                zsrc[:, fc * 512:(fc + 1) * 512],
                                start=(fc == 0), stop=(fc == NH - 1))),
                                waits=(w1w if fc == 0 else []),
                                inc=(sems["c_pe"] if fc == NH - 1 else None))
                        buf_free[b1] = (sems["c_pe"], tok_z1)
                        w3w = [(sems[f"d_wb{b3}"], w3_tok)]
                        w3w.append(h_toks.get(hc - 2 if hc >= 2 else 6 + hc))
                        for fc in range(NH):
                            tok_z3 = s.op("tensor", (lambda eng, hc=hc, fc=fc, zb3=zb3, b3=b3, zsrc=zsrc: eng.matmul(
                                psZ[zb3][:],
                                wbuf[b3][:, fc * HID + hc * 128:fc * HID + (hc + 1) * 128],
                                zsrc[:, fc * 512:(fc + 1) * 512],
                                start=(fc == 0), stop=(fc == NH - 1))),
                                waits=(w3w if fc == 0 else []),
                                inc=(sems["c_pe"] if fc == NH - 1 else None))
                        last_z = tok_z3
                        buf_free[b3] = (sems["c_pe"], tok_z3)
                        tok_silu = s.op("scalar", (lambda eng, zb1=zb1: eng.activation(
                            out=ssb[:], in_=psZ[zb1][:], func=AF.Silu)),
                            waits=[(sems["c_pe"], tok_z1),
                                   (h_toks[hc - 1] if hc >= 1 else None)],
                            inc=sems["c_se"])
                        hw = [(sems["c_se"], tok_silu), (sems["c_pe"], tok_z3)]
                        if hc == 0:
                            hw.append(tok_hT_done)
                        tok_h = s.op("vector", (lambda eng, hc=hc, zb3=zb3: eng.tensor_tensor(
                            out=hT[:, hc * 512:(hc + 1) * 512], in0=ssb[:],
                            in1=psZ[zb3][:], op=ALU.mult)),
                            waits=hw, inc=sems["c_ve"])
                        silu_toks[hc] = (sems["c_se"], tok_silu)
                        h_toks[hc] = (sems["c_ve"], tok_h)
                    tok_hT = tok_h
                    if shared:
                        tok_xg_done = (sems["c_pe"], last_z)
                    else:
                        tok_xg_read[m % 2] = (sems["c_pe"], last_z)

                    for tt in range(4):
                        qt = m * 4 + tt
                        if shared:
                            w = [tok_x1_use[qt - 1]] if qt - 1 in tok_x1_use else []
                            s.dma("sync", (lambda eng, qt=qt: eng.dma_start(
                                out=x1t[:, 0:D], in_=self.x1_scr[qt * 128:(qt + 1) * 128, :])),
                                waits=w, inc=sems["d_x1r"])
                            x1_tok[qt] = s.val(sems["d_x1r"])
                        oi = tt % 2
                        tok_ev_last = None
                        ev_toks = {}
                        for nq in range(4):
                            wb = nq % 2
                            w = [(sems["c_ve"], tok_hT), (sems[f"d_wb{b2}"], w2_tok),
                                 tok_osb_free[oi] if nq == 0 else None,
                                 prev_psW[wb]]
                            for hc in range(8):
                                tok_pe = s.op("tensor", (lambda eng, tt=tt, nq=nq, hc=hc, wb=wb, b2=b2: eng.matmul(
                                    psW[wb][:],
                                    hT[:, hc * 512 + tt * 128:hc * 512 + (tt + 1) * 128],
                                    wbuf[b2][:, hc * D + nq * 512:hc * D + (nq + 1) * 512],
                                    start=(hc == 0), stop=(hc == 7))),
                                    waits=(w if hc == 0 else []),
                                    inc=(sems["c_pe"] if hc == 7 else None))
                            if shared:
                                tok_ev = s.op("vector", (lambda eng, nq=nq, oi=oi, wb=wb: eng.tensor_tensor(
                                    out=osb[oi][:, nq * 512:(nq + 1) * 512], in0=psW[wb][:],
                                    in1=x1t[:, nq * 512:(nq + 1) * 512], op=ALU.add)),
                                    waits=[(sems["c_pe"], tok_pe),
                                           (sems["d_x1r"], x1_tok[qt])],
                                    inc=sems["c_ve"])
                                sem_ev = sems["c_ve"]
                            else:
                                if nq % 2 == 0:
                                    tok_ev = s.op("vector", (lambda eng, nq=nq, oi=oi, wb=wb: eng.tensor_copy(
                                        osb[oi][:, nq * 512:(nq + 1) * 512], psW[wb][:])),
                                        waits=[(sems["c_pe"], tok_pe)], inc=sems["c_ve"])
                                    sem_ev = sems["c_ve"]
                                else:
                                    tok_ev = s.op("scalar", (lambda eng, nq=nq, oi=oi, wb=wb: eng.activation(
                                        out=osb[oi][:, nq * 512:(nq + 1) * 512],
                                        in_=psW[wb][:], func=AF.Copy)),
                                        waits=[(sems["c_pe"], tok_pe)], inc=sems["c_se"])
                                    sem_ev = sems["c_se"]
                            ev_toks[nq] = (sem_ev, tok_ev)
                            prev_psW[wb] = (sem_ev, tok_ev)
                            tok_ev_last = (sem_ev, tok_ev)
                            buf_free[b2] = (sems["c_pe"], tok_pe)
                        if shared:
                            tok_x1_use[qt] = tok_ev_last
                            tok_store = s.dma("sync", (lambda eng, qt=qt, oi=oi: eng.dma_start(
                                out=self.out[qt * 128:(qt + 1) * 128, :], in_=osb[oi][:])),
                                waits=[ev_toks[2], ev_toks[3],
                                       (sems["d_outS"], s.val(sems["d_outS"]))],
                                inc=sems["d_outS"])
                            tok_osb_free[oi] = (sems["d_outS"], s.val(sems["d_outS"]))
                        else:
                            gsem = "d_outG" if oi == 0 else "d_outG1"
                            w = [ev_toks[0], ev_toks[1], ev_toks[2], ev_toks[3],
                                 (sems["d_outS"], s.val(sems["d_outS"])),
                                 (sems[gsem], s.val(sems[gsem]))]
                            tok_store = s.dma("gpsimd", (lambda eng, qt=qt, oi=oi: eng.dma_start(
                                out=self.out[qt * 128:(qt + 1) * 128, :], in_=osb[oi][:],
                                accum_op=ALU.add)),
                                waits=w, inc=sems[gsem])
                            tok_osb_free[oi] = (sems[gsem], s.val(sems[gsem]))
                    tok_hT_done = (sems["c_pe"], tok_pe)

                if ei < NE:
                    ne = 0 if shared else e + 1
                    for nw in range(3):
                        tok_w[(ne, nw)] = load_bf(ne, nw, [buf_free.get(nw)])

            s.op("sync", lambda eng: eng.nop(),
                 waits=[(sems["d_outS"], s.val(sems["d_outS"])),
                        (sems["d_outG"], s.val(sems["d_outG"])),
                        (sems["d_outG1"], s.val(sems["d_outG1"]))])


def _build(T):
    return Builder(T).build()


def kernel(L0_attn_x, L0_attn_Wq, L0_attn_bq, L0_attn_Wk, L0_attn_bk,
           L0_attn_Wv, L0_attn_bv, L0_attn_Wo, L0_attn_bo,
           L0_ffn_gate_w, L0_ffn_w1, L0_ffn_w2, L0_ffn_w3,
           L0_ffn_sw1, L0_ffn_sw2, L0_ffn_sw3):
    T = np.asarray(L0_attn_x).shape[1]
    n_cores = 8
    consts = host_consts(T)
    f32 = np.float32

    def c(a):
        return np.ascontiguousarray(np.asarray(a))

    w1t = c(np.concatenate([np.asarray(L0_ffn_w1).transpose(0, 2, 1),
                            np.asarray(L0_ffn_sw1).T[None]], axis=0).astype(f32))
    w3t = c(np.concatenate([np.asarray(L0_ffn_w3).transpose(0, 2, 1),
                            np.asarray(L0_ffn_sw3).T[None]], axis=0).astype(f32))
    w2t = c(np.concatenate([np.asarray(L0_ffn_w2).transpose(0, 2, 1),
                            np.asarray(L0_ffn_sw2).T[None]], axis=0).astype(f32))

    base = {
        "wq": c(L0_attn_Wq), "bq": c(np.asarray(L0_attn_bq)[None, :]),
        "wk": c(L0_attn_Wk), "bk": c(np.asarray(L0_attn_bk)[None, :]),
        "wv": c(L0_attn_Wv), "bv": c(np.asarray(L0_attn_bv)[None, :]),
        "wo": c(L0_attn_Wo), "bo": c(np.asarray(L0_attn_bo)[None, :]),
        "gw": c(L0_ffn_gate_w), "w1t": w1t, "w2t": w2t, "w3t": w3t,
    }
    base.update(consts)

    nc = _build(T)
    in_maps = []
    xfull = np.asarray(L0_attn_x)
    for core in range(n_cores):
        m = dict(base)
        m["x"] = c(xfull[core])
        in_maps.append(m)
    res = run_bass_kernel_spmd(nc, in_maps, core_ids=list(range(n_cores)))
    return np.stack([r["out"] for r in res.results], axis=0)


# version 20
# speedup vs baseline: 1.0172x; 1.0129x over previous
import contextlib
import numpy as np
import ml_dtypes
import concourse.bass as bass
import concourse.mybir as mybir
from concourse.bass_utils import run_bass_kernel_spmd

F32 = mybir.dt.float32
BF16 = mybir.dt.bfloat16
AF = mybir.ActivationFunctionType
ALU = mybir.AluOpType
AX = mybir.AxisListType
BF = ml_dtypes.bfloat16

D = 2048
NH = 16
NG = 4
GS = 4
DH = 128
HID = 1024
NE = 8
SINK = 4
SCALE = DH ** -0.5
EPS = 1e-5
NEG = -1e30
ENGINES = ("sync", "scalar", "vector", "tensor", "gpsimd")


class Sched:
    def __init__(self):
        self.items = []
        self.counts = {}
        self.pending = {e: [] for e in ENGINES}
        self.phase = "setup"

    def _bump(self, sem, amt):
        k = id(sem)
        self.counts[k] = self.counts.get(k, 0) + amt
        return self.counts[k]

    def val(self, sem):
        return self.counts.get(id(sem), 0)

    def op(self, eng, fn, waits=(), inc=None, amt=1):
        tok = None
        if inc is not None:
            tok = self._bump(inc, amt)
        w = self.pending[eng] + [x for x in waits if x is not None and x[0] is not None]
        self.pending[eng] = []
        self.items.append((eng, fn, w, inc, amt, self.phase))
        return tok

    def dma(self, eng, fn, waits=(), inc=None):
        return self.op(eng, fn, waits, inc, 16)

    def barrier(self, waits):
        ws = [x for x in waits if x is not None and x[0] is not None]
        for e in ENGINES:
            self.pending[e] = self.pending[e] + ws

    def emit(self, block, nc):
        for engname in ENGINES:
            def make(engname=engname):
                def emitter(eng):
                    seen = {}
                    for (e, fn, waits, incsem, amt, phase) in self.items:
                        if e != engname:
                            continue
                        with nc.named_scope(phase):
                            for (sm, v) in waits:
                                k = id(sm)
                                if seen.get(k, -1) < v:
                                    eng.wait_ge(sm, v)
                                    seen[k] = v
                            inst = fn(eng)
                            if incsem is not None:
                                inst.then_inc(incsem, amt)
                return emitter
            getattr(block, engname)(make())


def host_consts(T):
    half = DH // 2
    inv = 1.0 / (10000.0 ** (np.arange(half, dtype=np.float64) * 2.0 / DH))
    t = np.arange(T, dtype=np.float64)
    ang = np.outer(inv, t)
    cos = np.cos(ang).astype(np.float32)
    sin = np.sin(ang).astype(np.float32)
    cos2 = np.concatenate([cos, cos], 0)
    sin2n = np.concatenate([-sin, sin], 0)

    il = np.arange(128)[None, :]
    jl = np.arange(128)[:, None]

    def mk(keep):
        keep = np.broadcast_to(keep, (128, 128))
        m = np.where(keep, 0.0, NEG).astype(np.float32)
        return np.tile(m, (1, 4)).astype(BF)
    return {
        "cos2": cos2, "sin2n": sin2n,
        "m_diag": mk(il >= jl),
        "m_win": mk(jl >= il),
        "m_wsink": mk((jl >= il) | (jl < SINK)),
        "m_sink": mk(jl < SINK),
        "idn_bf": np.eye(128, dtype=BF),
        "ones_col_bf": np.ones((1, 128), dtype=BF),
        "ones128_bf": np.ones((128, 1), dtype=BF),
        "ones_row_f": np.ones((1, 128), dtype=np.float32),
        "one1": np.ones((1, 1), dtype=np.float32),
        "eps_col": np.full((128, 1), EPS, dtype=np.float32),
    }


def blocks_for(qt):
    if qt <= 4:
        out = []
        for kt in range(qt + 1):
            if kt == qt:
                out.append((kt, "m_diag"))
            elif kt == 0 and qt == 4:
                out.append((kt, "m_wsink"))
            else:
                out.append((kt, None))
        return out
    return [(qt - 4, "m_win"), (qt - 3, None), (qt - 2, None), (qt - 1, None),
            (qt, "m_diag"), (0, "m_sink")]


class Builder:
    def __init__(self, T):
        assert T % 512 == 0
        self.T = T
        self.NT = T // 128
        self.NM = T // 512
        nc = bass.Bass("TRN2", target_bir_lowering=False, debug=False)
        self.nc = nc
        self.s = Sched()
        self.ctx = contextlib.ExitStack()

        dt = nc.dram_tensor
        self.x = dt("x", [T, D], F32, kind="ExternalInput").ap()
        self.wq = dt("wq", [D, D], F32, kind="ExternalInput").ap()
        self.bq = dt("bq", [1, D], F32, kind="ExternalInput").ap()
        self.wk = dt("wk", [D, 512], F32, kind="ExternalInput").ap()
        self.bk = dt("bk", [1, 512], F32, kind="ExternalInput").ap()
        self.wv = dt("wv", [D, 512], F32, kind="ExternalInput").ap()
        self.bv = dt("bv", [1, 512], F32, kind="ExternalInput").ap()
        self.wo = dt("wo", [D, D], F32, kind="ExternalInput").ap()
        self.bo = dt("bo", [1, D], F32, kind="ExternalInput").ap()
        self.gw = dt("gw", [NE, D], F32, kind="ExternalInput").ap()
        self.w1t = dt("w1t", [NE + 1, D, HID], F32, kind="ExternalInput").ap()
        self.w2t = dt("w2t", [NE + 1, HID, D], F32, kind="ExternalInput").ap()
        self.w3t = dt("w3t", [NE + 1, D, HID], F32, kind="ExternalInput").ap()
        self.c_cos2 = dt("cos2", [128, T], F32, kind="ExternalInput").ap()
        self.c_sin2n = dt("sin2n", [128, T], F32, kind="ExternalInput").ap()
        for nm in ("m_diag", "m_win", "m_wsink", "m_sink"):
            setattr(self, "c_" + nm, dt(nm, [128, 512], BF16, kind="ExternalInput").ap())
        self.c_idn_bf = dt("idn_bf", [128, 128], BF16, kind="ExternalInput").ap()
        self.c_ones_col_bf = dt("ones_col_bf", [1, 128], BF16, kind="ExternalInput").ap()
        self.c_ones128_bf = dt("ones128_bf", [128, 1], BF16, kind="ExternalInput").ap()
        self.c_ones_row_f = dt("ones_row_f", [1, 128], F32, kind="ExternalInput").ap()
        self.c_one1 = dt("one1", [1, 1], F32, kind="ExternalInput").ap()
        self.c_eps_col = dt("eps_col", [128, 1], F32, kind="ExternalInput").ap()
        self.xn_scr = dt("xn_scr", [T, D], BF16, kind="Internal").ap()
        self.q_scr = dt("q_scr", [128, NH, T], BF16, kind="Internal").ap()
        self.sw_scr = dt("sw_scr", [2, 128, D], F32, kind="Internal").ap()
        self.x1_scr = dt("x1_scr", [T, D], F32, kind="Internal").ap()
        self.xn2_scr = dt("xn2_scr", [T, D], BF16, kind="Internal").ap()
        self.g_scr = dt("g_scr", [NE, T], BF16, kind="Internal").ap()
        self.w1_bfs = dt("w1_bfs", [(NE + 1) * 128, 16 * HID], BF16, kind="Internal").ap()
        self.w3_bfs = dt("w3_bfs", [(NE + 1) * 128, 16 * HID], BF16, kind="Internal").ap()
        self.w2_bfs = dt("w2_bfs", [(NE + 1) * 128, 16 * HID], BF16, kind="Internal").ap()
        self.out = dt("out", [T, D], F32, kind="ExternalOutput").ap()

        self.sems = {}
        for nm in ["c_pe", "c_se", "c_ve", "c_po",
                   "d_w0", "d_w1", "d_w2", "d_w3",
                   "d_x0", "d_x1",
                   "d_xnw0", "d_xnw1",
                   "d_xnt0", "d_xnt1",
                   "d_sw0", "d_sw1",
                   "d_qw",
                   "d_qt0", "d_qt1",
                   "d_cos", "d_cos1", "d_cst", "d_gw",
                   "d_x1w", "d_x1r", "d_xn2w",
                   "d_outS", "d_outG", "d_outG1", "d_gsw", "d_gsr0", "d_gsr1", "d_wc",
                   "d_wb0", "d_wb1", "d_wb2"]:
            self.sems[nm] = self.ctx.enter_context(self.nc.semaphore(nm))

    def sb(self, name, shape, dtype, stack=None):
        return (stack or self.ctx).enter_context(self.nc.sbuf_tensor(name, shape, dtype))

    def ps(self, name, shape, dtype, stack=None):
        return (stack or self.ctx).enter_context(self.nc.psum_tensor(name, shape, dtype))

    def load_wmat(self, src_ap, dst, sem, waits, nchunk=16):
        s = self.s
        for part in range(4):
            s.dma("gpsimd", (lambda eng, part=part, src_ap=src_ap, dst=dst, nchunk=nchunk:
                  eng.dma_start(
                      out=dst[:].rearrange("p (c n) -> p c n", c=nchunk)
                          [:, part * (nchunk // 4):(part + 1) * (nchunk // 4), :],
                      in_=src_ap.rearrange("(c p) n -> p c n", p=128)
                          [:, part * (nchunk // 4):(part + 1) * (nchunk // 4), :])),
                  waits=(waits if part == 0 else []), inc=self.sems[sem])
        return s.val(self.sems[sem])

    def ln_tile(self, x_ap, out_bf, stats, mv, first_waits):
        s, sems = self.s, self.sems
        for k in range(4):
            s.op("vector", (lambda eng, k=k, x_ap=x_ap, stats=stats: eng.bn_stats(
                stats[:, k * 6:(k + 1) * 6], x_ap[:, k * 512:(k + 1) * 512])),
                waits=(first_waits if k == 0 else []))
        s.op("vector", lambda eng: eng.drain())
        s.op("vector", (lambda eng, stats=stats, mv=mv: eng.bn_aggr(mv[:, 0:2], stats[:, 0:24])))
        s.op("vector", lambda eng: eng.drain())
        tok = s.op("vector", lambda eng: eng.nop(), inc=sems["c_ve"])
        tok_sq = s.op("scalar", (lambda eng, mv=mv: eng.activation(
            out=mv[:, 2:3], in_=mv[:, 1:2], func=AF.Sqrt,
            bias=self.cst["eps_col"][:], scale=1.0)),
            waits=[(sems["c_ve"], tok)], inc=sems["c_se"])
        s.op("vector", (lambda eng, mv=mv: eng.reciprocal(mv[:, 3:4], mv[:, 2:3])),
             waits=[(sems["c_se"], tok_sq)])
        s.op("vector", lambda eng: eng.drain())
        return s.op("vector", (lambda eng, x_ap=x_ap, out_bf=out_bf, mv=mv: eng.tensor_scalar(
            out=out_bf[:], in0=x_ap[:], scalar1=mv[:, 0:1], scalar2=mv[:, 3:4],
            op0=ALU.subtract, op1=ALU.mult)), inc=sems["c_ve"])

    def rope(self, m, gi, src_f32, qsw, t1b, t2b, out_bf, cos4, sin4, cos_tok,
             tok_src, slot_par, prev_t2, out_guard=None, cos_sem="d_cos",
             prev_add=None):
        s, sems = self.s, self.sems
        slot = 0
        swsem = sems["d_sw0"]
        tok_sw_w = s.dma("sync", (lambda eng, slot=slot, src_f32=src_f32: eng.dma_start(
            out=self.sw_scr[slot], in_=src_f32[:])),
            waits=[tok_src, prev_t2, prev_add], inc=swsem)
        s.dma("sync", (lambda eng, slot=slot, qsw=qsw: eng.dma_start(
            out=qsw[0:64, :], in_=self.sw_scr[slot, 64:128, :])),
            waits=[(swsem, tok_sw_w)], inc=swsem)
        tok_sw_r = s.dma("sync", (lambda eng, slot=slot, qsw=qsw: eng.dma_start(
            out=qsw[64:128, :], in_=self.sw_scr[slot, 0:64, :])), inc=swsem)
        s.op("vector", lambda eng: eng.drain())
        tok_t1 = s.op("vector", (lambda eng, src_f32=src_f32, t1b=t1b, cos4=cos4: eng.tensor_tensor(
            out=t1b[:], in0=src_f32[:], in1=cos4[:], op=ALU.mult)),
            waits=[tok_src, (sems[cos_sem], cos_tok)], inc=sems["c_ve"])
        tok_t2 = s.op("gpsimd", (lambda eng, qsw=qsw, t2b=t2b, sin4=sin4: eng.tensor_tensor(
            out=t2b[:], in0=qsw[:], in1=sin4[:], op=ALU.mult)),
            waits=[(swsem, tok_sw_r), (sems[cos_sem], cos_tok), prev_add], inc=sems["c_po"])
        s.op("vector", lambda eng: eng.drain())
        tok_out = s.op("vector", (lambda eng, t1b=t1b, t2b=t2b, out_bf=out_bf: eng.tensor_tensor(
            out=out_bf[:], in0=t1b[:], in1=t2b[:], op=ALU.add)),
            waits=[(sems["c_po"], tok_t2), (sems["c_ve"], tok_t1), out_guard],
            inc=sems["c_ve"])
        return tok_out, tok_t2

    def load_trig(self, m, cos4, sin4, gate, sem="d_cos"):
        s, sems = self.s, self.sems
        for r in range(4):
            s.dma("scalar", (lambda eng, m=m, r=r, cos4=cos4: eng.dma_start(
                out=cos4[:, r * 512:(r + 1) * 512],
                in_=self.c_cos2[:, m * 512:(m + 1) * 512])),
                waits=(gate if r == 0 else []), inc=sems[sem])
            s.dma("scalar", (lambda eng, m=m, r=r, sin4=sin4: eng.dma_start(
                out=sin4[:, r * 512:(r + 1) * 512],
                in_=self.c_sin2n[:, m * 512:(m + 1) * 512])), inc=sems[sem])
        return s.val(sems[sem])

    def build(self):
        self._setup()
        self._pass_q()
        self._pass_kv()
        self._pass_attn()
        self._pass_moe()
        block = self.ctx.enter_context(self.nc.Block())
        self.s.emit(block, self.nc)
        self.ctx.close()
        return self.nc

    def _setup(self):
        s, sems = self.s, self.sems
        cst = {}
        for nm, shape, dty, src in [
            ("idn_bf", [128, 128], BF16, self.c_idn_bf),
            ("ones_col_bf", [1, 128], BF16, self.c_ones_col_bf),
            ("ones128_bf", [128, 1], BF16, self.c_ones128_bf),
            ("ones_row_f", [1, 128], F32, self.c_ones_row_f),
            ("one1", [1, 1], F32, self.c_one1),
            ("eps_col", [128, 1], F32, self.c_eps_col),
        ]:
            t = self.sb("cst_" + nm, shape, dty)
            cst[nm] = t
            s.dma("sync", (lambda eng, t=t, src=src: eng.dma_start(out=t[:], in_=src)),
                  inc=sems["d_cst"])
        self.cst = cst
        self.bqs_col = self.sb("bqs_col", [128, NH], F32)
        self.bk_col = self.sb("bk_col", [128, NG], F32)
        self.gatewT = self.sb("gatewT", [128, NH * NE], BF16)

        with contextlib.ExitStack() as c:
            for nm, shape, dty, src in [
                ("bq_row", [1, D], F32, self.bq),
                ("bk_row", [1, 512], F32, self.bk),
            ]:
                t = self.sb("cst_" + nm, shape, dty, c)
                cst[nm] = t
                s.dma("sync", (lambda eng, t=t, src=src: eng.dma_start(out=t[:], in_=src)),
                      inc=sems["d_cst"])
            gw_bf = self.sb("gw_bf", [NE, D], BF16, c)
            s.dma("gpsimd", lambda eng: eng.dma_start(out=gw_bf[:], in_=self.gw),
                  inc=sems["d_gw"])
            psA = self.ps("psA", [128, 512], F32, c)
            psB = self.ps("psB", [128, 1024], BF16, c)
            cw = (sems["d_cst"], s.val(sems["d_cst"]))
            gwd = (sems["d_gw"], s.val(sems["d_gw"]))
            for h in range(NH):
                tok_pe = s.op("tensor", (lambda eng, h=h, psA=psA: eng.matmul(
                    psA[:, h:h + 1], self.cst["bq_row"][0:1, h * 128:(h + 1) * 128],
                    self.cst["one1"][:], start=True, stop=True)),
                    waits=([cw] if h == 0 else []), inc=sems["c_pe"])
            for g in range(NG):
                tok_pe = s.op("tensor", (lambda eng, g=g, psA=psA: eng.matmul(
                    psA[:, 16 + g:17 + g], self.cst["bk_row"][0:1, g * 128:(g + 1) * 128],
                    self.cst["one1"][:], start=True, stop=True)), inc=sems["c_pe"])
            tok_v = s.op("vector", (lambda eng, psA=psA: eng.tensor_scalar(
                out=self.bqs_col[:], in0=psA[:, 0:16], scalar1=SCALE, scalar2=None,
                op0=ALU.mult)),
                waits=[(sems["c_pe"], tok_pe)], inc=sems["c_ve"])
            tok_v = s.op("vector", (lambda eng, psA=psA: eng.tensor_copy(
                self.bk_col[:], psA[:, 16:20])), inc=sems["c_ve"])


            for fc in range(NH):
                tok_pe = s.op("tensor", (lambda eng, fc=fc, psB=psB, gw_bf=gw_bf: eng.transpose(
                    psB[:, fc * 8:(fc + 1) * 8], gw_bf[:, fc * 128:(fc + 1) * 128],
                    self.cst["idn_bf"][0:8, 0:8])),
                    waits=([gwd, cw] if fc == 0 else []), inc=sems["c_pe"])
            tok_v3 = s.op("vector", (lambda eng, psB=psB: eng.tensor_copy(
                self.gatewT[:], psB[:, 0:NH * NE])),
                waits=[(sems["c_pe"], tok_pe)], inc=sems["c_ve"])
            self.setup_done = [(sems["c_ve"], s.val(sems["c_ve"])),
                               (sems["c_pe"], s.val(sems["c_pe"]))]

    def _pass_q(self):
        s, sems, cst = self.s, self.sems, self.cst
        s.phase = "passq"
        NM, NT = self.NM, self.NT
        s.barrier(self.setup_done)
        with contextlib.ExitStack() as c:
            wq_bf = self.sb("wq_bf", [128, NH * D], BF16, c)
            qx_sb = self.sb("qx_sb", [128, D], F32, c)
            qx_sb2 = self.sb("qx_sb2", [128, D], F32, c)
            x_sb = [qx_sb, qx_sb2]
            xn_bf = [self.sb(f"xn_bf{i}", [128, D], BF16, c) for i in range(2)]
            xnt = [self.sb(f"xnt{i}", [128, NH * 512], BF16, c) for i in range(2)]
            qtf = [self.sb(f"qtf{i}", [128, D], F32, c) for i in range(2)]
            qsw = self.sb("qsw", [128, D], F32, c)
            t1b = self.sb("t1b", [128, D], F32, c)
            t2b = self.sb("t2b", [128, D], F32, c)
            qbf1 = self.sb("qbf", [128, D], BF16, c)
            qbf = [qbf1, qbf1]
            cos4 = [self.sb(f"cos4{i}", [128, D], F32, c) for i in range(2)]
            sin4 = [self.sb(f"sin4{i}", [128, D], F32, c) for i in range(2)]
            stats = self.sb("stats", [128, 32], F32, c)
            mv = self.sb("mv", [128, 4], F32, c)
            psq = [self.ps(f"psq{i}", [128, 512], F32, c) for i in range(2)]

            wq_tok = self.load_wmat(self.wq, wq_bf, "d_w0", [])

            tok_norm = {}
            tok_xnw = {}
            tok_xnt_use = {}
            tok_xnt = {}
            tok_qev = {}
            tok_rope_v = {}
            tok_rope_p = {}
            tok_qbf_st = {}
            prev_t2 = {}
            sp = 0

            cos_tok = {}

            def stage_a(m):
                mi = m % 2
                gate = [(sems["c_ve"], tok_rope_v[m - 2]),
                        (sems["c_po"], tok_rope_p[m - 2])] if m >= 2 else []
                cos_tok[m] = self.load_trig(m, cos4[mi], sin4[mi], gate,
                                            sem=("d_cos" if mi == 0 else "d_cos1"))
                xl_toks = {}

                def emit_xload(tt):
                    qt = m * 4 + tt
                    ti = qt % 2
                    w = [(sems["c_ve"], tok_norm[qt - 2])] if qt - 2 in tok_norm else []
                    s.dma("sync", (lambda eng, qt=qt, ti=ti: eng.dma_start(
                        out=x_sb[ti][:], in_=self.x[qt * 128:(qt + 1) * 128, :])),
                        waits=w, inc=sems[f"d_x{ti}"])
                    xl_toks[tt] = s.val(sems[f"d_x{ti}"])

                emit_xload(0)
                emit_xload(1)
                for tt in range(4):
                    if tt >= 2:
                        emit_xload(tt)
                    qt = m * 4 + tt
                    ti = qt % 2
                    wv = [(sems[f"d_x{ti}"], xl_toks[tt])]
                    if qt - 2 in tok_xnw:
                        wv.append((sems[f"d_xnw{ti}"], tok_xnw[qt - 2]))
                    tok_norm[qt] = self.ln_tile(x_sb[ti], xn_bf[ti], stats, mv, wv)
                    w = [(sems["c_ve"], tok_norm[qt])]
                    if m - 2 in tok_xnt and tt == 0:
                        w.append((sems[f"d_xnt{(m - 2) % 2}"], tok_xnt[m - 2]))
                    tok_xnw[qt] = s.dma("sync", (lambda eng, qt=qt, ti=ti: eng.dma_start(
                        out=self.xn_scr[qt * 128:(qt + 1) * 128, :], in_=xn_bf[ti][:])),
                        waits=w, inc=sems[f"d_xnw{ti}"])

                w = [(sems["d_xnw0"], s.val(sems["d_xnw0"])),
                     (sems["d_xnw1"], s.val(sems["d_xnw1"]))]
                if m - 2 in tok_xnt_use:
                    w.append((sems["c_pe"], tok_xnt_use[m - 2]))
                for fc in range(NH):
                    s.dma("scalar", (lambda eng, m=m, mi=mi, fc=fc: eng.dma_start(
                        out=xnt[mi][:, fc * 512:(fc + 1) * 512],
                        in_=self.xn_scr[m * 512:(m + 1) * 512, fc * 128:(fc + 1) * 128],
                        transpose=True)),
                        waits=(w if fc == 0 else []), inc=sems[f"d_xnt{mi}"])
                tok_xnt[m] = s.val(sems[f"d_xnt{mi}"])

            def stage_b(m):
                mi = m % 2
                tok_pe = None
                for g in range(NG):
                    gi = g % 2
                    for u in range(GS):
                        h = g * GS + u
                        pw = [(sems[f"d_xnt{mi}"], tok_xnt[m]), (sems["d_w0"], wq_tok)]
                        if (h - 2) in tok_qev or m > 0:
                            hh = h - 2 if h - 2 >= 0 else NG * GS + (h - 2)
                            if hh in tok_qev:
                                pw.append((sems["c_se"], tok_qev[hh]))
                        if u == 0:
                            key = (m * NG + g) - 2
                            if key >= 0:
                                pw.append((sems["c_ve"], self._q_rope_tok[key]))
                        for fc in range(NH):
                            tok_pe = s.op("tensor", (lambda eng, h=h, fc=fc, mi=mi: eng.matmul(
                                psq[h % 2][:],
                                wq_bf[:, fc * D + h * 128:fc * D + (h + 1) * 128],
                                xnt[mi][:, fc * 512:(fc + 1) * 512],
                                start=(fc == 0), stop=(fc == NH - 1))),
                                waits=(pw if fc == 0 else []),
                                inc=(sems["c_pe"] if fc == NH - 1 else None))
                        tok_qev[h] = s.op("scalar", (lambda eng, h=h, u=u, gi=gi: eng.activation(
                            out=qtf[gi][:, u * 512:(u + 1) * 512], in_=psq[h % 2][:],
                            func=AF.Identity, bias=self.bqs_col[:, h:h + 1], scale=SCALE)),
                            waits=[(sems["c_pe"], tok_pe)], inc=sems["c_se"])
                    tok_rope, tok_t2 = self.rope(
                        m, gi, qtf[gi], qsw, t1b, t2b, qbf[gi],
                        cos4[mi], sin4[mi], cos_tok[m],
                        (sems["c_se"], tok_qev[g * GS + GS - 1]),
                        self._sp, prev_t2.get(0),
                        out_guard=(sems["d_qw"], s.val(sems["d_qw"])),
                        cos_sem=("d_cos" if mi == 0 else "d_cos1"),
                        prev_add=prev_t2.get(1))
                    prev_t2[0] = (sems["c_po"], tok_t2)
                    prev_t2[1] = (sems["c_ve"], tok_rope)
                    self._q_rope_tok[m * NG + g] = tok_rope
                    self._sp += 1
                    tok_rope_v[m] = tok_rope
                    tok_rope_p[m] = tok_t2
                    w = [(sems["c_ve"], tok_rope)]
                    tok_qbf_st[(m, g)] = s.dma("scalar", (lambda eng, m=m, g=g, gi=gi: eng.dma_start(
                        out=self.q_scr[:, g * GS:(g + 1) * GS, m * 512:(m + 1) * 512],
                        in_=qbf[gi][:].rearrange("p (u n) -> p u n", u=GS))),
                        waits=w, inc=sems["d_qw"])
                tok_xnt_use[m] = tok_pe

            self._q_rope_tok = {}
            self._sp = 0
            for m in range(NM):
                if m >= 1:
                    stage_b(m - 1)
                stage_a(m)
            stage_b(NM - 1)

            self.q_all_done = [(sems["d_qw"], s.val(sems["d_qw"]))]
            self.pq_end = [(sems[k], s.val(sems[k])) for k in
                           ("c_pe", "c_se", "c_ve", "c_po", "d_sw0", "d_sw1",
                            "d_xnw0", "d_xnw1")]

    def _pass_kv(self):
        s, sems, cst = self.s, self.sems, self.cst
        s.phase = "passkv"
        T, NM = self.T, self.NM
        s.barrier(self.pq_end + self.q_all_done)
        self.kv_scope = contextlib.ExitStack()
        self.ktc = self.sb("ktc", [128, NG * T], BF16, self.kv_scope)
        self.vc = self.sb("vc", [128, self.NT * 512], BF16, self.kv_scope)
        self.kabs = self.sb("kabs", [128, NG * NM], F32, self.kv_scope)
        with contextlib.ExitStack() as c:
            wk_bf = self.sb("wk_bf", [128, NH * 512], BF16, c)
            wv_bf = self.sb("wv_bf", [128, NH * 512], BF16, c)
            xnt = [self.sb(f"kxnt{i}", [128, NH * 512], BF16, c) for i in range(2)]
            ktf = [self.sb(f"ktf{i}", [128, D], F32, c) for i in range(2)]
            ksw = self.sb("ksw", [128, D], F32, c)
            t1b = self.sb("kt1b", [128, D], F32, c)
            t2b = self.sb("kt2b", [128, D], F32, c)
            kbf = [self.sb(f"kbf{i}", [128, D], BF16, c) for i in range(2)]
            cos4 = self.sb("kcos4", [128, D], F32, c)
            sin4 = self.sb("ksin4", [128, D], F32, c)
            psk = [self.ps(f"psk{i}", [128, 512], F32, c) for i in range(2)]
            psv = [self.ps(f"psv{i}", [128, 512], F32, c) for i in range(2)]

            bv_rep = self.sb("bv_rep", [128, 512], F32, c)
            self.bv_rep = bv_rep
            bvrow = self.sb("bvrow", [1, 512], F32, c)
            wk_tok = self.load_wmat(self.wk, wk_bf, "d_w1", [], nchunk=NH)
            wv_tok = self.load_wmat(self.wv, wv_bf, "d_w1", [], nchunk=NH)
            tok_bl = s.dma("sync", lambda eng: eng.dma_start(
                out=bvrow[:], in_=self.bv), inc=sems["d_cst"])
            tok_bp = s.op("tensor", lambda eng: eng.matmul(
                psk[0][:], cst["ones_row_f"][:], bvrow[:], start=True, stop=True),
                waits=[(sems["d_cst"], tok_bl)], inc=sems["c_pe"])
            tok_bv = s.op("vector", lambda eng: eng.tensor_copy(bv_rep[:], psk[0][:]),
                          waits=[(sems["c_pe"], tok_bp)], inc=sems["c_ve"])
            self._bv_done = (sems["c_ve"], tok_bv)

            tok_xnt_use = {}
            tok_xnt = {}
            tok_kev = {}
            tok_vev = {}
            tok_rope_v = {}
            tok_rope_p = {}
            tok_kc = {}
            prev_t2 = {}
            sp = 0
            for m in range(NM):
                mi = m % 2
                gate = [(sems["c_ve"], tok_rope_v[m - 1]),
                        (sems["c_po"], tok_rope_p[m - 1])] if m >= 1 else []
                cos_tok = self.load_trig(m, cos4, sin4, gate)
                w = [(sems["c_pe"], tok_xnt_use[m - 2])] if m - 2 in tok_xnt_use else []
                for fc in range(NH):
                    s.dma("scalar", (lambda eng, m=m, mi=mi, fc=fc: eng.dma_start(
                        out=xnt[mi][:, fc * 512:(fc + 1) * 512],
                        in_=self.xn_scr[m * 512:(m + 1) * 512, fc * 128:(fc + 1) * 128],
                        transpose=True)),
                        waits=(w if fc == 0 else []), inc=sems[f"d_xnt{mi}"])
                tok_xnt[m] = s.val(sems[f"d_xnt{mi}"])

                for g in range(NG):
                    w0 = [(sems[f"d_xnt{mi}"], tok_xnt[m]), (sems["d_w1"], wv_tok),
                          self._bv_done]
                    if g - 2 in tok_kev or m > 0:
                        key = g - 2 if g - 2 >= 0 else NG + g - 2
                        if key in tok_kev:
                            w0.append((sems["c_se"], tok_kev[key]))
                    if g == 0 and m >= 1:
                        w0.append((sems["c_ve"], tok_rope_v[m - 1]))
                    for fc in range(NH):
                        tok_pe = s.op("tensor", (lambda eng, g=g, fc=fc, mi=mi: eng.matmul(
                            psk[g % 2][:],
                            wk_bf[:, fc * 512 + g * 128:fc * 512 + (g + 1) * 128],
                            xnt[mi][:, fc * 512:(fc + 1) * 512],
                            start=(fc == 0), stop=(fc == NH - 1))),
                            waits=(w0 if fc == 0 else []),
                            inc=(sems["c_pe"] if fc == NH - 1 else None))
                    tok_kev[g] = s.op("scalar", (lambda eng, g=g, mi=mi: eng.activation(
                        out=ktf[mi][:, g * 512:(g + 1) * 512], in_=psk[g % 2][:],
                        func=AF.Identity, bias=self.bk_col[:, g:g + 1], scale=1.0)),
                        waits=[(sems["c_pe"], tok_pe)], inc=sems["c_se"])
                for tt in range(4):
                    qt = m * 4 + tt
                    w0 = []
                    if tt - 2 in tok_vev or m > 0:
                        key = tt - 2 if tt - 2 >= 0 else 4 + tt - 2
                        if key in tok_vev:
                            w0.append((sems["c_ve"], tok_vev[key]))
                    for fc in range(NH):
                        tok_pe = s.op("tensor", (lambda eng, tt=tt, fc=fc, mi=mi: eng.matmul(
                            psv[tt % 2][:],
                            xnt[mi][:, fc * 512 + tt * 128:fc * 512 + (tt + 1) * 128],
                            wv_bf[:, fc * 512:(fc + 1) * 512],
                            start=(fc == 0), stop=(fc == NH - 1))),
                            waits=(w0 if fc == 0 else []),
                            inc=(sems["c_pe"] if fc == NH - 1 else None))
                    tok_vev[tt] = s.op("vector", (lambda eng, qt=qt, tt=tt: eng.tensor_tensor(
                        out=self.vc[:, qt * 512:(qt + 1) * 512], in0=psv[tt % 2][:],
                        in1=self.bv_rep[:], op=ALU.add)),
                        waits=[(sems["c_pe"], tok_pe)], inc=sems["c_ve"])
                tok_xnt_use[m] = tok_pe

                tok_rope, tok_t2 = self.rope(
                    m, mi, ktf[mi], ksw, t1b, t2b, kbf[mi],
                    cos4, sin4, cos_tok,
                    (sems["c_se"], tok_kev[NG - 1]),
                    sp, prev_t2.get(0), prev_add=prev_t2.get(1))
                prev_t2[0] = (sems["c_po"], tok_t2)
                prev_t2[1] = (sems["c_ve"], tok_rope)
                sp += 1
                tok_rope_v[m] = tok_rope
                tok_rope_p[m] = tok_t2
                s.op("vector", lambda eng: eng.drain())
                tok_kc[m] = s.op("vector", (lambda eng, m=m, mi=mi: eng.tensor_copy(
                    self.ktc[:].rearrange("p (g t) -> p g t", g=NG)[:, :, m * 512:(m + 1) * 512],
                    kbf[mi][:].rearrange("p (g n) -> p g n", g=NG))),
                    waits=[(sems["c_ve"], tok_rope)], inc=sems["c_ve"])
                for g in range(NG):
                    s.op("vector", (lambda eng, m=m, g=g, mi=mi: eng.reduce_max(
                        out=self.kabs[:, m * NG + g:m * NG + g + 1],
                        in_=kbf[mi][:, g * 512:(g + 1) * 512], axis=AX.X,
                        apply_absolute_value=True)))
                s.op("vector", lambda eng: eng.drain())
                tok_rope_v[m] = s.op("vector", lambda eng: eng.nop(), inc=sems["c_ve"])

            self.kv_end = [(sems[k], s.val(sems[k])) for k in
                           ("c_pe", "c_se", "c_ve", "c_po", "d_sw0", "d_sw1")]

    def _precast_mat(self, k):
        s, sems = self.s, self.sems
        e, which = divmod(k, 3)
        src_t = [self.w1t, self.w3t, self.w2t][which]
        dst_t = [self.w1_bfs, self.w3_bfs, self.w2_bfs][which]
        nch = 8 if which == 2 else 16
        for part in range(2):
            s.dma("gpsimd", (lambda eng, e=e, src_t=src_t, dst_t=dst_t,
                             nch=nch, part=part: eng.dma_start(
                out=dst_t[e * 128:(e + 1) * 128, :]
                    .rearrange("p (c n) -> p c n", c=nch)
                    [:, part * (nch // 2):(part + 1) * (nch // 2), :],
                in_=src_t[e].rearrange("(c p) n -> p c n", p=128)
                    [:, part * (nch // 2):(part + 1) * (nch // 2), :])),
                inc=sems["d_wc"])

    def _pass_attn(self):
        s, sems, cst = self.s, self.sems, self.cst
        s.phase = "attn"
        T, NT = self.T, self.NT
        s.barrier(self.kv_end)
        with contextlib.ExitStack() as c:
            for nm, msrc in [("m_diag", self.c_m_diag), ("m_win", self.c_m_win),
                             ("m_wsink", self.c_m_wsink), ("m_sink", self.c_m_sink)]:
                t = self.sb("cst_" + nm, [128, 512], BF16, c)
                cst[nm] = t
                s.dma("sync", (lambda eng, t=t, msrc=msrc: eng.dma_start(
                    out=t[:], in_=msrc)), inc=sems["d_cst"])
            self._mask_tok = (sems["d_cst"], s.val(sems["d_cst"]))
            bo_rep = self.sb("bo_rep", [128, D], F32, c)
            self.bo_rep = bo_rep
            wo_bf = self.sb("wo_bf", [128, NH * D], BF16, c)
            negk = self.sb("negk", [1, NG * 128], BF16, c)
            kinf_row = self.sb("kinf_row", [1, NG * self.NM], F32, c)
            kinf_sc = self.sb("kinf_sc", [1, NG], F32, c)
            qt_sb = [self.sb(f"qt_sb{i}", [128, D], BF16, c) for i in range(2)]
            ax_sb = self.sb("ax_sb", [128, D], F32, c)
            x_sb = [ax_sb, ax_sb]

            l1row = self.sb("l1row", [1, D], BF16, c)
            pt_sb = [self.sb(f"pt_sb{i}", [128, 6 * 512], BF16, c) for i in range(2)]
            ot_sb = self.sb("ot_sb", [128, D], BF16, c)
            rs_sb = self.sb("rs_sb", [1, 512], F32, c)
            rinv = self.sb("rinv", [128, 512], F32, c)
            tout = [self.sb(f"tout{i}", [128, 512], F32, c) for i in range(2)]
            x1_sb = self.sb("x1_sb", [128, D], F32, c)
            xn2_bf = self.sb("xn2_bf", [128, D], BF16, c)
            absq = self.sb("absq", [128, D], BF16, c)
            stats = self.sb("astats", [128, 32], F32, c)
            mv = self.sb("amv", [128, 4], F32, c)
            psS = [self.ps(f"psS{i}", [128, 512], F32, c) for i in range(3)]
            psO = [self.ps(f"psO{i}", [128, 512], F32, c) for i in range(2)]
            psR = self.ps("psR", [128, 512], F32, c)
            psU = [self.ps(f"psU{i}", [128, 512], F32, c) for i in range(2)]

            wo_tok = self.load_wmat(self.wo, wo_bf, "d_w2", [])

            tok_bo = None
            for nq in range(4):
                tok_l = s.dma("sync", (lambda eng, nq=nq: eng.dma_start(
                    out=rs_sb[:], in_=self.bo[0:1, nq * 512:(nq + 1) * 512])),
                    waits=[tok_bo], inc=sems["d_cst"])
                tok_pe = s.op("tensor", (lambda eng, nq=nq: eng.matmul(
                    psU[0][:], cst["ones_row_f"][:], rs_sb[:],
                    start=True, stop=True)),
                    waits=[(sems["d_cst"], tok_l)], inc=sems["c_pe"])
                tok_v = s.op("vector", (lambda eng, nq=nq: eng.tensor_copy(
                    bo_rep[:, nq * 512:(nq + 1) * 512], psU[0][:])),
                    waits=[(sems["c_pe"], tok_pe)], inc=sems["c_ve"])
                tok_bo = (sems["c_ve"], tok_v)
            self._bo_done = tok_bo
            tok_u_free_init = tok_bo

            tok_kg = s.op("gpsimd", lambda eng: eng.tensor_reduce(
                out=kinf_row[:], in_=self.kabs[:], axis=AX.C, op=ALU.max),
                inc=sems["c_po"])
            for g in range(NG):
                s.op("vector", (lambda eng, g=g: eng.reduce_max(
                    out=kinf_sc[0:1, g:g + 1],
                    in_=kinf_row[0:1, :].rearrange("o (m g) -> o g m", g=NG)[0:1, g, :],
                    axis=AX.X)),
                    waits=([(sems["c_po"], tok_kg)] if g == 0 else []))
            s.op("vector", lambda eng: eng.drain())
            for g in range(NG):
                s.op("vector", (lambda eng, g=g: eng.tensor_scalar(
                    out=negk[0:1, g * 128:(g + 1) * 128], in0=cst["ones_row_f"][:],
                    scalar1=kinf_sc[0:1, g:g + 1], scalar2=-1.0,
                    op0=ALU.mult, op1=ALU.mult)))
            s.op("vector", lambda eng: eng.drain())
            tok_negk = s.op("vector", lambda eng: eng.nop(), inc=sems["c_ve"])

            bi = 0
            gg = 0
            tok_exp = {}
            tok_avdone = {}
            tok_otev = {}
            tok_rs_chain = None
            tok_u_free = {0: tok_u_free_init, 1: None}
            tok_l1 = {}
            tok_scores_end = {}
            tok_l1mm_end = {}
            tok_x_use = {}
            tok_x1w = {}
            tok_xn2w = {}
            tok_norm_prev = None

            NMAT = 3 * (NE + 1)
            for qt in range(NT):
                if qt < NMAT:
                    self._precast_mat(qt)
                ti = qt % 2
                blocks = blocks_for(qt)
                nb = len(blocks)
                w = [(sems["c_pe"], tok_scores_end[qt - 2])] if qt - 2 in tok_scores_end else []
                if qt == 0:
                    w += self.q_all_done
                for part in range(4):
                    s.dma("sync", (lambda eng, qt=qt, ti=ti, part=part: eng.dma_start(
                        out=qt_sb[ti][:].rearrange("p (h n) -> p h n", h=NH)
                            [:, part * 4:(part + 1) * 4, :],
                        in_=self.q_scr[:, part * 4:(part + 1) * 4, qt * 128:(qt + 1) * 128])),
                        waits=(w if part == 0 else []), inc=sems[f"d_qt{ti}"])
                qt_tok = s.val(sems[f"d_qt{ti}"])
                w = [(sems["c_ve"], tok_x_use[qt - 1])] if qt - 1 in tok_x_use else []
                s.dma("sync", (lambda eng, qt=qt, ti=ti: eng.dma_start(
                    out=x_sb[ti][:], in_=self.x[qt * 128:(qt + 1) * 128, :])),
                    waits=w, inc=sems[f"d_x{ti}"])
                xl_tok = s.val(sems[f"d_x{ti}"])
                w = [(sems[f"d_qt{ti}"], qt_tok)]
                if qt - 1 in tok_l1mm_end:
                    w.append((sems["c_pe"], tok_l1mm_end[qt - 1]))
                tok_absq = s.op("vector", (lambda eng, ti=ti: eng.scalar_tensor_tensor(
                    out=absq[:], in0=qt_sb[ti][:], scalar=-1.0, in1=qt_sb[ti][:],
                    op0=ALU.mult, op1=ALU.max)),
                    waits=w, inc=sems["c_ve"])
                for g in range(NG):
                    ub = g % 2
                    w = [(sems["c_ve"], tok_absq), tok_u_free[ub]]
                    tok_pe = s.op("tensor", (lambda eng, g=g, ub=ub: eng.matmul(
                        psU[ub][0:1, :], cst["ones128_bf"][:],
                        absq[:, g * 512:(g + 1) * 512], start=True, stop=True)),
                        waits=w, inc=sems["c_pe"])
                    ev_w = [(sems["c_pe"], tok_pe)]
                    if g == 0 and qt - 1 in tok_scores_end:
                        ev_w.append((sems["c_pe"], tok_scores_end[qt - 1]))
                    tok_l1[g] = s.op("scalar", (lambda eng, g=g, ub=ub: eng.activation(
                        out=l1row[0:1, g * 512:(g + 1) * 512], in_=psU[ub][0:1, :],
                        func=AF.Copy)),
                        waits=ev_w, inc=sems["c_se"])
                    tok_u_free[ub] = (sems["c_se"], tok_l1[g])
                tok_l1mm_end[qt] = tok_pe

                for g in range(NG):
                    pi = gg % 2
                    my_bis = []
                    for j, (kt, mask) in enumerate(blocks):
                        bank = bi % 3
                        w = [(sems[f"d_qt{ti}"], qt_tok), (sems["c_se"], tok_l1[g]),
                             (sems["c_ve"], tok_negk), self._mask_tok]
                        if (bi - 3) in tok_exp:
                            w.append((sems["c_se"], tok_exp[bi - 3]))
                        s.op("tensor", (lambda eng, g=g, kt=kt, bank=bank, ti=ti: eng.matmul(
                            psS[bank][:],
                            self.ktc[:, g * T + kt * 128:g * T + (kt + 1) * 128],
                            qt_sb[ti][:, g * 512:(g + 1) * 512],
                            start=True, stop=False)), waits=w)
                        if mask is not None:
                            s.op("tensor", (lambda eng, mask=mask, bank=bank: eng.matmul(
                                psS[bank][:], cst["idn_bf"][:], cst[mask][:],
                                start=False, stop=False)))
                        tok_pe = s.op("tensor", (lambda eng, g=g, bank=bank: eng.matmul(
                            psS[bank][:], negk[0:1, g * 128:(g + 1) * 128],
                            l1row[0:1, g * 512:(g + 1) * 512],
                            start=False, stop=True)), inc=sems["c_pe"])
                        w = [(sems["c_pe"], tok_pe)]
                        if gg - 2 in tok_avdone:
                            w.append((sems["c_pe"], tok_avdone[gg - 2]))
                        tok_exp[bi] = s.op("scalar", (lambda eng, j=j, bank=bank, pi=pi: eng.activation(
                            out=pt_sb[pi][:, j * 512:(j + 1) * 512], in_=psS[bank][:],
                            func=AF.Exp)), waits=w, inc=sems["c_se"])
                        my_bis.append(bi)
                        bi += 1
                    tok_scores_end[qt] = tok_pe
                    w_rs0 = [(sems["c_ve"], tok_rs_chain)] if tok_rs_chain else []
                    for j, (kt, mask) in enumerate(blocks):
                        w = [(sems["c_se"], tok_exp[my_bis[j]])]
                        if j == 0 and gg - 2 in tok_otev:
                            w.append((sems["c_ve"], tok_otev[gg - 2]))
                        tok_av = s.op("tensor", (lambda eng, g=g, kt=kt, j=j, pi=pi, gg=gg, nb=nb: eng.matmul(
                            psO[gg % 2][:],
                            self.vc[:, kt * 512 + g * 128:kt * 512 + (g + 1) * 128],
                            pt_sb[pi][:, j * 512:(j + 1) * 512],
                            start=(j == 0), stop=(j == nb - 1))),
                            waits=w, inc=(sems["c_pe"] if j == nb - 1 else None))
                    for j, (kt, mask) in enumerate(blocks):
                        tok_rs = s.op("tensor", (lambda eng, j=j, pi=pi, nb=nb: eng.matmul(
                            psR[0:1, :], cst["ones128_bf"][:],
                            pt_sb[pi][:, j * 512:(j + 1) * 512],
                            start=(j == 0), stop=(j == nb - 1))),
                            waits=(w_rs0 if j == 0 else []),
                            inc=(sems["c_pe"] if j == nb - 1 else None))
                    tok_avdone[gg] = tok_rs
                    tok_rsev = s.op("scalar", lambda eng: eng.activation(
                        out=rs_sb[:], in_=psR[0:1, :], func=AF.Copy),
                        waits=[(sems["c_pe"], tok_rs)], inc=sems["c_se"])
                    tok_repl = s.op("tensor", lambda eng: eng.matmul(
                        psR[:], cst["ones_row_f"][:], rs_sb[:], start=True, stop=True),
                        waits=[(sems["c_se"], tok_rsev)], inc=sems["c_pe"])
                    s.op("vector", lambda eng: eng.drain())
                    tok_rinv = s.op("vector", lambda eng: eng.reciprocal(rinv[:], psR[:]),
                                    waits=[(sems["c_pe"], tok_repl)], inc=sems["c_ve"])
                    tok_rs_chain = tok_rinv
                    s.op("vector", lambda eng: eng.drain())
                    tok_otev[gg] = s.op("vector", (lambda eng, g=g, gg=gg: eng.tensor_tensor(
                        out=ot_sb[:, g * 512:(g + 1) * 512], in0=psO[gg % 2][:],
                        in1=rinv[:], op=ALU.mult)),
                        waits=[(sems["c_pe"], tok_avdone[gg]), (sems["c_ve"], tok_rinv)],
                        inc=sems["c_ve"])
                    gg += 1

                w_ot = [(sems["c_ve"], tok_otev[gg - 1 - k]) for k in range(NG)]
                tok_x1_done = None
                tok_pool_q = getattr(self, "_tok_pool_q", {})
                for nq in range(4):
                    ub = nq % 2
                    w = list(w_ot) + [(sems["d_w2"], wo_tok), tok_u_free[ub]]
                    for h in range(NH):
                        tok_pe = s.op("tensor", (lambda eng, h=h, nq=nq, ub=ub: eng.matmul(
                            psU[ub][:], ot_sb[:, h * 128:(h + 1) * 128],
                            wo_bf[:, h * D + nq * 512:h * D + (nq + 1) * 512],
                            start=(h == 0), stop=(h == NH - 1))),
                            waits=(w if h == 0 else []),
                            inc=(sems["c_pe"] if h == NH - 1 else None))
                    tok_t = s.op("vector", (lambda eng, nq=nq, ub=ub, ti=ti: eng.tensor_tensor(
                        out=tout[ub][:], in0=psU[ub][:],
                        in1=x_sb[ti][:, nq * 512:(nq + 1) * 512], op=ALU.add)),
                        waits=[(sems["c_pe"], tok_pe), (sems[f"d_x{ti}"], xl_tok),
                               tok_pool_q.get(ub)],
                        inc=sems["c_ve"])
                    tok_u_free[ub] = (sems["c_ve"], tok_t)
                    w = [(sems["c_ve"], tok_t)]
                    if nq == 0 and qt == 0:
                        w.append(self._bo_done)
                    if nq == 0 and qt - 1 in tok_x1w:
                        w.append((sems["d_x1w"], tok_x1w[qt - 1]))
                        w.append((sems["d_xn2w"], tok_xn2w[qt - 1]))
                        w.append((sems["c_ve"], tok_norm_prev))
                    tok_x1_done = s.op("gpsimd", (lambda eng, nq=nq, ub=ub: eng.tensor_tensor(
                        out=x1_sb[:, nq * 512:(nq + 1) * 512], in0=tout[ub][:],
                        in1=self.bo_rep[:, nq * 512:(nq + 1) * 512], op=ALU.add)),
                        waits=w, inc=sems["c_po"])
                    tok_pool_q[ub] = (sems["c_po"], tok_x1_done)
                self._tok_pool_q = tok_pool_q
                tok_x_use[qt] = tok_t
                tok_x1w[qt] = s.dma("scalar", (lambda eng, qt=qt: eng.dma_start(
                    out=self.x1_scr[qt * 128:(qt + 1) * 128, :], in_=x1_sb[:])),
                    waits=[(sems["c_po"], tok_x1_done)], inc=sems["d_x1w"])
                tok_norm_prev = self.ln_tile(
                    x1_sb, xn2_bf, stats, mv,
                    [(sems["c_po"], tok_x1_done),
                     (sems["c_pe"], tok_l1mm_end[qt])] +
                    ([(sems["d_xn2w"], tok_xn2w[qt - 1])] if qt - 1 in tok_xn2w else []))
                tok_xn2w[qt] = s.dma("scalar", (lambda eng, qt=qt: eng.dma_start(
                    out=self.xn2_scr[qt * 128:(qt + 1) * 128, :], in_=xn2_bf[:])),
                    waits=[(sems["c_ve"], tok_norm_prev)], inc=sems["d_xn2w"])

            for k in range(min(NT, 3 * (NE + 1)), 3 * (NE + 1)):
                self._precast_mat(k)
            self._wc_done = (sems["d_wc"], s.val(sems["d_wc"]))
            self.attn_end = [(sems[k], s.val(sems[k])) for k in
                             ("c_pe", "c_se", "c_ve", "c_po", "d_x1w", "d_xn2w",
                              "d_qt0", "d_qt1", "d_x0", "d_x1")]

    def _pass_moe(self):
        s, sems, cst = self.s, self.sems, self.cst
        s.phase = "moe"
        T, NM = self.T, self.NM
        s.barrier(self.attn_end)
        self.kv_scope.close()
        with contextlib.ExitStack() as c:
            wbuf = [self.sb(f"wm{i}", [128, 16 * HID], BF16, c) for i in range(3)]
            xnt = self.sb("mxnt", [128, NH * 512], BF16, c)
            xg = [self.sb(f"xg{i}", [128, NH * 512], BF16, c) for i in range(2)]
            hT = self.sb("hT", [128, 8 * 512], BF16, c)
            ssb = self.sb("ssb", [128, 512], F32, c)
            grep = self.sb("grep", [128, 512], BF16, c)
            gTs = [self.sb(f"gTs{i}", [8, 128], BF16, c) for i in range(2)]
            g_row = [self.sb(f"g_row{i}", [1, 512], BF16, c) for i in range(2)]
            lgt = self.sb("lgt", [128, 16], F32, c)
            gtmp = self.sb("gtmp", [128, 8], F32, c)
            gbf = self.sb("gbf", [128, 8], BF16, c)
            osb = [self.sb(f"osb{i}", [128, D], F32, c) for i in range(2)]
            x1t = xg[0][:].bitcast(F32)
            psZ = [self.ps(f"psZ{i}", [128, 512], F32, c) for i in range(4)]
            psW = [self.ps(f"psW{i}", [128, 512], F32, c) for i in range(2)]
            psG = self.ps("psG", [128, 512], F32, c)
            psT = self.ps("psT", [8, 1024], BF16, c)

            bsrcs = [self.w1_bfs, self.w3_bfs, self.w2_bfs]
            buf_free = {}
            tok_w = {}

            def load_bf(e, which, waits):
                toks = None
                q = "gpsimd" if which == 1 else "sync"
                for part in range(2):
                    toks = s.dma(q, (lambda eng, e=e, which=which, part=part:
                        eng.dma_start(
                            out=wbuf[which][:, part * 8192:(part + 1) * 8192],
                            in_=bsrcs[which][e * 128:(e + 1) * 128,
                                             part * 8192:(part + 1) * 8192])),
                        waits=(waits if part == 0 else []), inc=sems[f"d_wb{which}"])
                return toks

            for which in range(3):
                tok_w[(NE, which)] = load_bf(NE, which, [self._wc_done])

            tok_gtrans_ev = None
            tok_xg_done = None
            tok_osb_free = [None, None]
            tok_hT_done = None
            tok_grep_done = None
            tok_x1_use = {}
            x1_tok = {}
            tok_lgcopy = None
            gsw_toks = {}
            tok_grow_use = {}
            prev_psW = {0: None, 1: None}
            silu_toks = {}
            h_toks = {}
            tok_xg_m = {}
            tok_xg_read = {}

            for ei, e in enumerate([NE] + list(range(NE))):
                shared = (e == NE)
                b1, b3, b2 = 0, 1, 2
                w1_tok, w3_tok, w2_tok = tok_w[(e, 0)], tok_w[(e, 1)], tok_w[(e, 2)]
                for m in range(NM + 1):
                    if m < NM and not shared:
                        par = (ei * NM + m) % 2
                        wld = [(sems["d_gsw"], s.val(sems["d_gsw"]))]
                        if par in tok_grow_use:
                            wld.append(tok_grow_use[par])
                        s.dma("scalar", (lambda eng, e=e, m=m, par=par: eng.dma_start(
                            out=g_row[par][:], in_=self.g_scr[e:e + 1, m * 512:(m + 1) * 512])),
                            waits=wld, inc=sems[f"d_gsr{par}"])
                        grow_tok = s.val(sems[f"d_gsr{par}"])
                    if m < NM:
                        w = [tok_xg_done]
                        for fc in range(NH):
                            s.dma("scalar", (lambda eng, m=m, fc=fc: eng.dma_start(
                                out=xnt[:, fc * 512:(fc + 1) * 512],
                                in_=self.xn2_scr[m * 512:(m + 1) * 512, fc * 128:(fc + 1) * 128],
                                transpose=True)),
                                waits=(w if fc == 0 else []), inc=sems["d_xnt0"])
                        xnt_tok = s.val(sems["d_xnt0"])
                    if shared and m == NM:
                        break

                    if shared:
                        for tt in range(4):
                            w = [(sems["d_xnt0"], xnt_tok)]
                            if tok_lgcopy is not None:
                                w.append((sems["c_ve"], tok_lgcopy))
                            for fc in range(NH):
                                tok_pe = s.op("tensor", (lambda eng, tt=tt, fc=fc: eng.matmul(
                                    psG[:, 0:8],
                                    xnt[:, fc * 512 + tt * 128:fc * 512 + (tt + 1) * 128],
                                    self.gatewT[:, fc * 8:(fc + 1) * 8],
                                    start=(fc == 0), stop=(fc == NH - 1))),
                                    waits=(w if fc == 0 else []),
                                    inc=(sems["c_pe"] if fc == NH - 1 else None))
                            tok_lg = s.op("vector", lambda eng: eng.tensor_copy(
                                lgt[:, 0:8], psG[:, 0:8]),
                                waits=[(sems["c_pe"], tok_pe)], inc=sems["c_ve"])
                            tok_lgcopy = tok_lg
                            s.op("vector", lambda eng: eng.drain())
                            s.op("vector", lambda eng: eng.reduce_max(
                                lgt[:, 8:9], lgt[:, 0:8], axis=AX.X))
                            s.op("vector", lambda eng: eng.drain())
                            s.op("vector", lambda eng: eng.tensor_scalar(
                                out=gtmp[:], in0=lgt[:, 0:8], scalar1=lgt[:, 8:9],
                                scalar2=None, op0=ALU.is_ge))
                            s.op("vector", lambda eng: eng.drain())
                            s.op("vector", lambda eng: eng.scalar_tensor_tensor(
                                out=gtmp[:], in0=gtmp[:], scalar=-1e30, in1=lgt[:, 0:8],
                                op0=ALU.mult, op1=ALU.add))
                            s.op("vector", lambda eng: eng.drain())
                            s.op("vector", lambda eng: eng.reduce_max(
                                lgt[:, 9:10], gtmp[:], axis=AX.X))
                            s.op("vector", lambda eng: eng.drain())
                            tok_keep = s.op("vector", lambda eng: eng.tensor_scalar(
                                out=gtmp[:], in0=lgt[:, 0:8], scalar1=lgt[:, 9:10],
                                scalar2=None, op0=ALU.is_ge), inc=sems["c_ve"])
                            s.op("vector", lambda eng: eng.drain())
                            tok_sig = s.op("scalar", lambda eng: eng.activation(
                                out=lgt[:, 0:8], in_=lgt[:, 0:8], func=AF.Sigmoid),
                                waits=[(sems["c_ve"], tok_keep)], inc=sems["c_se"])
                            tok_g = s.op("vector", lambda eng: eng.tensor_tensor(
                                out=gbf[:], in0=gtmp[:], in1=lgt[:, 0:8], op=ALU.mult),
                                waits=[(sems["c_se"], tok_sig)], inc=sems["c_ve"])
                            w = [(sems["c_ve"], tok_g)]
                            if tok_gtrans_ev is not None:
                                w.append(tok_gtrans_ev)
                            tok_tp = s.op("tensor", (lambda eng, tt=tt: eng.transpose(
                                psT[:, tt * 128:(tt + 1) * 128], gbf[:], cst["idn_bf"][:])),
                                waits=w, inc=sems["c_pe"])
                            qt = m * 4 + tt
                            w = [(sems["c_pe"], tok_tp),
                                 (sems["d_gsw"], s.val(sems["d_gsw"]))]
                            tok_gtev = s.op("vector", (lambda eng, tt=tt: eng.tensor_copy(
                                gTs[tt % 2][:],
                                psT[0:8, tt * 128:(tt + 1) * 128])),
                                waits=w, inc=sems["c_ve"])
                            tok_gtrans_ev = (sems["c_ve"], tok_gtev)
                            gsw_toks[qt] = s.dma("sync", (lambda eng, qt=qt, tt=tt: eng.dma_start(
                                out=self.g_scr[:, qt * 128:(qt + 1) * 128],
                                in_=gTs[tt % 2][:])),
                                waits=[(sems["c_ve"], tok_gtev)], inc=sems["d_gsw"])
                        zsrc = xnt
                        tok_zsrc = (sems["d_xnt0"], xnt_tok)
                    else:
                        if m < NM:
                            mi = m % 2
                            w = [tok_grep_done, (sems[f"d_gsr{par}"], grow_tok)]
                            tok_pe = s.op("tensor", (lambda eng, par=par: eng.matmul(
                                psG[:], cst["ones_col_bf"][:],
                                g_row[par][:], start=True, stop=True)),
                                waits=w, inc=sems["c_pe"])
                            tok_grow_use[par] = (sems["c_pe"], tok_pe)
                            tok_grep = s.op("scalar", lambda eng: eng.activation(
                                out=grep[:], in_=psG[:], func=AF.Copy),
                                waits=[(sems["c_pe"], tok_pe)], inc=sems["c_se"])
                            for fc in range(NH):
                                xgw = []
                                if fc == 0:
                                    xgw = [(sems["c_se"], tok_grep),
                                           (sems["d_xnt0"], xnt_tok),
                                           tok_xg_read.get(mi)]
                                    if ei == 1 and m == 0 and (NM * 4 - 1) in tok_x1_use:
                                        xgw.append(tok_x1_use[NM * 4 - 1])
                                tok_xg = s.op("gpsimd", (lambda eng, fc=fc, mi=mi: eng.tensor_tensor(
                                    out=xg[mi][:, fc * 512:(fc + 1) * 512],
                                    in0=xnt[:, fc * 512:(fc + 1) * 512], in1=grep[:],
                                    op=ALU.mult)),
                                    waits=xgw,
                                    inc=(sems["c_po"] if fc == NH - 1 else None))
                            tok_xg_m[m] = (sems["c_po"], tok_xg)
                            tok_xg_done = (sems["c_po"], tok_xg)
                            tok_grep_done = (sems["c_po"], tok_xg)
                        if m == 0:
                            continue
                        m = m - 1
                        zsrc = xg[m % 2]
                        tok_zsrc = tok_xg_m[m]

                    last_z = None
                    for hc in range(8):
                        zb1 = hc % 2
                        zb3 = 2 + hc % 2
                        w1w = [tok_zsrc, (sems[f"d_wb{b1}"], w1_tok)]
                        w1w.append(silu_toks.get(hc - 2 if hc >= 2 else 6 + hc))
                        for fc in range(NH):
                            tok_z1 = s.op("tensor", (lambda eng, hc=hc, fc=fc, zb1=zb1, b1=b1, zsrc=zsrc: eng.matmul(
                                psZ[zb1][:],
                                wbuf[b1][:, fc * HID + hc * 128:fc * HID + (hc + 1) * 128],
                                zsrc[:, fc * 512:(fc + 1) * 512],
                                start=(fc == 0), stop=(fc == NH - 1))),
                                waits=(w1w if fc == 0 else []),
                                inc=(sems["c_pe"] if fc == NH - 1 else None))
                        buf_free[b1] = (sems["c_pe"], tok_z1)
                        w3w = [(sems[f"d_wb{b3}"], w3_tok)]
                        w3w.append(h_toks.get(hc - 2 if hc >= 2 else 6 + hc))
                        for fc in range(NH):
                            tok_z3 = s.op("tensor", (lambda eng, hc=hc, fc=fc, zb3=zb3, b3=b3, zsrc=zsrc: eng.matmul(
                                psZ[zb3][:],
                                wbuf[b3][:, fc * HID + hc * 128:fc * HID + (hc + 1) * 128],
                                zsrc[:, fc * 512:(fc + 1) * 512],
                                start=(fc == 0), stop=(fc == NH - 1))),
                                waits=(w3w if fc == 0 else []),
                                inc=(sems["c_pe"] if fc == NH - 1 else None))
                        last_z = tok_z3
                        buf_free[b3] = (sems["c_pe"], tok_z3)
                        tok_silu = s.op("scalar", (lambda eng, zb1=zb1: eng.activation(
                            out=ssb[:], in_=psZ[zb1][:], func=AF.Silu)),
                            waits=[(sems["c_pe"], tok_z1),
                                   (h_toks[hc - 1] if hc >= 1 else None)],
                            inc=sems["c_se"])
                        hw = [(sems["c_se"], tok_silu), (sems["c_pe"], tok_z3)]
                        if hc == 0:
                            hw.append(tok_hT_done)
                        tok_h = s.op("vector", (lambda eng, hc=hc, zb3=zb3: eng.tensor_tensor(
                            out=hT[:, hc * 512:(hc + 1) * 512], in0=ssb[:],
                            in1=psZ[zb3][:], op=ALU.mult)),
                            waits=hw, inc=sems["c_ve"])
                        silu_toks[hc] = (sems["c_se"], tok_silu)
                        h_toks[hc] = (sems["c_ve"], tok_h)
                    tok_hT = tok_h
                    if shared:
                        tok_xg_done = (sems["c_pe"], last_z)
                    else:
                        tok_xg_read[m % 2] = (sems["c_pe"], last_z)

                    for tt in range(4):
                        qt = m * 4 + tt
                        if shared:
                            w = [tok_x1_use[qt - 1]] if qt - 1 in tok_x1_use else []
                            s.dma("sync", (lambda eng, qt=qt: eng.dma_start(
                                out=x1t[:, 0:D], in_=self.x1_scr[qt * 128:(qt + 1) * 128, :])),
                                waits=w, inc=sems["d_x1r"])
                            x1_tok[qt] = s.val(sems["d_x1r"])
                        oi = tt % 2
                        tok_ev_last = None
                        ev_toks = {}
                        for nq in range(4):
                            wb = nq % 2
                            w = [(sems["c_ve"], tok_hT), (sems[f"d_wb{b2}"], w2_tok),
                                 tok_osb_free[oi] if nq == 0 else None,
                                 prev_psW[wb]]
                            for hc in range(8):
                                tok_pe = s.op("tensor", (lambda eng, tt=tt, nq=nq, hc=hc, wb=wb, b2=b2: eng.matmul(
                                    psW[wb][:],
                                    hT[:, hc * 512 + tt * 128:hc * 512 + (tt + 1) * 128],
                                    wbuf[b2][:, hc * D + nq * 512:hc * D + (nq + 1) * 512],
                                    start=(hc == 0), stop=(hc == 7))),
                                    waits=(w if hc == 0 else []),
                                    inc=(sems["c_pe"] if hc == 7 else None))
                            if shared:
                                tok_ev = s.op("vector", (lambda eng, nq=nq, oi=oi, wb=wb: eng.tensor_tensor(
                                    out=osb[oi][:, nq * 512:(nq + 1) * 512], in0=psW[wb][:],
                                    in1=x1t[:, nq * 512:(nq + 1) * 512], op=ALU.add)),
                                    waits=[(sems["c_pe"], tok_pe),
                                           (sems["d_x1r"], x1_tok[qt])],
                                    inc=sems["c_ve"])
                                sem_ev = sems["c_ve"]
                            else:
                                if nq % 2 == 0:
                                    tok_ev = s.op("vector", (lambda eng, nq=nq, oi=oi, wb=wb: eng.tensor_copy(
                                        osb[oi][:, nq * 512:(nq + 1) * 512], psW[wb][:])),
                                        waits=[(sems["c_pe"], tok_pe)], inc=sems["c_ve"])
                                    sem_ev = sems["c_ve"]
                                else:
                                    tok_ev = s.op("scalar", (lambda eng, nq=nq, oi=oi, wb=wb: eng.activation(
                                        out=osb[oi][:, nq * 512:(nq + 1) * 512],
                                        in_=psW[wb][:], func=AF.Copy)),
                                        waits=[(sems["c_pe"], tok_pe)], inc=sems["c_se"])
                                    sem_ev = sems["c_se"]
                            ev_toks[nq] = (sem_ev, tok_ev)
                            prev_psW[wb] = (sem_ev, tok_ev)
                            tok_ev_last = (sem_ev, tok_ev)
                            buf_free[b2] = (sems["c_pe"], tok_pe)
                        if shared:
                            tok_x1_use[qt] = tok_ev_last
                            tok_store = s.dma("sync", (lambda eng, qt=qt, oi=oi: eng.dma_start(
                                out=self.out[qt * 128:(qt + 1) * 128, :], in_=osb[oi][:])),
                                waits=[ev_toks[2], ev_toks[3],
                                       (sems["d_outS"], s.val(sems["d_outS"]))],
                                inc=sems["d_outS"])
                            tok_osb_free[oi] = (sems["d_outS"], s.val(sems["d_outS"]))
                        else:
                            gsem = "d_outG" if oi == 0 else "d_outG1"
                            w = [ev_toks[0], ev_toks[1], ev_toks[2], ev_toks[3],
                                 (sems["d_outS"], s.val(sems["d_outS"])),
                                 (sems[gsem], s.val(sems[gsem]))]
                            tok_store = s.dma("gpsimd", (lambda eng, qt=qt, oi=oi: eng.dma_start(
                                out=self.out[qt * 128:(qt + 1) * 128, :], in_=osb[oi][:],
                                accum_op=ALU.add)),
                                waits=w, inc=sems[gsem])
                            tok_osb_free[oi] = (sems[gsem], s.val(sems[gsem]))
                    tok_hT_done = (sems["c_pe"], tok_pe)

                if ei < NE:
                    ne = 0 if shared else e + 1
                    for nw in range(3):
                        tok_w[(ne, nw)] = load_bf(ne, nw, [buf_free.get(nw)])

            s.op("sync", lambda eng: eng.nop(),
                 waits=[(sems["d_outS"], s.val(sems["d_outS"])),
                        (sems["d_outG"], s.val(sems["d_outG"])),
                        (sems["d_outG1"], s.val(sems["d_outG1"]))])


def _build(T):
    return Builder(T).build()


def kernel(L0_attn_x, L0_attn_Wq, L0_attn_bq, L0_attn_Wk, L0_attn_bk,
           L0_attn_Wv, L0_attn_bv, L0_attn_Wo, L0_attn_bo,
           L0_ffn_gate_w, L0_ffn_w1, L0_ffn_w2, L0_ffn_w3,
           L0_ffn_sw1, L0_ffn_sw2, L0_ffn_sw3):
    T = np.asarray(L0_attn_x).shape[1]
    n_cores = 8
    consts = host_consts(T)
    f32 = np.float32

    def c(a):
        return np.ascontiguousarray(np.asarray(a))

    w1t = c(np.concatenate([np.asarray(L0_ffn_w1).transpose(0, 2, 1),
                            np.asarray(L0_ffn_sw1).T[None]], axis=0).astype(f32))
    w3t = c(np.concatenate([np.asarray(L0_ffn_w3).transpose(0, 2, 1),
                            np.asarray(L0_ffn_sw3).T[None]], axis=0).astype(f32))
    w2t = c(np.concatenate([np.asarray(L0_ffn_w2).transpose(0, 2, 1),
                            np.asarray(L0_ffn_sw2).T[None]], axis=0).astype(f32))

    base = {
        "wq": c(L0_attn_Wq), "bq": c(np.asarray(L0_attn_bq)[None, :]),
        "wk": c(L0_attn_Wk), "bk": c(np.asarray(L0_attn_bk)[None, :]),
        "wv": c(L0_attn_Wv), "bv": c(np.asarray(L0_attn_bv)[None, :]),
        "wo": c(L0_attn_Wo), "bo": c(np.asarray(L0_attn_bo)[None, :]),
        "gw": c(L0_ffn_gate_w), "w1t": w1t, "w2t": w2t, "w3t": w3t,
    }
    base.update(consts)

    nc = _build(T)
    in_maps = []
    xfull = np.asarray(L0_attn_x)
    for core in range(n_cores):
        m = dict(base)
        m["x"] = c(xfull[core])
        in_maps.append(m)
    res = run_bass_kernel_spmd(nc, in_maps, core_ids=list(range(n_cores)))
    return np.stack([r["out"] for r in res.results], axis=0)
